# version 03
# baseline (speedup 1.0000x reference)
import numpy as np
import ml_dtypes
from contextlib import ExitStack
import concourse.bass as bass
import concourse.mybir as mybir
from concourse.bass_utils import run_bass_kernel_spmd
from concourse.bass_core_rust import IndirectOffsetOnAxis

F32 = mybir.dt.float32
BF16 = mybir.dt.bfloat16
U32 = mybir.dt.uint32
AF = mybir.ActivationFunctionType
ALU = mybir.AluOpType
AX = mybir.AxisListType
TS = bass.ts

B, SL, D = 16, 4096, 1024
NCORES = 8
NSEQ = B // NCORES
HEADS, DH, HID = 8, 64, 512
NE, FFD = 8, 4096
LAT = 256
EPS = 1e-5
SCALE = DH ** -0.5
L1SCALE = 1.0 / float(np.sqrt(D))
BLK = 512


def _derive():
    global TC, NT, TPS, NBLK, NBLKC, SCRATCH, INPUT_SHAPES, OUT_SHAPE
    TC = NSEQ * SL
    NT = TC // 128
    TPS = SL // 128
    NBLK = SL // BLK
    NBLKC = TC // BLK
    SCRATCH = {
        "xn0":   ([TC, D], BF16),
        "xnlo0": ([TC, D], BF16),
        "wrlob": ([128, D], BF16),
        "mc0":   ([128, NT * 8], F32),
        "hdbg":  ([128, 32 * BLK], BF16),
        "ydbg":  ([128, 4 * D], F32),
        "xmid0": ([TC, D], F32),
        "x1":    ([TC, D], F32),
        "xn1":   ([TC, D], BF16),
        "xmid1": ([TC, D], F32),
        "q0":    ([TC, HID], BF16),
        "v0":    ([TC, HID], BF16),
        "k0":    ([NSEQ, HID, SL], F32),
        "ek0":   ([NSEQ, HID, SL], BF16),
        "w1b":   ([NE * FFD, D], BF16),
        "w2b":   ([NE * D, FFD], BF16),
        "wthb":  ([LAT, D], BF16),
        "wphb":  ([LAT, D], BF16),
        "wgb":   ([D, D], BF16),
        "wrb":   ([128, D], BF16),
        "pool1": ([NSEQ, D], F32),
        "zsc":   ([128, 4], F32),
        "ctxs":  ([2, 64, 4, DH], BF16),
        "th1":   ([TC, LAT], BF16),
        "ph1":   ([TC, LAT], BF16),
        "g1":    ([TC, D], BF16),
    }
    INPUT_SHAPES = {
        "xin": ([TC, D], F32),
        "L0_attn_Wq": ([D, HID], F32), "L0_attn_Wk": ([D, HID], F32),
        "L0_attn_Wv": ([D, HID], F32), "L0_attn_Wo": ([HID, D], F32),
        "L0_attn_bo": ([D], F32),
        "L0_ffn_Wr": ([NE, D], F32), "L0_ffn_br": ([NE], F32),
        "L0_ffn_W1": ([NE, FFD, D], F32), "L0_ffn_b1": ([NE, FFD], F32),
        "L0_ffn_W2": ([NE, D, FFD], F32), "L0_ffn_b2": ([NE, D], F32),
        "L1_attn_W_theta": ([LAT, D], F32), "L1_attn_W_phi": ([LAT, D], F32),
        "L1_attn_W_g": ([D, D], F32),
        "L1_ffn_Wr": ([D, NE], F32),
        "L1_ffn_W1": ([NE, D, FFD], F32), "L1_ffn_b1": ([NE, FFD], F32),
        "L1_ffn_W2": ([NE, FFD, D], F32), "L1_ffn_b2": ([NE, D], F32),
    }
    OUT_SHAPE = ([TC, D], F32)


def set_dims(nseq=2, sl=4096):
    global NSEQ, SL
    NSEQ, SL = nseq, sl
    _derive()


_derive()

PHASE_IO = {
    "cast0": (["L0_ffn_W1", "L0_ffn_W2", "L1_attn_W_theta", "L1_attn_W_phi",
               "L1_attn_W_g", "L0_ffn_Wr"],
              ["w1b", "w2b", "wthb", "wphb", "wgb", "wrb", "wrlob"]),
    "ln0":   (["xin"], ["xn0", "xnlo0"]),
    "attn0": (["xn0", "xin", "L0_attn_Wq", "L0_attn_Wk", "L0_attn_Wv",
               "L0_attn_Wo", "L0_attn_bo"],
              ["xmid0", "q0", "v0", "k0", "ek0"]),
    "ffn0":  (["xn0", "xnlo0", "xmid0", "w1b", "w2b", "wrb", "wrlob",
               "L0_ffn_br", "L0_ffn_b1", "L0_ffn_b2"],
              ["x1", "mc0", "hdbg", "ydbg"]),
    "ln1":   (["x1"], ["xn1", "pool1"]),
    "attn1": (["xn1", "x1", "wthb", "wphb", "wgb"],
              ["th1", "ph1", "g1", "xmid1"]),
    "ffn1":  (["xn1", "xmid1", "pool1", "L1_ffn_Wr", "L1_ffn_W1", "L1_ffn_b1",
               "L1_ffn_W2", "L1_ffn_b2"], ["out"]),
}


class P:

    def __init__(self, nc, launch_phases, before, needed_later):
        self.nc = nc
        self.cnt = {}
        self._sems = {}
        self._slots = {}
        self._es = ExitStack()
        self.stack = None
        self.dram_cache = {}
        self.produced_here = set()
        for ph in launch_phases:
            self.produced_here |= set(PHASE_IO[ph][1])
        self.before = before
        self.needed_later = needed_later
        self.dma_ch = set()

    def dram(self, name):
        if name in self.dram_cache:
            return self.dram_cache[name]
        if name == "out":
            shape, dt = OUT_SHAPE
            kind = "ExternalOutput"
        elif name in INPUT_SHAPES:
            shape, dt = INPUT_SHAPES[name]
            kind = "ExternalInput"
        else:
            shape, dt = SCRATCH[name]
            if name in self.produced_here:
                kind = "ExternalOutput" if name in self.needed_later else "Internal"
            elif name in self.before:
                kind = "ExternalInput"
            else:
                kind = "Internal"
        ap = self.nc.dram_tensor(name, shape, dt, kind=kind).ap()
        self.dram_cache[name] = ap
        return ap

    SEMPOOL = 90

    def _slot(self, name):
        if name not in self._slots:
            self._slots[name] = len(self._slots) % self.SEMPOOL
        return self._slots[name]

    def sem(self, slot):
        if slot not in self._sems:
            self._sems[slot] = self._es.enter_context(self.nc.semaphore(f"s{slot}"))
            self.cnt[slot] = 0
        return self._sems[slot]

    def ev(self, inst, ch, n=1):
        slot = self._slot(ch)
        inst.then_inc(self.sem(slot), n)
        self.cnt[slot] += n
        if n >= 16:
            self.dma_ch.add(ch)
        return (slot, self.cnt[slot])

    def wt(self, eng, *evts):
        for e in evts:
            if e is None:
                continue
            if isinstance(e, list):
                self.wt(eng, *e)
                continue
            slot, v = e
            if v > 0:
                eng.wait_ge(self.sem(slot), v)

    def now(self, ch):
        slot = self._slot(ch)
        return (slot, self.cnt.get(slot, 0))

    def fence(self, blk):
        chans = sorted(self.dma_ch)

        @blk.sync
        def _(s):
            for ch in chans:
                self.wt(s, self.now(ch))

    def sb(self, name, shape, dtype):
        return self.stack.enter_context(self.nc.sbuf_tensor(name, shape, dtype))

    def ps(self, name, shape, dtype=F32):
        return self.stack.enter_context(self.nc.psum_tensor(name, shape, dtype))


def ph_cast0(p: P):
    nc = p.nc
    jobs = [("L0_ffn_W1", "w1b", NE * FFD, D), ("L0_ffn_W2", "w2b", NE * D, FFD),
            ("L1_attn_W_theta", "wthb", LAT, D), ("L1_attn_W_phi", "wphb", LAT, D),
            ("L1_attn_W_g", "wgb", D, D)]
    with ExitStack() as stk:
        p.stack = stk
        buf = p.sb("c0buf", [128, 2 * FFD], BF16)
        wr = p.sb("c0wr", [128, D], BF16)
        wrf = p.sb("c0wrf", [128, D], F32)
        wrh32 = p.sb("c0wrh", [128, D], F32)
        wrlo_sb = p.sb("c0wrl", [128, D], BF16)
        with nc.Block("cast0") as blk:
            seq = []
            for src, dst, rows, w in jobs:
                sap = p.dram(src)
                sflat = sap.rearrange(" ".join(f"a{i}" for i in range(sap.ndim)) +
                                      " -> (" + " ".join(f"a{i}" for i in range(sap.ndim))
                                      + ")").rearrange("(n p w) -> n p w", p=128, w=w)
                dflat = p.dram(dst).rearrange("a b -> (a b)").rearrange(
                    "(n p w) -> n p w", p=128, w=w)
                for i in range(sflat.shape[0]):
                    seq.append((sflat[i], dflat[i], w))
            st_ev = [None, None]
            ld_ev = [None, None]
            for i, (s_ap, d_ap, w) in enumerate(seq):
                b = i % 2

                @blk.gpsimd
                def _(g, b=b, s_ap=s_ap, w=w):
                    p.wt(g, st_ev[b])
                    ld_ev[b] = p.ev(g.dma_start(out=buf[:, TS(b, FFD)][:, 0:w],
                                                in_=s_ap), f"c0ld{b}", 16)

                @blk.scalar
                def _(s, b=b, d_ap=d_ap, w=w):
                    p.wt(s, ld_ev[b])
                    st_ev[b] = p.ev(s.dma_start(out=d_ap, in_=buf[:, TS(b, FFD)][:, 0:w]),
                                    f"c0st{b}", 16)

            @blk.gpsimd
            def _(g):
                mev = p.ev(g.memset(wrf[:, :], 0.0), "c0v", 1)
                p.wt(g, mev)
                ld_ev[0] = p.ev(g.dma_start(out=wrf[0:8, :], in_=p.dram("L0_ffn_Wr")),
                                "c0ldw", 16)

            @blk.scalar
            def _(s):
                p.wt(s, ld_ev[0])
                e1 = p.ev(s.activation(out=wr[:, :], in_=wrf[:, :], func=AF.Copy),
                          "c0we", 1)
                p.wt(s, e1)
                p.ev(s.activation(out=wrh32[:, :], in_=wr[:, :], func=AF.Copy),
                     "c0we", 1)

            @blk.vector
            def _(v):
                p.wt(v, p.now("c0we"))
                p.ev(v.tensor_tensor(out=wrlo_sb[:, :], in0=wrf[:, :], in1=wrh32[:, :],
                                     op=ALU.subtract), "c0wv", 1)

            @blk.scalar
            def _(s):
                p.wt(s, p.now("c0wv"))
                p.ev(s.dma_start(out=p.dram("wrb"), in_=wr[:, :]), "c0stw", 16)
                p.ev(s.dma_start(out=p.dram("wrlob"), in_=wrlo_sb[:, :]), "c0stw", 16)
            p.fence(blk)


def ph_ln(p: P, x_name, xn_name, pooled=False, lo_name=None):
    nc = p.nc
    x_t = p.dram(x_name).rearrange("(n p) d -> n p d", p=128)
    xn_t = p.dram(xn_name).rearrange("(n p) d -> n p d", p=128)
    if lo_name is not None:
        xnlo_t = p.dram(lo_name).rearrange("(n p) d -> n p d", p=128)
    pre = "L" + xn_name
    LD, LV, LA, ST, PEc = pre + "ld", pre + "v", pre + "a", pre + "st", pre + "pe"
    LAc, LV2, ST2 = pre + "ac", pre + "v2", pre + "s2"
    with ExitStack() as stk:
        p.stack = stk
        xb = p.sb(pre + "x", [128, 2 * D], F32)
        ob = p.sb(pre + "o", [128, 2 * D], BF16)
        sq = p.sb(pre + "sq", [128, D], F32)
        st_ = p.sb(pre + "sv", [128, 8], F32)
        cst = p.sb(pre + "cst", [128, 2], F32)
        if lo_name is not None:
            ob32 = p.sb(pre + "o32", [128, 2 * D], F32)
            obf = p.sb(pre + "obf", [128, 2 * D], F32)
            lob = p.sb(pre + "lob", [128, 2 * D], BF16)
        if pooled:
            onec = p.sb(pre + "one", [128, 1], BF16)
            poolp = [p.ps(pre + f"pp{h}", [1, 512]) for h in range(2 * NSEQ)]
            poolr = p.sb(pre + "pr", [1, NSEQ * D], F32)
        cons_ev = [None, None]
        stor_ev = [None, None]
        pe_ev = [None, None]
        ld_evs = [None, None]
        ocp_ev = [None, None]
        losub_ev = [None, None]
        lost_ev = [None, None]
        with nc.Block(pre) as blk:
            @blk.gpsimd
            def _(g):
                p.ev(g.memset(cst[:, 0:1], float(EPS)), LA, 1)
                p.ev(g.memset(cst[:, 1:2], 1.0 / D), LA, 1)
                if pooled:
                    p.ev(g.memset(onec[:, :], 1.0), LV, 1)
            for i in range(NT):
                b = i % 2
                sl = slice(b * D, (b + 1) * D)

                @blk.sync
                def _(s, i=i, b=b, sl=sl):
                    p.wt(s, cons_ev[b], pe_ev[b])
                    ld_evs[b] = p.ev(s.dma_start(out=xb[:, sl], in_=x_t[i]), LD + str(b), 16)

                @blk.vector
                def _(v, i=i, b=b, sl=sl):
                    p.wt(v, ld_evs[b])
                    e1 = p.ev(v.reduce_sum(st_[:, b:b + 1], xb[:, sl], axis=AX.X), LV, 1)
                    p.wt(v, e1)
                    p.ev(v.tensor_scalar(out=st_[:, 2 + b:3 + b], in0=st_[:, b:b + 1],
                                         scalar1=-1.0 / D, scalar2=None, op0=ALU.mult),
                         LV, 1)

                @blk.scalar
                def _(a, i=i, b=b, sl=sl):
                    p.wt(a, p.now(LV), p.now(LA))
                    e1 = p.ev(a.activation(out=sq[:, :], in_=xb[:, sl], func=AF.Square,
                                           bias=st_[:, 2 + b:3 + b], accum_out=st_[:, 4:5]),
                              LA, 1)
                    p.wt(a, e1)
                    p.ev(a.activation(out=st_[:, 5:6], in_=st_[:, 4:5], func=AF.Sqrt,
                                      scale=cst[:, 1:2], bias=cst[:, 0:1]), LA, 1)

                if lo_name is None:
                    @blk.vector
                    def _(v, i=i, b=b, sl=sl):
                        p.wt(v, p.now(LA), p.now(LV))
                        e1 = p.ev(v.reciprocal(st_[:, 6:7], st_[:, 5:6]), LV, 1)
                        p.wt(v, e1, stor_ev[b])
                        cons_ev[b] = p.ev(v.tensor_scalar(out=ob[:, sl], in0=xb[:, sl],
                                                          scalar1=st_[:, 2 + b:3 + b],
                                                          scalar2=st_[:, 6:7],
                                                          op0=ALU.add, op1=ALU.mult), LV, 1)

                    @blk.scalar
                    def _(s, i=i, b=b, sl=sl):
                        p.wt(s, p.now(LV))
                        stor_ev[b] = p.ev(s.dma_start(out=xn_t[i], in_=ob[:, sl]),
                                          ST + str(b), 16)
                else:
                    @blk.vector
                    def _(v, i=i, b=b, sl=sl):
                        p.wt(v, p.now(LA), p.now(LV))
                        e1 = p.ev(v.reciprocal(st_[:, 6:7], st_[:, 5:6]), LV, 1)
                        p.wt(v, e1, ocp_ev[b], losub_ev[b])
                        cons_ev[b] = p.ev(v.tensor_scalar(out=ob32[:, sl], in0=xb[:, sl],
                                                          scalar1=st_[:, 2 + b:3 + b],
                                                          scalar2=st_[:, 6:7],
                                                          op0=ALU.add, op1=ALU.mult), LV, 1)

                    @blk.scalar
                    def _(s, i=i, b=b, sl=sl):
                        p.wt(s, cons_ev[b], stor_ev[b])
                        e1 = p.ev(s.activation(out=ob[:, sl], in_=ob32[:, sl],
                                               func=AF.Copy), LAc, 1)
                        p.wt(s, e1)
                        ocp_ev[b] = p.ev(s.activation(out=obf[:, sl], in_=ob[:, sl],
                                                      func=AF.Copy), LAc, 1)
                        stor_ev[b] = p.ev(s.dma_start(out=xn_t[i], in_=ob[:, sl]),
                                          ST + str(b), 16)

                    @blk.vector
                    def _(v, i=i, b=b, sl=sl):
                        p.wt(v, ocp_ev[b], lost_ev[b])
                        losub_ev[b] = p.ev(v.tensor_tensor(out=lob[:, sl],
                                                           in0=ob32[:, sl],
                                                           in1=obf[:, sl],
                                                           op=ALU.subtract), LV2, 1)

                    @blk.scalar
                    def _(s, i=i, b=b, sl=sl):
                        p.wt(s, losub_ev[b])
                        lost_ev[b] = p.ev(s.dma_start(out=xnlo_t[i], in_=lob[:, sl]),
                                          ST2 + str(b), 16)

                if pooled:
                    @blk.tensor
                    def _(t, i=i, b=b, sl=sl):
                        p.wt(t, p.now(LV))
                        sidx = i // TPS
                        first = (i % TPS) == 0
                        last = (i % TPS) == TPS - 1
                        t.matmul(poolp[2 * sidx][:, :], onec[:, :],
                                 ob[:, sl][:, 0:512], start=first, stop=last)
                        i2 = t.matmul(poolp[2 * sidx + 1][:, :], onec[:, :],
                                      ob[:, sl][:, 512:1024], start=first, stop=last)
                        pe_ev[b] = p.ev(i2, PEc, 1)

            if pooled:
                @blk.vector
                def _(v):
                    p.wt(v, p.now(PEc), p.now(LV))
                    for s in range(NSEQ):
                        for h in range(2):
                            p.ev(v.tensor_scalar(
                                out=poolr[:, s * D + h * 512: s * D + (h + 1) * 512],
                                in0=poolp[2 * s + h][:, :], scalar1=1.0 / SL,
                                scalar2=None, op0=ALU.mult), LV, 1)

                @blk.scalar
                def _(s):
                    p.wt(s, p.now(LV))
                    p.ev(s.dma_start(out=p.dram("pool1").rearrange("s d -> (s d)")[None, :],
                                     in_=poolr[:, :]), ST, 16)
            p.fence(blk)


def ph_attn0(p: P):
    nc = p.nc
    xn = p.dram("xn0")
    xin_t = p.dram("xin").rearrange("(n p) d -> n p d", p=128)
    xmid_t = p.dram("xmid0").rearrange("(n p) d -> n p d", p=128)
    q0 = p.dram("q0")
    q0_t = q0.rearrange("(n p) d -> n p d", p=128)
    v0_t = p.dram("v0").rearrange("(n p) d -> n p d", p=128)
    k0 = p.dram("k0")
    ek0 = p.dram("ek0")
    with ExitStack() as stk:
        p.stack = stk
        wq = p.sb("a0wq", [128, 8 * HID], BF16)
        wk = p.sb("a0wk", [128, 8 * HID], BF16)
        wv = p.sb("a0wv", [128, 8 * HID], BF16)
        wo = p.sb("a0wo", [128, 4 * D], BF16)
        bo = p.sb("a0bo", [1, D], BF16)
        ones1 = p.sb("a0one", [1, 128], BF16)
        xnf = p.sb("a0xnf", [128, 2 * 8 * BLK], BF16)
        kev = p.sb("a0kev", [128, 4 * BLK], F32)
        qf = p.sb("a0qf", [128, HID], F32)
        qe = p.sb("a0qe", [128, HID], F32)
        qs = p.sb("a0qs", [128, 2 * HID], BF16)
        vb = p.sb("a0vb", [128, 2 * HID], BF16)
        mq = p.sb("a0mq", [128, 3 * HEADS], F32)
        kst = p.sb("a0kst", [128, 32], F32)
        tm1 = p.sb("a0tm1", [128, 4], F32)
        ekb = p.sb("a0ekb", [128, 2 * BLK], BF16)
        zp = p.sb("a0zp", [128, 2], F32)
        ektm = p.sb("a0ektm", [128, 2 * HID], BF16)
        vtm = p.sb("a0vtm", [128, 2 * HID], BF16)
        ctxn = p.sb("a0ctxn", [64, HID], BF16)
        ctx2 = p.sb("a0ctx2", [128, 256], BF16)
        zlo = p.sb("a0zlo", [64, 8], F32)
        qfm = p.sb("a0qfm", [128, 2 * HID], BF16)
        af = p.sb("a0af", [128, HID], BF16)
        xmb = p.sb("a0xmb", [128, 2 * D], F32)
        xr = p.sb("a0xr", [128, 2 * D], F32)
        stkA = ExitStack()
        pk = [stkA.enter_context(nc.psum_tensor(f"a0pk{j}", [128, BLK], F32))
              for j in range(4)]
        pq = stkA.enter_context(nc.psum_tensor("a0pq", [128, BLK], F32))
        pv = stkA.enter_context(nc.psum_tensor("a0pv", [128, BLK], F32))

        with nc.Block("a0A") as blk:
            @blk.gpsimd
            def _(g):
                for name, dst, wdt, nch in [("L0_attn_Wq", wq, HID, 8),
                                            ("L0_attn_Wk", wk, HID, 8),
                                            ("L0_attn_Wv", wv, HID, 8)]:
                    ap = p.dram(name).rearrange("(c p) h -> c p h", p=128)
                    for c in range(nch):
                        p.ev(g.dma_start(out=dst[:, TS(c, wdt)], in_=ap[c]), "a0wl", 16)
                apo = p.dram("L0_attn_Wo").rearrange("(c p) d -> c p d", p=128)
                for c in range(4):
                    p.ev(g.dma_start(out=wo[:, TS(c, D)], in_=apo[c]), "a0wl", 16)
                p.ev(g.dma_start(out=bo[:, :], in_=p.dram("L0_attn_bo")[None, :]),
                     "a0wl", 16)
                p.ev(g.memset(ones1[:, :], 1.0), "a0ini", 1)
                for sq in range(NSEQ):
                    p.ev(g.memset(kst[:, 16 * sq:16 * sq + 4], -30000.0), "a0ini", 1)
                    p.ev(g.memset(kst[:, 16 * sq + 8:16 * sq + 12], 0.0), "a0ini", 1)
            wl_ev = [p.now("a0wl"), p.now("a0ini")]

            xnf_ev = [None, None]
            xl_ev = [None, None]
            kev_ev = [[], [], [], []]
            kps_ev = [None] * 4
            qs_ev = [None, None]
            vb_ev = [None, None]
            pq_ev = None
            pv_ev = None
            for s in range(NSEQ):
                for j in range(NBLK):
                    bb = (s * NBLK + j) % 2
                    t0 = s * SL + j * BLK

                    @blk.sync
                    def _(sy, bb=bb, t0=t0):
                        p.wt(sy, xnf_ev[bb])
                        for c in range(8):
                            ev = p.ev(sy.dma_start(
                                out=xnf[:, bb * 8 * BLK + c * BLK: bb * 8 * BLK + (c + 1) * BLK],
                                in_=xn[t0:t0 + BLK, TS(c, 128)], transpose=True),
                                f"a0xl{bb}", 16)
                        xl_ev[bb] = ev

                    @blk.tensor
                    def _(t, bb=bb):
                        p.wt(t, wl_ev, xl_ev[bb], *[e for e in kps_ev if e])
                        for kc in range(4):
                            for c in range(8):
                                inst = t.matmul(
                                    pk[kc][:, :],
                                    wk[:, c * HID + kc * 128: c * HID + (kc + 1) * 128],
                                    xnf[:, bb * 8 * BLK + c * BLK: bb * 8 * BLK + (c + 1) * BLK],
                                    start=(c == 0), stop=(c == 7))
                        p.ev(inst, "a0kmm", 1)

                    @blk.scalar
                    def _(a, s=s, j=j):
                        p.wt(a, p.now("a0kmm"))
                        for kc in range(4):
                            p.wt(a, kev_ev[kc])
                            e = p.ev(a.activation(out=kev[:, TS(kc, BLK)], in_=pk[kc][:, :],
                                                  func=AF.Copy), "a0ke", 1)
                            kps_ev[kc] = e
                            p.wt(a, e)
                            sev = p.ev(a.dma_start(
                                out=k0[s, kc * 128:(kc + 1) * 128, j * BLK:(j + 1) * BLK],
                                in_=kev[:, TS(kc, BLK)]), f"a0ks{kc}", 16)
                            kev_ev[kc] = [sev]

                    @blk.vector
                    def _(v, s=s):
                        p.wt(v, p.now("a0ke"), p.now("a0kv"))
                        for kc in range(4):
                            e = p.ev(v.reduce_max(tm1[:, kc:kc + 1], kev[:, TS(kc, BLK)],
                                                  axis=AX.X), "a0kv", 1)
                            kev_ev[kc].append(e)
                        p.wt(v, p.now("a0kv"))
                        p.ev(v.tensor_tensor(out=kst[:, 16 * s:16 * s + 4],
                                             in0=kst[:, 16 * s:16 * s + 4],
                                             in1=tm1[:, 0:4], op=ALU.max), "a0kv", 1)

                    for tt in range(4):
                        ti = (t0 + tt * 128) // 128
                        qb = (s * NBLK * 4 + j * 4 + tt) % 2

                        @blk.tensor
                        def _(t, bb=bb, tt=tt):
                            p.wt(t, pq_ev)
                            for c in range(8):
                                inst = t.matmul(
                                    pq[:, :],
                                    xnf[:, bb * 8 * BLK + c * BLK + tt * 128:
                                        bb * 8 * BLK + c * BLK + (tt + 1) * 128],
                                    wq[:, TS(c, HID)], start=(c == 0), stop=(c == 7))
                            p.ev(inst, "a0qmm", 1)

                        @blk.scalar
                        def _(a):
                            p.wt(a, p.now("a0qmm"), p.now("a0qv"))
                            p.ev(a.activation(out=qf[:, :], in_=pq[:, :][:, 0:HID],
                                              func=AF.Copy), "a0qe", 1)
                        pq_ev = p.now("a0qe")

                        @blk.vector
                        def _(v):
                            p.wt(v, p.now("a0qe"))
                            q3 = qf[:, :].rearrange("p (h d) -> p h d", d=DH)
                            e1 = p.ev(v.reduce_max(mq[:, 0:HEADS], q3, axis=AX.X), "a0qv", 1)
                            p.wt(v, e1)
                            mqb = mq[:, 0:HEADS].unsqueeze(2).broadcast_to([128, HEADS, DH])
                            p.ev(v.tensor_tensor(
                                out=qe[:, :].rearrange("p (h d) -> p h d", d=DH),
                                in0=q3, in1=mqb, op=ALU.subtract), "a0qv", 1)

                        @blk.scalar
                        def _(a):
                            p.wt(a, p.now("a0qv"))
                            p.ev(a.activation(out=qe[:, :], in_=qe[:, :], func=AF.Exp),
                                 "a0qe", 1)

                        @blk.vector
                        def _(v, qb=qb):
                            p.wt(v, p.now("a0qe"))
                            q3 = qe[:, :].rearrange("p (h d) -> p h d", d=DH)
                            e1 = p.ev(v.reduce_sum(mq[:, HEADS:2 * HEADS], q3, axis=AX.X),
                                      "a0qv", 1)
                            p.wt(v, e1)
                            e2 = p.ev(v.reciprocal(mq[:, 2 * HEADS:3 * HEADS],
                                                   mq[:, HEADS:2 * HEADS]), "a0qv", 1)
                            p.wt(v, e2, qs_ev[qb])
                            rqb = mq[:, 2 * HEADS:3 * HEADS].unsqueeze(2).broadcast_to(
                                [128, HEADS, DH])
                            p.ev(v.tensor_tensor(
                                out=qs[:, TS(qb, HID)].rearrange("p (h d) -> p h d", d=DH),
                                in0=q3, in1=rqb, op=ALU.mult), "a0qv", 1)

                        @blk.scalar
                        def _(a, ti=ti, qb=qb):
                            p.wt(a, p.now("a0qv"))
                            qs_ev[qb] = p.ev(a.dma_start(out=q0_t[ti],
                                                         in_=qs[:, TS(qb, HID)]),
                                             f"a0qss{qb}", 16)

                        @blk.tensor
                        def _(t, bb=bb, tt=tt):
                            p.wt(t, pv_ev)
                            for c in range(8):
                                inst = t.matmul(
                                    pv[:, :],
                                    xnf[:, bb * 8 * BLK + c * BLK + tt * 128:
                                        bb * 8 * BLK + c * BLK + (tt + 1) * 128],
                                    wv[:, TS(c, HID)], start=(c == 0), stop=(c == 7))
                            ev = p.ev(inst, "a0vmm", 1)
                            if tt == 3:
                                xnf_ev[bb] = ev

                        @blk.scalar
                        def _(a, ti=ti, qb=qb):
                            p.wt(a, p.now("a0vmm"), vb_ev[qb])
                            e = p.ev(a.activation(out=vb[:, TS(qb, HID)],
                                                  in_=pv[:, :][:, 0:HID], func=AF.Copy),
                                     "a0ve", 1)
                            p.wt(a, e)
                            vb_ev[qb] = p.ev(a.dma_start(out=v0_t[ti],
                                                         in_=vb[:, TS(qb, HID)]),
                                             f"a0vss{qb}", 16)
                        pv_ev = p.now("a0ve")

        stkA.close()
        stkB = ExitStack()
        pctx = stkB.enter_context(nc.psum_tensor("a0pctx", [64, HID], F32))
        patt = stkB.enter_context(nc.psum_tensor("a0patt", [128, BLK], F32))
        py = stkB.enter_context(nc.psum_tensor("a0py", [128, D], F32))

        ctx_ev = [None]
        for s in range(NSEQ):
            with nc.Block(f"a0B{s}") as blk:
                @blk.vector
                def _(v, s=s):
                    p.ev(v.tensor_scalar(out=kst[:, 16 * s + 4:16 * s + 8],
                                         in0=kst[:, 16 * s:16 * s + 4], scalar1=-1.0,
                                         scalar2=None, op0=ALU.mult), "b0v", 1)

                kl_ev = [None, None]
                exp_ev = [None, None]
                eks_ev = [p.now("b0es0"), p.now("b0es1")]
                zp_ev = [None, None]
                for kc in range(4):
                    for j in range(NBLK):
                        bb = (kc * NBLK + j) % 2

                        @blk.sync
                        def _(sy, s=s, kc=kc, j=j, bb=bb):
                            if s == 0 and kc == 0 and j == 0:
                                for kk in range(4):
                                    p.wt(sy, p.now(f"a0ks{kk}"))
                                p.wt(sy, p.now("a0vss0"), p.now("a0vss1"))
                            p.wt(sy, exp_ev[bb])
                            kl_ev[bb] = p.ev(sy.dma_start(
                                out=kev[:, TS(bb, BLK)],
                                in_=k0[s, kc * 128:(kc + 1) * 128, j * BLK:(j + 1) * BLK]),
                                f"b0kl{bb}", 16)

                        @blk.scalar
                        def _(a, s=s, kc=kc, j=j, bb=bb):
                            p.wt(a, kl_ev[bb], zp_ev[bb], eks_ev[bb], p.now("b0v"))
                            e = p.ev(a.activation(out=ekb[:, TS(bb, BLK)],
                                                  in_=kev[:, TS(bb, BLK)], func=AF.Exp,
                                                  bias=kst[:, 16 * s + 4 + kc:
                                                           16 * s + 5 + kc],
                                                  accum_out=zp[:, bb:bb + 1]), "b0e", 1)
                            exp_ev[bb] = e
                            p.wt(a, e)
                            eks_ev[bb] = p.ev(a.dma_start(
                                out=ek0[s, kc * 128:(kc + 1) * 128, j * BLK:(j + 1) * BLK],
                                in_=ekb[:, TS(bb, BLK)]), f"b0es{bb}", 16)

                        @blk.vector
                        def _(v, s=s, kc=kc, bb=bb):
                            p.wt(v, p.now("b0e"), p.now("b0v"))
                            zp_ev[bb] = p.ev(v.tensor_tensor(
                                out=kst[:, 16 * s + 8 + kc:16 * s + 9 + kc],
                                in0=kst[:, 16 * s + 8 + kc:16 * s + 9 + kc],
                                in1=zp[:, bb:bb + 1], op=ALU.add), "b0v", 1)

                @blk.vector
                def _(v, s=s):
                    p.wt(v, p.now("b0v"))
                    p.ev(v.reciprocal(kst[:, 16 * s + 12:16 * s + 16],
                                      kst[:, 16 * s + 8:16 * s + 12]), "b0v", 1)

                @blk.scalar
                def _(a, s=s):
                    p.wt(a, p.now("b0v"))
                    if s == 1:
                        p.wt(a, p.now("b0zl0"), p.now("b0cv"))
                    e = p.ev(a.dma_start(out=p.dram("zsc"),
                                         in_=kst[:, 16 * s + 12:16 * s + 16]),
                             f"b0zs{s}", 16)
                    p.wt(a, e)
                    p.ev(a.dma_start(
                        out=zlo[:, :].rearrange("d (hp kc) -> d hp kc", hp=2),
                        in_=p.dram("zsc").rearrange("(hp d) kc -> d hp kc", hp=2)),
                        f"b0zl{s}", 16)

                et_ev = [p.now("b0mm"), p.now("b0mm")]
                vt_ev = [p.now("b0mm"), p.now("b0mm")]
                el_evs = [None, None]
                vl_evs = [None, None]
                for i in range(TPS):
                    bb = i % 2

                    @blk.sync
                    def _(sy, s=s, i=i, bb=bb):
                        if i == 0:
                            p.wt(sy, p.now("b0es0"), p.now("b0es1"))
                        p.wt(sy, et_ev[bb])
                        el_evs[bb] = p.ev(sy.dma_start(
                            out=ektm[:, TS(bb, HID)],
                            in_=ek0[s, 0:HID, i * 128:(i + 1) * 128],
                            transpose=True), f"b0el{bb}", 16)

                    @blk.scalar
                    def _(a, s=s, i=i, bb=bb):
                        p.wt(a, vt_ev[bb])
                        vl_evs[bb] = p.ev(a.dma_start(out=vtm[:, TS(bb, HID)],
                                                      in_=v0_t[(s * SL) // 128 + i]),
                                          f"b0vl{bb}", 16)

                    @blk.tensor
                    def _(t, s=s, i=i, bb=bb):
                        p.wt(t, el_evs[bb], vl_evs[bb])
                        if i == 0:
                            p.wt(t, ctx_ev[0])
                        for h in range(HEADS):
                            inst = t.matmul(
                                pctx[:, TS(h, DH)],
                                ektm[:, bb * HID + h * DH: bb * HID + (h + 1) * DH],
                                vtm[:, bb * HID + h * DH: bb * HID + (h + 1) * DH],
                                start=(i == 0 and h == 0),
                                stop=(i == TPS - 1 and h == HEADS - 1))
                        e = p.ev(inst, "b0mm", 1)
                        et_ev[bb] = e
                        vt_ev[bb] = e

                @blk.vector
                def _(v, s=s):
                    p.wt(v, p.now("b0mm"), p.now("b0v"), p.now(f"b0zl{s}"))
                    for h in range(HEADS):
                        zc = zlo[:, (h % 2) * 4 + h // 2: (h % 2) * 4 + h // 2 + 1]
                        e = p.ev(v.tensor_scalar(out=ctxn[:, TS(h, DH)],
                                                 in0=pctx[:, TS(h, DH)],
                                                 scalar1=zc,
                                                 scalar2=float(SCALE), op0=ALU.mult,
                                                 op1=ALU.mult), "b0cv", 1)
                    ctx_ev[0] = e

                @blk.scalar
                def _(a, s=s):
                    p.wt(a, p.now("b0cv"))
                    if s == 1:
                        p.wt(a, p.now("c0am"))
                    e = p.ev(a.dma_start(
                        out=p.dram("ctxs").rearrange("hp d hc e -> d hc hp e"),
                        in_=ctxn[:, :].rearrange("p (hc hp e) -> p hc hp e",
                                                 hp=2, e=DH)),
                        f"b0cs{s}", 16)
                    p.wt(a, e)
                    p.ev(a.dma_start(
                        out=ctx2[:, :].rearrange("p (hc e) -> p hc e", e=DH),
                        in_=p.dram("ctxs").rearrange("hp d hc e -> (hp d) hc e")),
                        f"b0c2{s}", 16)

            with nc.Block(f"a0C{s}") as blk:
                qf_ev = [p.now("c0am"), p.now("c0am")]
                xr_ev = [p.now("c0v"), p.now("c0v")]
                af_hold = [p.now("c0ym")]
                xm_ev = [p.now("c0xs0"), p.now("c0xs1")]
                ql_evs = [None, None]
                xl_evs = [None, None]
                patt_ev = p.now("c0ae")
                py_ev = p.now("c0v")
                for i in range(TPS):
                    bb = i % 2
                    ti = (s * SL) // 128 + i

                    @blk.sync
                    def _(sy, s=s, i=i, bb=bb, ti=ti):
                        if i == 0 and s == 0:
                            p.wt(sy, p.now("a0qss0"), p.now("a0qss1"))
                        p.wt(sy, qf_ev[bb])
                        for c in range(4):
                            e = p.ev(sy.dma_start(
                                out=qfm[:, bb * HID + c * 128: bb * HID + (c + 1) * 128],
                                in_=q0[ti * 128:(ti + 1) * 128, TS(c, 128)],
                                transpose=True), f"c0ql{bb}", 16)
                        ql_evs[bb] = e

                    @blk.scalar
                    def _(a, i=i, bb=bb, ti=ti):
                        p.wt(a, xr_ev[bb])
                        xl_evs[bb] = p.ev(a.dma_start(out=xr[:, TS(bb, D)], in_=xin_t[ti]),
                                          f"c0xl{bb}", 16)

                    @blk.tensor
                    def _(t, i=i, bb=bb, s=s):
                        p.wt(t, ql_evs[bb], patt_ev)
                        if i == 0:
                            p.wt(t, p.now(f"b0c2{s}"))
                        for c in range(4):
                            for hh in range(2):
                                pb = hh * 64
                                inst = t.matmul(
                                    patt[pb:pb + 64, TS(c, 128)],
                                    ctx2[pb:pb + 64, TS(c, DH)],
                                    qfm[:, bb * HID + c * 128:
                                        bb * HID + (c + 1) * 128][pb:pb + 64, :],
                                    start=(c == 0), stop=(c == 3),
                                    skip_group_check=True)
                        qf_ev[bb] = p.ev(inst, "c0am", 1)

                    @blk.scalar
                    def _(a, i=i, bb=bb):
                        p.wt(a, p.now("c0am"), af_hold[0])
                        p.ev(a.activation(out=af[:, :], in_=patt[:, :],
                                          func=AF.Copy), "c0ae", 1)
                    patt_ev = p.now("c0ae")

                    @blk.tensor
                    def _(t, i=i, bb=bb):
                        p.wt(t, p.now("c0ae"), py_ev)
                        for half in range(2):
                            for c in range(4):
                                t.matmul(py[:, TS(half, 512)],
                                         af[:, TS(c, 128)],
                                         wo[:, c * D + half * 512: c * D + (half + 1) * 512],
                                         start=(c == 0), stop=False)
                            inst = t.matmul(py[:, TS(half, 512)], ones1[:, :],
                                            bo[:, TS(half, 512)], start=False, stop=True)
                        af_hold[0] = p.ev(inst, "c0ym", 1)

                    @blk.vector
                    def _(v, i=i, bb=bb):
                        p.wt(v, p.now("c0ym"), xl_evs[bb], xm_ev[bb])
                        e = p.ev(v.tensor_tensor(out=xmb[:, TS(bb, D)], in0=py[:, :],
                                                 in1=xr[:, TS(bb, D)], op=ALU.add), "c0v", 1)
                        xr_ev[bb] = e
                    py_ev = p.now("c0v")

                    @blk.scalar
                    def _(a, i=i, bb=bb, ti=ti):
                        p.wt(a, p.now("c0v"))
                        xm_ev[bb] = p.ev(a.dma_start(out=xmid_t[ti], in_=xmb[:, TS(bb, D)]),
                                         f"c0xs{bb}", 16)

        stkB.close()
        with nc.Block("a0F") as blk:
            p.fence(blk)


def _ph_ffn(p: P, layer):
    nc = p.nc
    if layer == 0:
        xn, base, outs, pre = p.dram("xn0"), p.dram("xmid0"), p.dram("x1"), "f0"
    else:
        xn, base, outs, pre = p.dram("xn1"), p.dram("xmid1"), p.dram("out"), "f1"
    outs_t = outs.rearrange("(n p) d -> n p d", p=128)
    with ExitStack() as stk:
        p.stack = stk
        w1t = p.sb(pre + "w1", [128, 8 * FFD], BF16)
        w2t = p.sb(pre + "w2", [128, 8 * FFD], BF16)
        xnf = p.sb(pre + "xnf", [128, 8 * BLK], BF16)
        hb = p.sb(pre + "hb", [128, 32 * BLK], BF16)
        yst = p.sb(pre + "yst", [128, 4 * D], F32)
        ones1 = p.sb(pre + "o1", [1, 128], BF16)
        ones5 = p.sb(pre + "o5", [1, BLK], BF16)
        b1e = p.sb(pre + "b1", [1, FFD], BF16)
        b2e = p.sb(pre + "b2", [1, D], BF16)
        ph_ = [p.ps(pre + "ph0", [128, BLK]), p.ps(pre + "ph1", [128, BLK])]
        py_ = [p.ps(pre + "py0", [128, BLK]), p.ps(pre + "py1", [128, BLK])]

        with nc.Block(pre + "R") as blk:
            @blk.gpsimd
            def _(g):
                p.ev(g.memset(ones1[:, :], 1.0), pre + "g", 1)
                p.ev(g.memset(ones5[:, :], 1.0), pre + "g", 1)
                p.ev(g.dma_start(out=outs, in_=base), pre + "bc", 16)

            if layer == 0:
                rxn = p.sb(pre + "rxn", [128, 8 * 128], BF16)
                rxnlo = p.sb(pre + "rxl", [128, 8 * 128], BF16)
                wrt = p.sb(pre + "wrt", [128, 8 * 128], BF16)
                wrtlo = p.sb(pre + "wrtl", [128, 8 * 128], BF16)
                brr = p.sb(pre + "brr", [1, 8], BF16)
                brf = p.sb(pre + "brf", [1, 8], F32)
                brh = p.sb(pre + "brh", [1, 8], F32)
                brlo = p.sb(pre + "brlo", [1, 8], BF16)
                lgt = p.sb(pre + "lgt", [128, 8], F32)
                lmx = p.sb(pre + "lmx", [128, 1], F32)
                mcols = p.sb(pre + "mc", [128, NT * 8], F32)
                plg = p.ps(pre + "plg", [128, 512])
                wrbd = p.dram("wrb")
                wrlod = p.dram("wrlob")
                xnlo = p.dram("xnlo0")

                @blk.sync
                def _(sy):
                    for c in range(8):
                        p.ev(sy.dma_start(out=wrt[:, TS(c, 128)],
                                          in_=wrbd[:, TS(c, 128)], transpose=True),
                             pre + "wl", 16)
                        p.ev(sy.dma_start(out=wrtlo[:, TS(c, 128)],
                                          in_=wrlod[:, TS(c, 128)], transpose=True),
                             pre + "wl", 16)

                @blk.gpsimd
                def _(g):
                    p.ev(g.dma_start(out=brr[:, :], in_=p.dram("L0_ffn_br")[None, :]),
                         pre + "wlb", 16)
                    p.ev(g.dma_start(out=brf[:, :], in_=p.dram("L0_ffn_br")[None, :]),
                         pre + "wlb", 16)

                @blk.scalar
                def _(a):
                    p.wt(a, p.now(pre + "wlb"))
                    p.ev(a.activation(out=brh[:, :], in_=brr[:, :], func=AF.Copy),
                         pre + "ba", 1)

                @blk.vector
                def _(v):
                    p.wt(v, p.now(pre + "ba"))
                    p.ev(v.tensor_tensor(out=brlo[:, :], in0=brf[:, :], in1=brh[:, :],
                                         op=ALU.subtract), pre + "bv", 1)
                rt_wl = [p.now(pre + "wl"), p.now(pre + "wlb"), p.now(pre + "bv")]

                xe_ev = [None, None]
                rxl_evs = [None, None]
                lg_ev = None
                for ti in range(NT):
                    bb = ti % 2

                    @blk.sync
                    def _(sy, ti=ti):
                        p.wt(sy, xe_ev[0])
                        for c in range(8):
                            p.ev(sy.dma_start(
                                out=rxn[:, TS(c, 128)],
                                in_=xn[ti * 128:(ti + 1) * 128, TS(c, 128)],
                                transpose=True), pre + "rxl", 16)
                            e = p.ev(sy.dma_start(
                                out=rxnlo[:, TS(c, 128)],
                                in_=xnlo[ti * 128:(ti + 1) * 128, TS(c, 128)],
                                transpose=True), pre + "rxl", 16)
                        rxl_evs[0] = e

                    @blk.tensor
                    def _(t, ti=ti):
                        p.wt(t, rt_wl, rxl_evs[0], lg_ev)
                        for c in range(8):
                            t.matmul(plg[:, 0:8],
                                     rxn[:, TS(c, 128)],
                                     wrt[:, c * 128:c * 128 + 8],
                                     start=(c == 0), stop=False)
                        for c in range(8):
                            t.matmul(plg[:, 0:8],
                                     rxnlo[:, TS(c, 128)],
                                     wrt[:, c * 128:c * 128 + 8],
                                     start=False, stop=False)
                        for c in range(8):
                            t.matmul(plg[:, 0:8],
                                     rxn[:, TS(c, 128)],
                                     wrtlo[:, c * 128:c * 128 + 8],
                                     start=False, stop=False)
                        t.matmul(plg[:, 0:8], ones1[:, :], brr[:, :],
                                 start=False, stop=False)
                        inst = t.matmul(plg[:, 0:8], ones1[:, :], brlo[:, :],
                                        start=False, stop=True)
                        xe_ev[0] = p.ev(inst, pre + "rmm", 1)

                    @blk.scalar
                    def _(a, ti=ti):
                        p.wt(a, p.now(pre + "rmm"), p.now(pre + "rv"))
                        p.ev(a.activation(out=lgt[:, :], in_=plg[:, 0:8], func=AF.Copy),
                             pre + "re", 1)
                    lg_ev = p.now(pre + "re")

                    @blk.vector
                    def _(v, ti=ti):
                        p.wt(v, p.now(pre + "re"))
                        e = p.ev(v.reduce_max(lmx[:, :], lgt[:, :], axis=AX.X), pre + "rv", 1)
                        p.wt(v, e)
                        p.ev(v.tensor_scalar(out=mcols[:, ti * 8:(ti + 1) * 8],
                                             in0=lgt[:, :], scalar1=lmx[:, 0:1],
                                             scalar2=None, op0=ALU.is_ge), pre + "rv", 1)

                @blk.gpsimd
                def _(g):
                    p.wt(g, p.now(pre + "rv"))
                    p.ev(g.dma_start(out=p.dram("mc0"), in_=mcols[:, :]),
                         pre + "mcs", 16)
            else:
                wr1 = p.sb(pre + "wr1", [128, 8 * 8], F32)
                pcol = p.sb(pre + "pc", [128, 8], F32)
                lgt = p.sb(pre + "lg", [1, 32], F32)
                onesf = p.sb(pre + "of", [1, 128], F32)
                wcols = p.sb(pre + "wc", [128, NSEQ * 8], F32)
                plg = p.ps(pre + "plg", [128, 512])
                pool1 = p.dram("pool1")

                @blk.gpsimd
                def _(g):
                    p.ev(g.memset(onesf[:, :], 1.0), pre + "g", 1)
                    ap = p.dram("L1_ffn_Wr").rearrange("(c p) e -> c p e", p=128)
                    for c in range(8):
                        p.ev(g.dma_start(out=wr1[:, TS(c, 8)], in_=ap[c]), pre + "wl", 16)

                for s in range(NSEQ):
                    @blk.gpsimd
                    def _(g, s=s):
                        p.wt(g, p.now(pre + "rmm"))
                        p.ev(g.dma_start(out=pcol[:, :],
                                         in_=pool1[s].rearrange("(c p) -> p c", p=128),
                                         allow_slow_non_contiguous=True), pre + "pl", 16)

                    @blk.tensor
                    def _(t, s=s):
                        p.wt(t, p.now(pre + "wl"), p.now(pre + "pl"), p.now(pre + "re"))
                        for c in range(8):
                            inst = t.matmul(plg[0:1, 0:8], pcol[:, c:c + 1],
                                            wr1[:, TS(c, 8)], start=(c == 0), stop=(c == 7))
                        p.ev(inst, pre + "rmm", 1)

                    @blk.scalar
                    def _(a, s=s):
                        p.wt(a, p.now(pre + "rmm"), p.now(pre + "rv"))
                        p.ev(a.activation(out=lgt[:, 0:8], in_=plg[0:1, 0:8], func=AF.Copy),
                             pre + "re", 1)

                    @blk.vector
                    def _(v, s=s):
                        p.wt(v, p.now(pre + "re"))
                        e = p.ev(v.reduce_max(lgt[:, 8:9], lgt[:, 0:8], axis=AX.X),
                                 pre + "rv", 1)
                        p.wt(v, e)
                        e = p.ev(v.tensor_scalar(out=lgt[:, 16:24], in0=lgt[:, 0:8],
                                                 scalar1=lgt[:, 8:9], scalar2=None,
                                                 op0=ALU.is_ge), pre + "rv", 1)
                        p.wt(v, e)
                        e = p.ev(v.scalar_tensor_tensor(out=lgt[:, 24:32], in0=lgt[:, 16:24],
                                                        scalar=-1e30, in1=lgt[:, 0:8],
                                                        op0=ALU.mult, op1=ALU.add),
                                 pre + "rv", 1)
                        p.wt(v, e)
                        e = p.ev(v.reduce_max(lgt[:, 9:10], lgt[:, 24:32], axis=AX.X),
                                 pre + "rv", 1)
                        p.wt(v, e)
                        e = p.ev(v.tensor_scalar(out=lgt[:, 24:32], in0=lgt[:, 24:32],
                                                 scalar1=lgt[:, 9:10], scalar2=None,
                                                 op0=ALU.is_ge), pre + "rv", 1)
                        p.wt(v, e)
                        e = p.ev(v.tensor_tensor(out=lgt[:, 10:11], in0=lgt[:, 9:10],
                                                 in1=lgt[:, 8:9], op=ALU.subtract),
                                 pre + "rv", 1)

                    @blk.scalar
                    def _(a, s=s):
                        p.wt(a, p.now(pre + "rv"))
                        p.ev(a.activation(out=lgt[:, 11:12], in_=lgt[:, 10:11], func=AF.Exp),
                             pre + "re", 1)

                    @blk.vector
                    def _(v, s=s):
                        p.wt(v, p.now(pre + "re"))
                        e = p.ev(v.tensor_scalar(out=lgt[:, 12:13], in0=lgt[:, 11:12],
                                                 scalar1=1.0, scalar2=None, op0=ALU.add),
                                 pre + "rv", 1)
                        p.wt(v, e)
                        e = p.ev(v.reciprocal(lgt[:, 13:14], lgt[:, 12:13]), pre + "rv", 1)
                        p.wt(v, e)
                        e = p.ev(v.tensor_tensor(out=lgt[:, 14:15], in0=lgt[:, 11:12],
                                                 in1=lgt[:, 13:14], op=ALU.mult), pre + "rv", 1)
                        p.wt(v, e)
                        e = p.ev(v.tensor_scalar(out=lgt[:, 16:24], in0=lgt[:, 16:24],
                                                 scalar1=lgt[:, 13:14], scalar2=None,
                                                 op0=ALU.mult), pre + "rv", 1)
                        p.wt(v, e)
                        e = p.ev(v.scalar_tensor_tensor(out=lgt[:, 24:32], in0=lgt[:, 24:32],
                                                        scalar=lgt[:, 14:15],
                                                        in1=lgt[:, 16:24],
                                                        op0=ALU.mult, op1=ALU.add),
                                 pre + "rv", 1)

                    @blk.tensor
                    def _(t, s=s):
                        p.wt(t, p.now(pre + "rv"), p.now(pre + "re"))
                        inst = t.matmul(plg[:, 8:16], onesf[:, :], lgt[:, 24:32],
                                        start=True, stop=True)
                        p.ev(inst, pre + "rmm", 1)

                    @blk.scalar
                    def _(a, s=s):
                        p.wt(a, p.now(pre + "rmm"))
                        p.ev(a.activation(out=wcols[:, TS(s, 8)], in_=plg[:, 8:16],
                                          func=AF.Copy), pre + "re", 1)

        with nc.Block(pre + "E") as blk:
            if layer == 0:
                w1bd = p.dram("w1b")
                w2bd = p.dram("w2b")
            acc_ev = {}
            xl_now = [None]
            hch_ev = [None, None]
            ych_ev = [None, None]
            hbl_ev = [None] * 32
            yst_ev = [None] * 8
            hmm_now = [None]
            ymm_now = [None]
            for e in range(NE):
                @blk.sync
                def _(sy, e=e):
                    if e >= 1:
                        p.wt(sy, p.now(pre + "hmm"))
                    if layer == 0:
                        for dc in range(8):
                            for fg in range(8):
                                p.ev(sy.dma_start(
                                    out=w1t[:, dc * FFD + fg * 512: dc * FFD + (fg + 1) * 512],
                                    in_=w1bd[e * FFD + fg * 512:e * FFD + (fg + 1) * 512,
                                             TS(dc, 128)], transpose=True), pre + "w1l", 16)

                @blk.sync
                def _(sy, e=e):
                    if e >= 1:
                        p.wt(sy, p.now(pre + "ymm"))
                    if layer == 0:
                        for fc in range(32):
                            for dh in range(2):
                                p.ev(sy.dma_start(
                                    out=w2t[:, fc * 1024 + dh * 512: fc * 1024 + (dh + 1) * 512],
                                    in_=w2bd[e * D + dh * 512:e * D + (dh + 1) * 512,
                                             TS(fc, 128)], transpose=True), pre + "w2l", 16)

                @blk.gpsimd
                def _(g, e=e):
                    if e >= 1:
                        p.wt(g, p.now(pre + "hmm"), p.now(pre + "ymm"))
                    if layer == 1:
                        ap = p.dram("L1_ffn_W1")[e].rearrange("(dc p) f -> dc p f", p=128)
                        for dc in range(8):
                            p.ev(g.dma_start(out=w1t[:, TS(dc, FFD)], in_=ap[dc]),
                                 pre + "w1l", 16)
                        ap2 = p.dram("L1_ffn_W2")[e].rearrange("(fc p) d -> fc p d", p=128)
                        for fc in range(32):
                            p.ev(g.dma_start(out=w2t[:, TS(fc, D)], in_=ap2[fc]),
                                 pre + "w2l", 16)
                    p.ev(g.dma_start(out=b1e[:, :],
                                     in_=(p.dram("L0_ffn_b1") if layer == 0 else
                                          p.dram("L1_ffn_b1"))[e][None, :]), pre + "bl", 16)
                    p.ev(g.dma_start(out=b2e[:, :],
                                     in_=(p.dram("L0_ffn_b2") if layer == 0 else
                                          p.dram("L1_ffn_b2"))[e][None, :]), pre + "bl", 16)
                wle = [p.now(pre + "w1l"), p.now(pre + "w2l"), p.now(pre + "bl"),
                       p.now(pre + "g")]

                for blkid in range(NBLKC):
                    s = blkid // NBLK
                    t0 = blkid * BLK

                    @blk.sync
                    def _(sy, e=e, blkid=blkid, t0=t0):
                        p.wt(sy, p.now(pre + "hmm"))
                        for c in range(8):
                            ev = p.ev(sy.dma_start(
                                out=xnf[:, c * BLK:(c + 1) * BLK],
                                in_=xn[t0:t0 + BLK, TS(c, 128)], transpose=True),
                                pre + "xl", 16)
                        xl_now[0] = ev

                    for fc in range(32):
                        @blk.tensor
                        def _(t, e=e, fc=fc):
                            if fc == 0:
                                p.wt(t, *wle)
                                p.wt(t, xl_now[0])
                            p.wt(t, hch_ev[fc % 2])
                            for dc in range(8):
                                t.matmul(ph_[fc % 2][:, :],
                                         w1t[:, dc * FFD + fc * 128: dc * FFD + (fc + 1) * 128],
                                         xnf[:, TS(dc, BLK)], start=(dc == 0), stop=False)
                            inst = t.matmul(ph_[fc % 2][:, :], b1e[:, TS(fc, 128)],
                                            ones5[:, :], start=False, stop=True)
                            hmm_now[0] = p.ev(inst, pre + "hmm", 1)

                        @blk.scalar
                        def _(a, fc=fc):
                            p.wt(a, hmm_now[0], hbl_ev[fc])
                            hch_ev[fc % 2] = p.ev(a.activation(
                                out=hb[:, TS(fc, BLK)], in_=ph_[fc % 2][:, :],
                                func=(AF.Relu if layer == 0 else AF.Gelu)), pre + "he", 1)

                    for tt in range(4):
                        for half in range(2):
                            pid = (2 * tt + half) % 2

                            @blk.tensor
                            def _(t, e=e, tt=tt, half=half, pid=pid):
                                if tt == 0 and half == 0:
                                    p.wt(t, p.now(pre + "he"))
                                p.wt(t, ych_ev[pid])
                                for fc in range(32):
                                    t.matmul(
                                        py_[pid][:, :],
                                        hb[:, fc * BLK + tt * 128: fc * BLK + (tt + 1) * 128],
                                        w2t[:, fc * 1024 + half * 512:
                                            fc * 1024 + (half + 1) * 512] if layer == 0 else
                                        w2t[:, fc * D + half * 512: fc * D + (half + 1) * 512],
                                        start=(fc == 0), stop=False)
                                inst = t.matmul(py_[pid][:, :], ones1[:, :],
                                                b2e[:, TS(half, 512)], start=False, stop=True)
                                ymm_now[0] = p.ev(inst, pre + "ymm", 1)
                                if tt == 3 and half == 1:
                                    for fc in range(32):
                                        hbl_ev[fc] = ymm_now[0]

                            @blk.vector
                            def _(v, e=e, blkid=blkid, s=s, tt=tt, half=half, pid=pid):
                                p.wt(v, ymm_now[0], yst_ev[2 * tt + half])
                                if layer == 0:
                                    sc = mcols[:, (blkid * 4 + tt) * 8 + e:
                                               (blkid * 4 + tt) * 8 + e + 1]
                                else:
                                    sc = wcols[:, s * 8 + e:s * 8 + e + 1]
                                ych_ev[pid] = p.ev(v.tensor_scalar(
                                    out=yst[:, tt * D + half * 512: tt * D + (half + 1) * 512],
                                    in0=py_[pid][:, :], scalar1=sc, scalar2=None,
                                    op0=ALU.mult), pre + "yv", 1)

                    yv_done = p.now(pre + "yv")
                    dbg_here = (e == 0 and blkid == 0 and layer == 0)

                    @blk.gpsimd
                    def _(g, e=e, blkid=blkid, dbg_here=dbg_here):
                        p.wt(g, yv_done)
                        p.wt(g, acc_ev.get(blkid))
                        if e == 0:
                            p.wt(g, p.now(pre + "bc"))
                        if dbg_here:
                            p.wt(g, p.now(pre + "he"))
                            p.ev(g.dma_start(out=p.dram("hdbg"), in_=hb[:, :]),
                                 pre + "hdx", 16)
                            p.ev(g.dma_start(out=p.dram("ydbg"), in_=yst[:, :]),
                                 pre + "hdx", 16)
                        for tt in range(4):
                            ev = p.ev(g.dma_start(
                                out=outs_t[blkid * 4 + tt],
                                in_=yst[:, TS(tt, D)], accum_op=ALU.add),
                                pre + f"ac{blkid}", 16)
                        for q in range(8):
                            yst_ev[q] = [ev, p.now(pre + "hdx")] if dbg_here else ev
                        if dbg_here:
                            for fc in range(32):
                                hbl_ev[fc] = [hbl_ev[fc], p.now(pre + "hdx")]
                        acc_ev[blkid] = ev

        with nc.Block(pre + "F") as blk:
            p.fence(blk)


def ph_ffn0(p: P):
    _ph_ffn(p, 0)


def ph_ffn1(p: P):
    _ph_ffn(p, 1)


def ph_attn1(p: P):
    nc = p.nc
    xn = p.dram("xn1")
    x1_t = p.dram("x1").rearrange("(n p) d -> n p d", p=128)
    xm_t = p.dram("xmid1").rearrange("(n p) d -> n p d", p=128)
    th1 = p.dram("th1")
    th1_t = th1.rearrange("(n p) d -> n p d", p=128)
    ph1_t = p.dram("ph1").rearrange("(n p) d -> n p d", p=128)
    g1_t = p.dram("g1").rearrange("(n p) d -> n p d", p=128)
    with ExitStack() as stk:
        p.stack = stk
        wth = p.sb("a1wth", [128, 8 * LAT], BF16)
        wph = p.sb("a1wph", [128, 8 * LAT], BF16)
        wg = p.sb("a1wg", [128, 8 * D], BF16)
        xnf = p.sb("a1xnf", [128, 2 * 8 * 128], BF16)
        thb = p.sb("a1thb", [128, 2 * LAT], BF16)
        phb = p.sb("a1phb", [128, 2 * LAT], BF16)
        gb = p.sb("a1gb", [128, 2 * D], BF16)
        phl = p.sb("a1phl", [128, 2 * LAT], BF16)
        gl = p.sb("a1gl", [128, 2 * D], BF16)
        msb = p.sb("a1msb", [128, 2 * D], BF16)
        thf = p.sb("a1thf", [128, 2 * LAT], BF16)
        x1b = p.sb("a1x1", [128, 2 * D], F32)
        xmb = p.sb("a1xm", [128, 2 * D], F32)
        stkP = ExitStack()
        pth = stkP.enter_context(nc.psum_tensor("a1pth", [128, 512], F32))
        pph = stkP.enter_context(nc.psum_tensor("a1pph", [128, 512], F32))
        pg = stkP.enter_context(nc.psum_tensor("a1pg", [128, D], F32))

        with nc.Block("a1P") as blk:
            @blk.sync
            def _(sy):
                wthb, wphb, wgb = p.dram("wthb"), p.dram("wphb"), p.dram("wgb")
                for c in range(8):
                    p.ev(sy.dma_start(out=wth[:, TS(c, LAT)],
                                      in_=wthb[0:LAT, TS(c, 128)], transpose=True),
                         "a1wl", 16)
                    p.ev(sy.dma_start(out=wph[:, TS(c, LAT)],
                                      in_=wphb[0:LAT, TS(c, 128)], transpose=True),
                         "a1wl", 16)
                    for hf in range(2):
                        p.ev(sy.dma_start(out=wg[:, c * D + hf * 512: c * D + (hf + 1) * 512],
                                          in_=wgb[hf * 512:(hf + 1) * 512, TS(c, 128)],
                                          transpose=True), "a1wl", 16)
            wl_ev = p.now("a1wl")
            xnf_ev = [None, None]
            xl_evs = [None, None]
            st_ev = [None, None]
            pev = [None]
            for ti in range(NT):
                bb = ti % 2

                @blk.sync
                def _(sy, ti=ti, bb=bb):
                    p.wt(sy, xnf_ev[bb])
                    for c in range(8):
                        ev = p.ev(sy.dma_start(
                            out=xnf[:, bb * 8 * 128 + c * 128: bb * 8 * 128 + (c + 1) * 128],
                            in_=xn[ti * 128:(ti + 1) * 128, TS(c, 128)], transpose=True),
                            f"a1xl{bb}", 16)
                    xl_evs[bb] = ev

                @blk.tensor
                def _(t, ti=ti, bb=bb):
                    p.wt(t, wl_ev, xl_evs[bb], pev[0])
                    for c in range(8):
                        t.matmul(pth[:, 0:LAT],
                                 xnf[:, bb * 8 * 128 + c * 128: bb * 8 * 128 + (c + 1) * 128],
                                 wth[:, TS(c, LAT)], start=(c == 0), stop=(c == 7))
                    for c in range(8):
                        t.matmul(pph[:, 0:LAT],
                                 xnf[:, bb * 8 * 128 + c * 128: bb * 8 * 128 + (c + 1) * 128],
                                 wph[:, TS(c, LAT)], start=(c == 0), stop=(c == 7))
                    for hf in range(2):
                        for c in range(8):
                            inst = t.matmul(
                                pg[:, TS(hf, 512)],
                                xnf[:, bb * 8 * 128 + c * 128: bb * 8 * 128 + (c + 1) * 128],
                                wg[:, c * D + hf * 512: c * D + (hf + 1) * 512],
                                start=(c == 0), stop=(c == 7))
                    xnf_ev[bb] = p.ev(inst, "a1mm", 1)

                @blk.scalar
                def _(a, ti=ti, bb=bb):
                    p.wt(a, p.now("a1mm"), st_ev[bb])
                    p.ev(a.activation(out=thb[:, TS(bb, LAT)], in_=pth[:, 0:LAT],
                                      func=AF.Copy), "a1e", 1)
                    p.ev(a.activation(out=phb[:, TS(bb, LAT)], in_=pph[:, 0:LAT],
                                      func=AF.Copy), "a1e", 1)
                    pev[0] = p.ev(a.activation(out=gb[:, TS(bb, D)], in_=pg[:, :],
                                               func=AF.Copy), "a1e", 1)
                    p.wt(a, pev[0])
                    p.ev(a.dma_start(out=th1_t[ti], in_=thb[:, TS(bb, LAT)]),
                         f"a1st{bb}", 16)
                    p.ev(a.dma_start(out=ph1_t[ti], in_=phb[:, TS(bb, LAT)]),
                         f"a1st{bb}", 16)
                    st_ev[bb] = p.ev(a.dma_start(out=g1_t[ti], in_=gb[:, TS(bb, D)]),
                                     f"a1st{bb}", 16)

        stkP.close()
        stkM = ExitStack()
        pM = [stkM.enter_context(nc.psum_tensor(f"a1pM{j}", [128, 512], F32))
              for j in range(4)]
        pout = stkM.enter_context(nc.psum_tensor("a1po", [128, D], F32))

        mev_ev = [None]
        for s in range(NSEQ):
            with nc.Block(f"a1M{s}") as blk:
                pl_ev = [p.now("a1Mm"), p.now("a1Mm")]
                ml_evs = [None, None]
                for i in range(TPS):
                    bb = i % 2
                    ti = s * TPS + i

                    @blk.sync
                    def _(sy, ti=ti, bb=bb, i=i):
                        if i == 0:
                            p.wt(sy, p.now("a1st0"), p.now("a1st1"))
                        p.wt(sy, pl_ev[bb])
                        p.ev(sy.dma_start(out=phl[:, TS(bb, LAT)], in_=ph1_t[ti]),
                             f"a1pl{bb}", 16)
                        ml_evs[bb] = p.ev(sy.dma_start(out=gl[:, TS(bb, D)], in_=g1_t[ti]),
                                          f"a1pl{bb}", 16)

                    @blk.tensor
                    def _(t, i=i, bb=bb, s=s):
                        p.wt(t, ml_evs[bb])
                        if i == 0 and s == 1:
                            p.wt(t, mev_ev[0])
                        for lc in range(2):
                            for oh in range(2):
                                inst = t.matmul(
                                    pM[2 * lc + oh][:, :],
                                    phl[:, bb * LAT + lc * 128: bb * LAT + (lc + 1) * 128],
                                    gl[:, bb * D + oh * 512: bb * D + (oh + 1) * 512],
                                    start=(i == 0), stop=(i == TPS - 1))
                        pl_ev[bb] = p.ev(inst, "a1Mm", 1)

                @blk.vector
                def _(v, s=s):
                    p.wt(v, p.now("a1Mm"))
                    if s == 1:
                        p.wt(v, p.now("a1om"))
                    for lc in range(2):
                        for oh in range(2):
                            e = p.ev(v.tensor_scalar(
                                out=msb[:, lc * D + oh * 512: lc * D + (oh + 1) * 512],
                                in0=pM[2 * lc + oh][:, :], scalar1=float(L1SCALE),
                                scalar2=None, op0=ALU.mult), "a1Mv", 1)
                    mev_ev[0] = e

            with nc.Block(f"a1O{s}") as blk:
                tf_ev = [p.now("a1om"), p.now("a1om")]
                x1_ev = [p.now("a1ov"), p.now("a1ov")]
                xs_ev = [p.now("a1os0"), p.now("a1os1")]
                tl_evs = [None, None]
                xl2_evs = [None, None]
                po_ev = p.now("a1ov")
                for i in range(TPS):
                    bb = i % 2
                    ti = s * TPS + i

                    @blk.sync
                    def _(sy, ti=ti, bb=bb, i=i):
                        if i == 0:
                            p.wt(sy, p.now("a1st0"), p.now("a1st1"))
                        p.wt(sy, tf_ev[bb])
                        for lc in range(2):
                            ev = p.ev(sy.dma_start(
                                out=thf[:, bb * LAT + lc * 128: bb * LAT + (lc + 1) * 128],
                                in_=th1[ti * 128:(ti + 1) * 128, TS(lc, 128)],
                                transpose=True), f"a1tl{bb}", 16)
                        tl_evs[bb] = ev

                    @blk.scalar
                    def _(a, ti=ti, bb=bb):
                        p.wt(a, x1_ev[bb])
                        xl2_evs[bb] = p.ev(a.dma_start(out=x1b[:, TS(bb, D)], in_=x1_t[ti]),
                                           f"a1xr{bb}", 16)

                    @blk.tensor
                    def _(t, i=i, bb=bb, s=s):
                        p.wt(t, tl_evs[bb], po_ev)
                        if i == 0:
                            p.wt(t, p.now("a1Mv"))
                        for half in range(2):
                            for lc in range(2):
                                inst = t.matmul(
                                    pout[:, TS(half, 512)],
                                    thf[:, bb * LAT + lc * 128: bb * LAT + (lc + 1) * 128],
                                    msb[:, lc * D + half * 512: lc * D + (half + 1) * 512],
                                    start=(lc == 0), stop=(lc == 1))
                        tf_ev[bb] = p.ev(inst, "a1om", 1)

                    @blk.vector
                    def _(v, i=i, bb=bb):
                        p.wt(v, p.now("a1om"), xl2_evs[bb], xs_ev[bb])
                        e = p.ev(v.tensor_tensor(out=xmb[:, TS(bb, D)], in0=pout[:, :],
                                                 in1=x1b[:, TS(bb, D)], op=ALU.add),
                                 "a1ov", 1)
                        x1_ev[bb] = e
                    po_ev = p.now("a1ov")

                    @blk.scalar
                    def _(a, ti=ti, bb=bb):
                        p.wt(a, p.now("a1ov"))
                        xs_ev[bb] = p.ev(a.dma_start(out=xm_t[ti], in_=xmb[:, TS(bb, D)]),
                                         f"a1os{bb}", 16)

        stkM.close()
        with nc.Block("a1F") as blk:
            p.fence(blk)


def _phase_builders():
    return {
        "cast0": ph_cast0,
        "ln0": lambda p: ph_ln(p, "xin", "xn0", pooled=False, lo_name="xnlo0"),
        "ln1": lambda p: ph_ln(p, "x1", "xn1", pooled=True),
        "attn0": ph_attn0,
        "ffn0": ph_ffn0,
        "attn1": ph_attn1,
        "ffn1": ph_ffn1,
    }


def build_launch(phases, before, needed_later):
    nc = bass.Bass("TRN2", target_bir_lowering=False)
    p = P(nc, phases, before, needed_later)
    builders = _phase_builders()
    for ph in phases:
        builders[ph](p)
    return nc, p


LAUNCHES = [["cast0"], ["ln0"], ["attn0"], ["ffn0"], ["ln1"], ["attn1"], ["ffn1"]]


def kernel(**inputs):
    x = np.asarray(inputs["L0_attn_x"])
    per_core = {"xin": [np.ascontiguousarray(x[NSEQ * c:NSEQ * (c + 1)].reshape(TC, D))
                        for c in range(NCORES)]}
    produced = set()
    out = None
    for li, phases in enumerate(LAUNCHES):
        later_reads = set()
        for phs in LAUNCHES[li + 1:]:
            for ph in phs:
                later_reads |= set(PHASE_IO[ph][0])
        writes, reads = set(), set()
        for ph in phases:
            reads |= set(PHASE_IO[ph][0])
            writes |= set(PHASE_IO[ph][1])
        needed_later = set(writes)
        nc, p = build_launch(phases, produced, needed_later)
        in_maps = []
        for c in range(NCORES):
            m = {}
            for name in reads:
                if name == "xin" or name in SCRATCH:
                    if name in per_core:
                        m[name] = per_core[name][c]
                else:
                    m[name] = np.ascontiguousarray(np.asarray(inputs[name]))
            in_maps.append(m)
        res = run_bass_kernel_spmd(nc, in_maps, core_ids=list(range(NCORES)))
        for name in writes:
            if name == "out":
                out = [r["out"] for r in res.results]
            elif name in needed_later:
                per_core[name] = [r[name] for r in res.results]
        produced |= writes
    full = np.concatenate([o.reshape(NSEQ, SL, D) for o in out], axis=0)
    return np.ascontiguousarray(full.astype(np.float32))


# version 04
# speedup vs baseline: 1.0022x; 1.0022x over previous
import numpy as np
import ml_dtypes
from contextlib import ExitStack
import concourse.bass as bass
import concourse.mybir as mybir
from concourse.bass_utils import run_bass_kernel_spmd
from concourse.bass_core_rust import IndirectOffsetOnAxis

F32 = mybir.dt.float32
BF16 = mybir.dt.bfloat16
U32 = mybir.dt.uint32
AF = mybir.ActivationFunctionType
ALU = mybir.AluOpType
AX = mybir.AxisListType
TS = bass.ts

B, SL, D = 16, 4096, 1024
NCORES = 8
NSEQ = B // NCORES
HEADS, DH, HID = 8, 64, 512
NE, FFD = 8, 4096
LAT = 256
EPS = 1e-5
SCALE = DH ** -0.5
L1SCALE = 1.0 / float(np.sqrt(D))
BLK = 512


def _derive():
    global TC, NT, TPS, NBLK, NBLKC, SCRATCH, INPUT_SHAPES, OUT_SHAPE
    TC = NSEQ * SL
    NT = TC // 128
    TPS = SL // 128
    NBLK = SL // BLK
    NBLKC = TC // BLK
    SCRATCH = {
        "xn0":   ([TC, D], BF16),
        "xnlo0": ([TC, D], BF16),
        "wrlob": ([128, D], BF16),
        "xmid0": ([TC, D], F32),
        "x1":    ([TC, D], F32),
        "xn1":   ([TC, D], BF16),
        "xmid1": ([TC, D], F32),
        "q0":    ([TC, HID], BF16),
        "v0":    ([TC, HID], BF16),
        "k0":    ([NSEQ, HID, SL], F32),
        "ek0":   ([NSEQ, HID, SL], BF16),
        "w1b":   ([NE * FFD, D], BF16),
        "w2b":   ([NE * D, FFD], BF16),
        "wthb":  ([LAT, D], BF16),
        "wphb":  ([LAT, D], BF16),
        "wgb":   ([D, D], BF16),
        "wrb":   ([128, D], BF16),
        "pool1": ([NSEQ, D], F32),
        "zsc":   ([128, 4], F32),
        "ctxs":  ([2, 64, 4, DH], BF16),
        "th1":   ([TC, LAT], BF16),
        "ph1":   ([TC, LAT], BF16),
        "g1":    ([TC, D], BF16),
    }
    INPUT_SHAPES = {
        "xin": ([TC, D], F32),
        "L0_attn_Wq": ([D, HID], F32), "L0_attn_Wk": ([D, HID], F32),
        "L0_attn_Wv": ([D, HID], F32), "L0_attn_Wo": ([HID, D], F32),
        "L0_attn_bo": ([D], F32),
        "L0_ffn_Wr": ([NE, D], F32), "L0_ffn_br": ([NE], F32),
        "L0_ffn_W1": ([NE, FFD, D], F32), "L0_ffn_b1": ([NE, FFD], F32),
        "L0_ffn_W2": ([NE, D, FFD], F32), "L0_ffn_b2": ([NE, D], F32),
        "L1_attn_W_theta": ([LAT, D], F32), "L1_attn_W_phi": ([LAT, D], F32),
        "L1_attn_W_g": ([D, D], F32),
        "L1_ffn_Wr": ([D, NE], F32),
        "L1_ffn_W1": ([NE, D, FFD], F32), "L1_ffn_b1": ([NE, FFD], F32),
        "L1_ffn_W2": ([NE, FFD, D], F32), "L1_ffn_b2": ([NE, D], F32),
    }
    OUT_SHAPE = ([TC, D], F32)


def set_dims(nseq=2, sl=4096):
    global NSEQ, SL
    NSEQ, SL = nseq, sl
    _derive()


_derive()

PHASE_IO = {
    "cast0": (["L0_ffn_W1", "L0_ffn_W2", "L1_attn_W_theta", "L1_attn_W_phi",
               "L1_attn_W_g", "L0_ffn_Wr"],
              ["w1b", "w2b", "wthb", "wphb", "wgb", "wrb", "wrlob"]),
    "ln0":   (["xin"], ["xn0", "xnlo0"]),
    "attn0": (["xn0", "xin", "L0_attn_Wq", "L0_attn_Wk", "L0_attn_Wv",
               "L0_attn_Wo", "L0_attn_bo"],
              ["xmid0", "q0", "v0", "k0", "ek0"]),
    "ffn0":  (["xn0", "xnlo0", "xmid0", "w1b", "w2b", "wrb", "wrlob",
               "L0_ffn_br", "L0_ffn_b1", "L0_ffn_b2"], ["x1"]),
    "ln1":   (["x1"], ["xn1", "pool1"]),
    "attn1": (["xn1", "x1", "wthb", "wphb", "wgb"],
              ["th1", "ph1", "g1", "xmid1"]),
    "ffn1":  (["xn1", "xmid1", "pool1", "L1_ffn_Wr", "L1_ffn_W1", "L1_ffn_b1",
               "L1_ffn_W2", "L1_ffn_b2"], ["out"]),
}


class P:

    def __init__(self, nc, launch_phases, before, needed_later):
        self.nc = nc
        self.cnt = {}
        self._sems = {}
        self._slots = {}
        self._es = ExitStack()
        self.stack = None
        self.dram_cache = {}
        self.produced_here = set()
        for ph in launch_phases:
            self.produced_here |= set(PHASE_IO[ph][1])
        self.before = before
        self.needed_later = needed_later
        self.dma_ch = set()

    def dram(self, name):
        if name in self.dram_cache:
            return self.dram_cache[name]
        if name == "out":
            shape, dt = OUT_SHAPE
            kind = "ExternalOutput"
        elif name in INPUT_SHAPES:
            shape, dt = INPUT_SHAPES[name]
            kind = "ExternalInput"
        else:
            shape, dt = SCRATCH[name]
            if name in self.produced_here:
                kind = "ExternalOutput" if name in self.needed_later else "Internal"
            elif name in self.before:
                kind = "ExternalInput"
            else:
                kind = "Internal"
        ap = self.nc.dram_tensor(name, shape, dt, kind=kind).ap()
        self.dram_cache[name] = ap
        return ap

    SEMPOOL = 90

    def _slot(self, name):
        if name not in self._slots:
            self._slots[name] = len(self._slots) % self.SEMPOOL
        return self._slots[name]

    def sem(self, slot):
        if slot not in self._sems:
            self._sems[slot] = self._es.enter_context(self.nc.semaphore(f"s{slot}"))
            self.cnt[slot] = 0
        return self._sems[slot]

    def ev(self, inst, ch, n=1):
        slot = self._slot(ch)
        inst.then_inc(self.sem(slot), n)
        self.cnt[slot] += n
        if n >= 16:
            self.dma_ch.add(ch)
        return (slot, self.cnt[slot])

    def wt(self, eng, *evts):
        for e in evts:
            if e is None:
                continue
            if isinstance(e, list):
                self.wt(eng, *e)
                continue
            slot, v = e
            if v > 0:
                eng.wait_ge(self.sem(slot), v)

    def now(self, ch):
        slot = self._slot(ch)
        return (slot, self.cnt.get(slot, 0))

    def fence(self, blk):
        chans = sorted(self.dma_ch)

        @blk.sync
        def _(s):
            for ch in chans:
                self.wt(s, self.now(ch))

    def sb(self, name, shape, dtype):
        return self.stack.enter_context(self.nc.sbuf_tensor(name, shape, dtype))

    def ps(self, name, shape, dtype=F32):
        return self.stack.enter_context(self.nc.psum_tensor(name, shape, dtype))


def ph_cast0(p: P):
    nc = p.nc
    jobs = [("L0_ffn_W1", "w1b", NE * FFD, D), ("L0_ffn_W2", "w2b", NE * D, FFD),
            ("L1_attn_W_theta", "wthb", LAT, D), ("L1_attn_W_phi", "wphb", LAT, D),
            ("L1_attn_W_g", "wgb", D, D)]
    with ExitStack() as stk:
        p.stack = stk
        buf = p.sb("c0buf", [128, 2 * FFD], BF16)
        wr = p.sb("c0wr", [128, D], BF16)
        wrf = p.sb("c0wrf", [128, D], F32)
        wrh32 = p.sb("c0wrh", [128, D], F32)
        wrlo_sb = p.sb("c0wrl", [128, D], BF16)
        with nc.Block("cast0") as blk:
            seq = []
            for src, dst, rows, w in jobs:
                sap = p.dram(src)
                sflat = sap.rearrange(" ".join(f"a{i}" for i in range(sap.ndim)) +
                                      " -> (" + " ".join(f"a{i}" for i in range(sap.ndim))
                                      + ")").rearrange("(n p w) -> n p w", p=128, w=w)
                dflat = p.dram(dst).rearrange("a b -> (a b)").rearrange(
                    "(n p w) -> n p w", p=128, w=w)
                for i in range(sflat.shape[0]):
                    seq.append((sflat[i], dflat[i], w))
            st_ev = [None, None]
            ld_ev = [None, None]
            for i, (s_ap, d_ap, w) in enumerate(seq):
                b = i % 2

                @blk.gpsimd
                def _(g, b=b, s_ap=s_ap, w=w):
                    p.wt(g, st_ev[b])
                    ld_ev[b] = p.ev(g.dma_start(out=buf[:, TS(b, FFD)][:, 0:w],
                                                in_=s_ap), f"c0ld{b}", 16)

                @blk.scalar
                def _(s, b=b, d_ap=d_ap, w=w):
                    p.wt(s, ld_ev[b])
                    st_ev[b] = p.ev(s.dma_start(out=d_ap, in_=buf[:, TS(b, FFD)][:, 0:w]),
                                    f"c0st{b}", 16)

            @blk.gpsimd
            def _(g):
                mev = p.ev(g.memset(wrf[:, :], 0.0), "c0v", 1)
                p.wt(g, mev)
                ld_ev[0] = p.ev(g.dma_start(out=wrf[0:8, :], in_=p.dram("L0_ffn_Wr")),
                                "c0ldw", 16)

            @blk.scalar
            def _(s):
                p.wt(s, ld_ev[0])
                e1 = p.ev(s.activation(out=wr[:, :], in_=wrf[:, :], func=AF.Copy),
                          "c0we", 1)
                p.wt(s, e1)
                p.ev(s.activation(out=wrh32[:, :], in_=wr[:, :], func=AF.Copy),
                     "c0we", 1)

            @blk.vector
            def _(v):
                p.wt(v, p.now("c0we"))
                p.ev(v.tensor_tensor(out=wrlo_sb[:, :], in0=wrf[:, :], in1=wrh32[:, :],
                                     op=ALU.subtract), "c0wv", 1)

            @blk.scalar
            def _(s):
                p.wt(s, p.now("c0wv"))
                p.ev(s.dma_start(out=p.dram("wrb"), in_=wr[:, :]), "c0stw", 16)
                p.ev(s.dma_start(out=p.dram("wrlob"), in_=wrlo_sb[:, :]), "c0stw", 16)
            p.fence(blk)


def ph_ln(p: P, x_name, xn_name, pooled=False, lo_name=None):
    nc = p.nc
    x_t = p.dram(x_name).rearrange("(n p) d -> n p d", p=128)
    xn_t = p.dram(xn_name).rearrange("(n p) d -> n p d", p=128)
    if lo_name is not None:
        xnlo_t = p.dram(lo_name).rearrange("(n p) d -> n p d", p=128)
    pre = "L" + xn_name
    LD, LV, LA, ST, PEc = pre + "ld", pre + "v", pre + "a", pre + "st", pre + "pe"
    LAc, LV2, ST2 = pre + "ac", pre + "v2", pre + "s2"
    with ExitStack() as stk:
        p.stack = stk
        xb = p.sb(pre + "x", [128, 2 * D], F32)
        ob = p.sb(pre + "o", [128, 2 * D], BF16)
        sq = p.sb(pre + "sq", [128, D], F32)
        st_ = p.sb(pre + "sv", [128, 8], F32)
        cst = p.sb(pre + "cst", [128, 2], F32)
        if lo_name is not None:
            ob32 = p.sb(pre + "o32", [128, 2 * D], F32)
            obf = p.sb(pre + "obf", [128, 2 * D], F32)
            lob = p.sb(pre + "lob", [128, 2 * D], BF16)
        if pooled:
            onec = p.sb(pre + "one", [128, 1], BF16)
            poolp = [p.ps(pre + f"pp{h}", [1, 512]) for h in range(2 * NSEQ)]
            poolr = p.sb(pre + "pr", [1, NSEQ * D], F32)
        cons_ev = [None, None]
        stor_ev = [None, None]
        pe_ev = [None, None]
        ld_evs = [None, None]
        ocp_ev = [None, None]
        losub_ev = [None, None]
        lost_ev = [None, None]
        with nc.Block(pre) as blk:
            @blk.gpsimd
            def _(g):
                p.ev(g.memset(cst[:, 0:1], float(EPS)), LA, 1)
                p.ev(g.memset(cst[:, 1:2], 1.0 / D), LA, 1)
                if pooled:
                    p.ev(g.memset(onec[:, :], 1.0), LV, 1)
            for i in range(NT):
                b = i % 2
                sl = slice(b * D, (b + 1) * D)

                @blk.sync
                def _(s, i=i, b=b, sl=sl):
                    p.wt(s, cons_ev[b], pe_ev[b])
                    ld_evs[b] = p.ev(s.dma_start(out=xb[:, sl], in_=x_t[i]), LD + str(b), 16)

                @blk.vector
                def _(v, i=i, b=b, sl=sl):
                    p.wt(v, ld_evs[b])
                    e1 = p.ev(v.reduce_sum(st_[:, b:b + 1], xb[:, sl], axis=AX.X), LV, 1)
                    p.wt(v, e1)
                    p.ev(v.tensor_scalar(out=st_[:, 2 + b:3 + b], in0=st_[:, b:b + 1],
                                         scalar1=-1.0 / D, scalar2=None, op0=ALU.mult),
                         LV, 1)

                @blk.scalar
                def _(a, i=i, b=b, sl=sl):
                    p.wt(a, p.now(LV), p.now(LA))
                    e1 = p.ev(a.activation(out=sq[:, :], in_=xb[:, sl], func=AF.Square,
                                           bias=st_[:, 2 + b:3 + b], accum_out=st_[:, 4:5]),
                              LA, 1)
                    p.wt(a, e1)
                    p.ev(a.activation(out=st_[:, 5:6], in_=st_[:, 4:5], func=AF.Sqrt,
                                      scale=cst[:, 1:2], bias=cst[:, 0:1]), LA, 1)

                if lo_name is None:
                    @blk.vector
                    def _(v, i=i, b=b, sl=sl):
                        p.wt(v, p.now(LA), p.now(LV))
                        e1 = p.ev(v.reciprocal(st_[:, 6:7], st_[:, 5:6]), LV, 1)
                        p.wt(v, e1, stor_ev[b])
                        cons_ev[b] = p.ev(v.tensor_scalar(out=ob[:, sl], in0=xb[:, sl],
                                                          scalar1=st_[:, 2 + b:3 + b],
                                                          scalar2=st_[:, 6:7],
                                                          op0=ALU.add, op1=ALU.mult), LV, 1)

                    @blk.scalar
                    def _(s, i=i, b=b, sl=sl):
                        p.wt(s, p.now(LV))
                        stor_ev[b] = p.ev(s.dma_start(out=xn_t[i], in_=ob[:, sl]),
                                          ST + str(b), 16)
                else:
                    @blk.vector
                    def _(v, i=i, b=b, sl=sl):
                        p.wt(v, p.now(LA), p.now(LV))
                        e1 = p.ev(v.reciprocal(st_[:, 6:7], st_[:, 5:6]), LV, 1)
                        p.wt(v, e1, ocp_ev[b], losub_ev[b])
                        cons_ev[b] = p.ev(v.tensor_scalar(out=ob32[:, sl], in0=xb[:, sl],
                                                          scalar1=st_[:, 2 + b:3 + b],
                                                          scalar2=st_[:, 6:7],
                                                          op0=ALU.add, op1=ALU.mult), LV, 1)

                    @blk.scalar
                    def _(s, i=i, b=b, sl=sl):
                        p.wt(s, cons_ev[b], stor_ev[b])
                        e1 = p.ev(s.activation(out=ob[:, sl], in_=ob32[:, sl],
                                               func=AF.Copy), LAc, 1)
                        p.wt(s, e1)
                        ocp_ev[b] = p.ev(s.activation(out=obf[:, sl], in_=ob[:, sl],
                                                      func=AF.Copy), LAc, 1)
                        stor_ev[b] = p.ev(s.dma_start(out=xn_t[i], in_=ob[:, sl]),
                                          ST + str(b), 16)

                    @blk.vector
                    def _(v, i=i, b=b, sl=sl):
                        p.wt(v, ocp_ev[b], lost_ev[b])
                        losub_ev[b] = p.ev(v.tensor_tensor(out=lob[:, sl],
                                                           in0=ob32[:, sl],
                                                           in1=obf[:, sl],
                                                           op=ALU.subtract), LV2, 1)

                    @blk.scalar
                    def _(s, i=i, b=b, sl=sl):
                        p.wt(s, losub_ev[b])
                        lost_ev[b] = p.ev(s.dma_start(out=xnlo_t[i], in_=lob[:, sl]),
                                          ST2 + str(b), 16)

                if pooled:
                    @blk.tensor
                    def _(t, i=i, b=b, sl=sl):
                        p.wt(t, p.now(LV))
                        sidx = i // TPS
                        first = (i % TPS) == 0
                        last = (i % TPS) == TPS - 1
                        t.matmul(poolp[2 * sidx][:, :], onec[:, :],
                                 ob[:, sl][:, 0:512], start=first, stop=last)
                        i2 = t.matmul(poolp[2 * sidx + 1][:, :], onec[:, :],
                                      ob[:, sl][:, 512:1024], start=first, stop=last)
                        pe_ev[b] = p.ev(i2, PEc, 1)

            if pooled:
                @blk.vector
                def _(v):
                    p.wt(v, p.now(PEc), p.now(LV))
                    for s in range(NSEQ):
                        for h in range(2):
                            p.ev(v.tensor_scalar(
                                out=poolr[:, s * D + h * 512: s * D + (h + 1) * 512],
                                in0=poolp[2 * s + h][:, :], scalar1=1.0 / SL,
                                scalar2=None, op0=ALU.mult), LV, 1)

                @blk.scalar
                def _(s):
                    p.wt(s, p.now(LV))
                    p.ev(s.dma_start(out=p.dram("pool1").rearrange("s d -> (s d)")[None, :],
                                     in_=poolr[:, :]), ST, 16)
            p.fence(blk)


def ph_attn0(p: P):
    nc = p.nc
    xn = p.dram("xn0")
    xin_t = p.dram("xin").rearrange("(n p) d -> n p d", p=128)
    xmid_t = p.dram("xmid0").rearrange("(n p) d -> n p d", p=128)
    q0 = p.dram("q0")
    q0_t = q0.rearrange("(n p) d -> n p d", p=128)
    v0_t = p.dram("v0").rearrange("(n p) d -> n p d", p=128)
    k0 = p.dram("k0")
    ek0 = p.dram("ek0")
    with ExitStack() as stk:
        p.stack = stk
        wq = p.sb("a0wq", [128, 8 * HID], BF16)
        wk = p.sb("a0wk", [128, 8 * HID], BF16)
        wv = p.sb("a0wv", [128, 8 * HID], BF16)
        wo = p.sb("a0wo", [128, 4 * D], BF16)
        bo = p.sb("a0bo", [1, D], BF16)
        ones1 = p.sb("a0one", [1, 128], BF16)
        xnf = p.sb("a0xnf", [128, 2 * 8 * BLK], BF16)
        kev = p.sb("a0kev", [128, 4 * BLK], F32)
        qf = p.sb("a0qf", [128, HID], F32)
        qe = p.sb("a0qe", [128, HID], F32)
        qs = p.sb("a0qs", [128, 2 * HID], BF16)
        vb = p.sb("a0vb", [128, 2 * HID], BF16)
        mq = p.sb("a0mq", [128, 3 * HEADS], F32)
        kst = p.sb("a0kst", [128, 32], F32)
        tm1 = p.sb("a0tm1", [128, 4], F32)
        ekb = p.sb("a0ekb", [128, 2 * BLK], BF16)
        zp = p.sb("a0zp", [128, 2], F32)
        ektm = p.sb("a0ektm", [128, 2 * HID], BF16)
        vtm = p.sb("a0vtm", [128, 2 * HID], BF16)
        ctxn = p.sb("a0ctxn", [64, HID], BF16)
        ctx2 = p.sb("a0ctx2", [128, 256], BF16)
        zlo = p.sb("a0zlo", [64, 8], F32)
        qfm = p.sb("a0qfm", [128, 2 * HID], BF16)
        af = p.sb("a0af", [128, HID], BF16)
        xmb = p.sb("a0xmb", [128, 2 * D], F32)
        xr = p.sb("a0xr", [128, 2 * D], F32)
        stkA = ExitStack()
        pk = [stkA.enter_context(nc.psum_tensor(f"a0pk{j}", [128, BLK], F32))
              for j in range(4)]
        pq = stkA.enter_context(nc.psum_tensor("a0pq", [128, BLK], F32))
        pv = stkA.enter_context(nc.psum_tensor("a0pv", [128, BLK], F32))

        with nc.Block("a0A") as blk:
            @blk.gpsimd
            def _(g):
                for name, dst, wdt, nch in [("L0_attn_Wq", wq, HID, 8),
                                            ("L0_attn_Wk", wk, HID, 8),
                                            ("L0_attn_Wv", wv, HID, 8)]:
                    ap = p.dram(name).rearrange("(c p) h -> c p h", p=128)
                    for c in range(nch):
                        p.ev(g.dma_start(out=dst[:, TS(c, wdt)], in_=ap[c]), "a0wl", 16)
                apo = p.dram("L0_attn_Wo").rearrange("(c p) d -> c p d", p=128)
                for c in range(4):
                    p.ev(g.dma_start(out=wo[:, TS(c, D)], in_=apo[c]), "a0wl", 16)
                p.ev(g.dma_start(out=bo[:, :], in_=p.dram("L0_attn_bo")[None, :]),
                     "a0wl", 16)
                p.ev(g.memset(ones1[:, :], 1.0), "a0ini", 1)
                for sq in range(NSEQ):
                    p.ev(g.memset(kst[:, 16 * sq:16 * sq + 4], -30000.0), "a0ini", 1)
                    p.ev(g.memset(kst[:, 16 * sq + 8:16 * sq + 12], 0.0), "a0ini", 1)
            wl_ev = [p.now("a0wl"), p.now("a0ini")]

            xnf_ev = [None, None]
            xl_ev = [None, None]
            kev_ev = [[], [], [], []]
            kps_ev = [None] * 4
            qs_ev = [None, None]
            vb_ev = [None, None]
            pq_ev = None
            pv_ev = None
            for s in range(NSEQ):
                for j in range(NBLK):
                    bb = (s * NBLK + j) % 2
                    t0 = s * SL + j * BLK

                    @blk.sync
                    def _(sy, bb=bb, t0=t0):
                        p.wt(sy, xnf_ev[bb])
                        for c in range(8):
                            ev = p.ev(sy.dma_start(
                                out=xnf[:, bb * 8 * BLK + c * BLK: bb * 8 * BLK + (c + 1) * BLK],
                                in_=xn[t0:t0 + BLK, TS(c, 128)], transpose=True),
                                f"a0xl{bb}", 16)
                        xl_ev[bb] = ev

                    @blk.tensor
                    def _(t, bb=bb):
                        p.wt(t, wl_ev, xl_ev[bb], *[e for e in kps_ev if e])
                        for kc in range(4):
                            for c in range(8):
                                inst = t.matmul(
                                    pk[kc][:, :],
                                    wk[:, c * HID + kc * 128: c * HID + (kc + 1) * 128],
                                    xnf[:, bb * 8 * BLK + c * BLK: bb * 8 * BLK + (c + 1) * BLK],
                                    start=(c == 0), stop=(c == 7))
                        p.ev(inst, "a0kmm", 1)

                    @blk.scalar
                    def _(a, s=s, j=j):
                        p.wt(a, p.now("a0kmm"))
                        for kc in range(4):
                            p.wt(a, kev_ev[kc])
                            e = p.ev(a.activation(out=kev[:, TS(kc, BLK)], in_=pk[kc][:, :],
                                                  func=AF.Copy), "a0ke", 1)
                            kps_ev[kc] = e
                            p.wt(a, e)
                            sev = p.ev(a.dma_start(
                                out=k0[s, kc * 128:(kc + 1) * 128, j * BLK:(j + 1) * BLK],
                                in_=kev[:, TS(kc, BLK)]), f"a0ks{kc}", 16)
                            kev_ev[kc] = [sev]

                    @blk.vector
                    def _(v, s=s):
                        p.wt(v, p.now("a0ke"), p.now("a0kv"))
                        for kc in range(4):
                            e = p.ev(v.reduce_max(tm1[:, kc:kc + 1], kev[:, TS(kc, BLK)],
                                                  axis=AX.X), "a0kv", 1)
                            kev_ev[kc].append(e)
                        p.wt(v, p.now("a0kv"))
                        p.ev(v.tensor_tensor(out=kst[:, 16 * s:16 * s + 4],
                                             in0=kst[:, 16 * s:16 * s + 4],
                                             in1=tm1[:, 0:4], op=ALU.max), "a0kv", 1)

                    for tt in range(4):
                        ti = (t0 + tt * 128) // 128
                        qb = (s * NBLK * 4 + j * 4 + tt) % 2

                        @blk.tensor
                        def _(t, bb=bb, tt=tt):
                            p.wt(t, pq_ev)
                            for c in range(8):
                                inst = t.matmul(
                                    pq[:, :],
                                    xnf[:, bb * 8 * BLK + c * BLK + tt * 128:
                                        bb * 8 * BLK + c * BLK + (tt + 1) * 128],
                                    wq[:, TS(c, HID)], start=(c == 0), stop=(c == 7))
                            p.ev(inst, "a0qmm", 1)

                        @blk.scalar
                        def _(a):
                            p.wt(a, p.now("a0qmm"), p.now("a0qv"))
                            p.ev(a.activation(out=qf[:, :], in_=pq[:, :][:, 0:HID],
                                              func=AF.Copy), "a0qe", 1)
                        pq_ev = p.now("a0qe")

                        @blk.vector
                        def _(v):
                            p.wt(v, p.now("a0qe"))
                            q3 = qf[:, :].rearrange("p (h d) -> p h d", d=DH)
                            e1 = p.ev(v.reduce_max(mq[:, 0:HEADS], q3, axis=AX.X), "a0qv", 1)
                            p.wt(v, e1)
                            mqb = mq[:, 0:HEADS].unsqueeze(2).broadcast_to([128, HEADS, DH])
                            p.ev(v.tensor_tensor(
                                out=qe[:, :].rearrange("p (h d) -> p h d", d=DH),
                                in0=q3, in1=mqb, op=ALU.subtract), "a0qv", 1)

                        @blk.scalar
                        def _(a):
                            p.wt(a, p.now("a0qv"))
                            p.ev(a.activation(out=qe[:, :], in_=qe[:, :], func=AF.Exp),
                                 "a0qe", 1)

                        @blk.vector
                        def _(v, qb=qb):
                            p.wt(v, p.now("a0qe"))
                            q3 = qe[:, :].rearrange("p (h d) -> p h d", d=DH)
                            e1 = p.ev(v.reduce_sum(mq[:, HEADS:2 * HEADS], q3, axis=AX.X),
                                      "a0qv", 1)
                            p.wt(v, e1)
                            e2 = p.ev(v.reciprocal(mq[:, 2 * HEADS:3 * HEADS],
                                                   mq[:, HEADS:2 * HEADS]), "a0qv", 1)
                            p.wt(v, e2, qs_ev[qb])
                            rqb = mq[:, 2 * HEADS:3 * HEADS].unsqueeze(2).broadcast_to(
                                [128, HEADS, DH])
                            p.ev(v.tensor_tensor(
                                out=qs[:, TS(qb, HID)].rearrange("p (h d) -> p h d", d=DH),
                                in0=q3, in1=rqb, op=ALU.mult), "a0qv", 1)

                        @blk.scalar
                        def _(a, ti=ti, qb=qb):
                            p.wt(a, p.now("a0qv"))
                            qs_ev[qb] = p.ev(a.dma_start(out=q0_t[ti],
                                                         in_=qs[:, TS(qb, HID)]),
                                             f"a0qss{qb}", 16)

                        @blk.tensor
                        def _(t, bb=bb, tt=tt):
                            p.wt(t, pv_ev)
                            for c in range(8):
                                inst = t.matmul(
                                    pv[:, :],
                                    xnf[:, bb * 8 * BLK + c * BLK + tt * 128:
                                        bb * 8 * BLK + c * BLK + (tt + 1) * 128],
                                    wv[:, TS(c, HID)], start=(c == 0), stop=(c == 7))
                            ev = p.ev(inst, "a0vmm", 1)
                            if tt == 3:
                                xnf_ev[bb] = ev

                        @blk.scalar
                        def _(a, ti=ti, qb=qb):
                            p.wt(a, p.now("a0vmm"), vb_ev[qb])
                            e = p.ev(a.activation(out=vb[:, TS(qb, HID)],
                                                  in_=pv[:, :][:, 0:HID], func=AF.Copy),
                                     "a0ve", 1)
                            p.wt(a, e)
                            vb_ev[qb] = p.ev(a.dma_start(out=v0_t[ti],
                                                         in_=vb[:, TS(qb, HID)]),
                                             f"a0vss{qb}", 16)
                        pv_ev = p.now("a0ve")

        stkA.close()
        stkB = ExitStack()
        pctx = stkB.enter_context(nc.psum_tensor("a0pctx", [64, HID], F32))
        patt = stkB.enter_context(nc.psum_tensor("a0patt", [128, BLK], F32))
        py = stkB.enter_context(nc.psum_tensor("a0py", [128, D], F32))

        ctx_ev = [None]
        for s in range(NSEQ):
            with nc.Block(f"a0B{s}") as blk:
                @blk.vector
                def _(v, s=s):
                    p.ev(v.tensor_scalar(out=kst[:, 16 * s + 4:16 * s + 8],
                                         in0=kst[:, 16 * s:16 * s + 4], scalar1=-1.0,
                                         scalar2=None, op0=ALU.mult), "b0v", 1)

                kl_ev = [None, None]
                exp_ev = [None, None]
                eks_ev = [p.now("b0es0"), p.now("b0es1")]
                zp_ev = [None, None]
                for kc in range(4):
                    for j in range(NBLK):
                        bb = (kc * NBLK + j) % 2

                        @blk.sync
                        def _(sy, s=s, kc=kc, j=j, bb=bb):
                            if s == 0 and kc == 0 and j == 0:
                                for kk in range(4):
                                    p.wt(sy, p.now(f"a0ks{kk}"))
                                p.wt(sy, p.now("a0vss0"), p.now("a0vss1"))
                            p.wt(sy, exp_ev[bb])
                            kl_ev[bb] = p.ev(sy.dma_start(
                                out=kev[:, TS(bb, BLK)],
                                in_=k0[s, kc * 128:(kc + 1) * 128, j * BLK:(j + 1) * BLK]),
                                f"b0kl{bb}", 16)

                        @blk.scalar
                        def _(a, s=s, kc=kc, j=j, bb=bb):
                            p.wt(a, kl_ev[bb], zp_ev[bb], eks_ev[bb], p.now("b0v"))
                            e = p.ev(a.activation(out=ekb[:, TS(bb, BLK)],
                                                  in_=kev[:, TS(bb, BLK)], func=AF.Exp,
                                                  bias=kst[:, 16 * s + 4 + kc:
                                                           16 * s + 5 + kc],
                                                  accum_out=zp[:, bb:bb + 1]), "b0e", 1)
                            exp_ev[bb] = e
                            p.wt(a, e)
                            eks_ev[bb] = p.ev(a.dma_start(
                                out=ek0[s, kc * 128:(kc + 1) * 128, j * BLK:(j + 1) * BLK],
                                in_=ekb[:, TS(bb, BLK)]), f"b0es{bb}", 16)

                        @blk.vector
                        def _(v, s=s, kc=kc, bb=bb):
                            p.wt(v, p.now("b0e"), p.now("b0v"))
                            zp_ev[bb] = p.ev(v.tensor_tensor(
                                out=kst[:, 16 * s + 8 + kc:16 * s + 9 + kc],
                                in0=kst[:, 16 * s + 8 + kc:16 * s + 9 + kc],
                                in1=zp[:, bb:bb + 1], op=ALU.add), "b0v", 1)

                @blk.vector
                def _(v, s=s):
                    p.wt(v, p.now("b0v"))
                    p.ev(v.reciprocal(kst[:, 16 * s + 12:16 * s + 16],
                                      kst[:, 16 * s + 8:16 * s + 12]), "b0v", 1)

                @blk.scalar
                def _(a, s=s):
                    p.wt(a, p.now("b0v"))
                    if s == 1:
                        p.wt(a, p.now("b0zl0"), p.now("b0cv"))
                    e = p.ev(a.dma_start(out=p.dram("zsc"),
                                         in_=kst[:, 16 * s + 12:16 * s + 16]),
                             f"b0zs{s}", 16)
                    p.wt(a, e)
                    p.ev(a.dma_start(
                        out=zlo[:, :].rearrange("d (hp kc) -> d hp kc", hp=2),
                        in_=p.dram("zsc").rearrange("(hp d) kc -> d hp kc", hp=2)),
                        f"b0zl{s}", 16)

                et_ev = [p.now("b0mm"), p.now("b0mm")]
                vt_ev = [p.now("b0mm"), p.now("b0mm")]
                el_evs = [None, None]
                vl_evs = [None, None]
                for i in range(TPS):
                    bb = i % 2

                    @blk.sync
                    def _(sy, s=s, i=i, bb=bb):
                        if i == 0:
                            p.wt(sy, p.now("b0es0"), p.now("b0es1"))
                        p.wt(sy, et_ev[bb])
                        el_evs[bb] = p.ev(sy.dma_start(
                            out=ektm[:, TS(bb, HID)],
                            in_=ek0[s, 0:HID, i * 128:(i + 1) * 128],
                            transpose=True), f"b0el{bb}", 16)

                    @blk.scalar
                    def _(a, s=s, i=i, bb=bb):
                        p.wt(a, vt_ev[bb])
                        vl_evs[bb] = p.ev(a.dma_start(out=vtm[:, TS(bb, HID)],
                                                      in_=v0_t[(s * SL) // 128 + i]),
                                          f"b0vl{bb}", 16)

                    @blk.tensor
                    def _(t, s=s, i=i, bb=bb):
                        p.wt(t, el_evs[bb], vl_evs[bb])
                        if i == 0:
                            p.wt(t, ctx_ev[0])
                        for h in range(HEADS):
                            inst = t.matmul(
                                pctx[:, TS(h, DH)],
                                ektm[:, bb * HID + h * DH: bb * HID + (h + 1) * DH],
                                vtm[:, bb * HID + h * DH: bb * HID + (h + 1) * DH],
                                start=(i == 0 and h == 0),
                                stop=(i == TPS - 1 and h == HEADS - 1))
                        e = p.ev(inst, "b0mm", 1)
                        et_ev[bb] = e
                        vt_ev[bb] = e

                @blk.vector
                def _(v, s=s):
                    p.wt(v, p.now("b0mm"), p.now("b0v"), p.now(f"b0zl{s}"))
                    for h in range(HEADS):
                        zc = zlo[:, (h % 2) * 4 + h // 2: (h % 2) * 4 + h // 2 + 1]
                        e = p.ev(v.tensor_scalar(out=ctxn[:, TS(h, DH)],
                                                 in0=pctx[:, TS(h, DH)],
                                                 scalar1=zc,
                                                 scalar2=float(SCALE), op0=ALU.mult,
                                                 op1=ALU.mult), "b0cv", 1)
                    ctx_ev[0] = e

                @blk.scalar
                def _(a, s=s):
                    p.wt(a, p.now("b0cv"))
                    if s == 1:
                        p.wt(a, p.now("c0am"))
                    e = p.ev(a.dma_start(
                        out=p.dram("ctxs").rearrange("hp d hc e -> d hc hp e"),
                        in_=ctxn[:, :].rearrange("p (hc hp e) -> p hc hp e",
                                                 hp=2, e=DH)),
                        f"b0cs{s}", 16)
                    p.wt(a, e)
                    p.ev(a.dma_start(
                        out=ctx2[:, :].rearrange("p (hc e) -> p hc e", e=DH),
                        in_=p.dram("ctxs").rearrange("hp d hc e -> (hp d) hc e")),
                        f"b0c2{s}", 16)

            with nc.Block(f"a0C{s}") as blk:
                qf_ev = [p.now("c0am"), p.now("c0am")]
                xr_ev = [p.now("c0v"), p.now("c0v")]
                af_hold = [p.now("c0ym")]
                xm_ev = [p.now("c0xs0"), p.now("c0xs1")]
                ql_evs = [None, None]
                xl_evs = [None, None]
                patt_ev = p.now("c0ae")
                py_ev = p.now("c0v")
                for i in range(TPS):
                    bb = i % 2
                    ti = (s * SL) // 128 + i

                    @blk.sync
                    def _(sy, s=s, i=i, bb=bb, ti=ti):
                        if i == 0 and s == 0:
                            p.wt(sy, p.now("a0qss0"), p.now("a0qss1"))
                        p.wt(sy, qf_ev[bb])
                        for c in range(4):
                            e = p.ev(sy.dma_start(
                                out=qfm[:, bb * HID + c * 128: bb * HID + (c + 1) * 128],
                                in_=q0[ti * 128:(ti + 1) * 128, TS(c, 128)],
                                transpose=True), f"c0ql{bb}", 16)
                        ql_evs[bb] = e

                    @blk.scalar
                    def _(a, i=i, bb=bb, ti=ti):
                        p.wt(a, xr_ev[bb])
                        xl_evs[bb] = p.ev(a.dma_start(out=xr[:, TS(bb, D)], in_=xin_t[ti]),
                                          f"c0xl{bb}", 16)

                    @blk.tensor
                    def _(t, i=i, bb=bb, s=s):
                        p.wt(t, ql_evs[bb], patt_ev)
                        if i == 0:
                            p.wt(t, p.now(f"b0c2{s}"))
                        for c in range(4):
                            for hh in range(2):
                                pb = hh * 64
                                inst = t.matmul(
                                    patt[pb:pb + 64, TS(c, 128)],
                                    ctx2[pb:pb + 64, TS(c, DH)],
                                    qfm[:, bb * HID + c * 128:
                                        bb * HID + (c + 1) * 128][pb:pb + 64, :],
                                    start=(c == 0), stop=(c == 3),
                                    skip_group_check=True)
                        qf_ev[bb] = p.ev(inst, "c0am", 1)

                    @blk.scalar
                    def _(a, i=i, bb=bb):
                        p.wt(a, p.now("c0am"), af_hold[0])
                        p.ev(a.activation(out=af[:, :], in_=patt[:, :],
                                          func=AF.Copy), "c0ae", 1)
                    patt_ev = p.now("c0ae")

                    @blk.tensor
                    def _(t, i=i, bb=bb):
                        p.wt(t, p.now("c0ae"), py_ev)
                        for half in range(2):
                            for c in range(4):
                                t.matmul(py[:, TS(half, 512)],
                                         af[:, TS(c, 128)],
                                         wo[:, c * D + half * 512: c * D + (half + 1) * 512],
                                         start=(c == 0), stop=False)
                            inst = t.matmul(py[:, TS(half, 512)], ones1[:, :],
                                            bo[:, TS(half, 512)], start=False, stop=True)
                        af_hold[0] = p.ev(inst, "c0ym", 1)

                    @blk.vector
                    def _(v, i=i, bb=bb):
                        p.wt(v, p.now("c0ym"), xl_evs[bb], xm_ev[bb])
                        e = p.ev(v.tensor_tensor(out=xmb[:, TS(bb, D)], in0=py[:, :],
                                                 in1=xr[:, TS(bb, D)], op=ALU.add), "c0v", 1)
                        xr_ev[bb] = e
                    py_ev = p.now("c0v")

                    @blk.scalar
                    def _(a, i=i, bb=bb, ti=ti):
                        p.wt(a, p.now("c0v"))
                        xm_ev[bb] = p.ev(a.dma_start(out=xmid_t[ti], in_=xmb[:, TS(bb, D)]),
                                         f"c0xs{bb}", 16)

        stkB.close()
        with nc.Block("a0F") as blk:
            p.fence(blk)


def _ph_ffn(p: P, layer):
    nc = p.nc
    if layer == 0:
        xn, base, outs, pre = p.dram("xn0"), p.dram("xmid0"), p.dram("x1"), "f0"
    else:
        xn, base, outs, pre = p.dram("xn1"), p.dram("xmid1"), p.dram("out"), "f1"
    outs_t = outs.rearrange("(n p) d -> n p d", p=128)
    with ExitStack() as stk:
        p.stack = stk
        w1t = p.sb(pre + "w1", [128, 8 * FFD], BF16)
        w2t = p.sb(pre + "w2", [128, 8 * FFD], BF16)
        xnf = p.sb(pre + "xnf", [128, 8 * BLK], BF16)
        hb = p.sb(pre + "hb", [128, 32 * BLK], BF16)
        yst = p.sb(pre + "yst", [128, 4 * D], F32)
        ones1 = p.sb(pre + "o1", [1, 128], BF16)
        ones5 = p.sb(pre + "o5", [1, BLK], BF16)
        b1e = p.sb(pre + "b1", [1, FFD], BF16)
        b2e = p.sb(pre + "b2", [1, D], BF16)
        ph_ = [p.ps(pre + "ph0", [128, BLK]), p.ps(pre + "ph1", [128, BLK])]
        py_ = [p.ps(pre + "py0", [128, BLK]), p.ps(pre + "py1", [128, BLK])]

        with nc.Block(pre + "R") as blk:
            @blk.gpsimd
            def _(g):
                p.ev(g.memset(ones1[:, :], 1.0), pre + "g", 1)
                p.ev(g.memset(ones5[:, :], 1.0), pre + "g", 1)
                p.ev(g.dma_start(out=outs, in_=base), pre + "bc", 16)

            if layer == 0:
                rxn = p.sb(pre + "rxn", [128, 8 * 128], BF16)
                rxnlo = p.sb(pre + "rxl", [128, 8 * 128], BF16)
                wrt = p.sb(pre + "wrt", [128, 8 * 128], BF16)
                wrtlo = p.sb(pre + "wrtl", [128, 8 * 128], BF16)
                brr = p.sb(pre + "brr", [1, 8], BF16)
                brf = p.sb(pre + "brf", [1, 8], F32)
                brh = p.sb(pre + "brh", [1, 8], F32)
                brlo = p.sb(pre + "brlo", [1, 8], BF16)
                lgt = p.sb(pre + "lgt", [128, 8], F32)
                lmx = p.sb(pre + "lmx", [128, 1], F32)
                mcols = p.sb(pre + "mc", [128, NT * 8], F32)
                plg = p.ps(pre + "plg", [128, 512])
                wrbd = p.dram("wrb")
                wrlod = p.dram("wrlob")
                xnlo = p.dram("xnlo0")

                @blk.sync
                def _(sy):
                    for c in range(8):
                        p.ev(sy.dma_start(out=wrt[:, TS(c, 128)],
                                          in_=wrbd[:, TS(c, 128)], transpose=True),
                             pre + "wl", 16)
                        p.ev(sy.dma_start(out=wrtlo[:, TS(c, 128)],
                                          in_=wrlod[:, TS(c, 128)], transpose=True),
                             pre + "wl", 16)

                @blk.gpsimd
                def _(g):
                    p.ev(g.dma_start(out=brr[:, :], in_=p.dram("L0_ffn_br")[None, :]),
                         pre + "wlb", 16)
                    p.ev(g.dma_start(out=brf[:, :], in_=p.dram("L0_ffn_br")[None, :]),
                         pre + "wlb", 16)

                @blk.scalar
                def _(a):
                    p.wt(a, p.now(pre + "wlb"))
                    p.ev(a.activation(out=brh[:, :], in_=brr[:, :], func=AF.Copy),
                         pre + "ba", 1)

                @blk.vector
                def _(v):
                    p.wt(v, p.now(pre + "ba"))
                    p.ev(v.tensor_tensor(out=brlo[:, :], in0=brf[:, :], in1=brh[:, :],
                                         op=ALU.subtract), pre + "bv", 1)
                rt_wl = [p.now(pre + "wl"), p.now(pre + "wlb"), p.now(pre + "bv")]

                xe_ev = [None, None]
                rxl_evs = [None, None]
                lg_ev = None
                for ti in range(NT):
                    bb = ti % 2

                    @blk.sync
                    def _(sy, ti=ti):
                        p.wt(sy, xe_ev[0])
                        for c in range(8):
                            p.ev(sy.dma_start(
                                out=rxn[:, TS(c, 128)],
                                in_=xn[ti * 128:(ti + 1) * 128, TS(c, 128)],
                                transpose=True), pre + "rxl", 16)
                            e = p.ev(sy.dma_start(
                                out=rxnlo[:, TS(c, 128)],
                                in_=xnlo[ti * 128:(ti + 1) * 128, TS(c, 128)],
                                transpose=True), pre + "rxl", 16)
                        rxl_evs[0] = e

                    @blk.tensor
                    def _(t, ti=ti):
                        p.wt(t, rt_wl, rxl_evs[0], lg_ev)
                        for c in range(8):
                            t.matmul(plg[:, 0:8],
                                     rxn[:, TS(c, 128)],
                                     wrt[:, c * 128:c * 128 + 8],
                                     start=(c == 0), stop=False)
                        for c in range(8):
                            t.matmul(plg[:, 0:8],
                                     rxnlo[:, TS(c, 128)],
                                     wrt[:, c * 128:c * 128 + 8],
                                     start=False, stop=False)
                        for c in range(8):
                            t.matmul(plg[:, 0:8],
                                     rxn[:, TS(c, 128)],
                                     wrtlo[:, c * 128:c * 128 + 8],
                                     start=False, stop=False)
                        t.matmul(plg[:, 0:8], ones1[:, :], brr[:, :],
                                 start=False, stop=False)
                        inst = t.matmul(plg[:, 0:8], ones1[:, :], brlo[:, :],
                                        start=False, stop=True)
                        xe_ev[0] = p.ev(inst, pre + "rmm", 1)

                    @blk.scalar
                    def _(a, ti=ti):
                        p.wt(a, p.now(pre + "rmm"), p.now(pre + "rv"))
                        p.ev(a.activation(out=lgt[:, :], in_=plg[:, 0:8], func=AF.Copy),
                             pre + "re", 1)
                    lg_ev = p.now(pre + "re")

                    @blk.vector
                    def _(v, ti=ti):
                        p.wt(v, p.now(pre + "re"))
                        e = p.ev(v.reduce_max(lmx[:, :], lgt[:, :], axis=AX.X), pre + "rv", 1)
                        p.wt(v, e)
                        p.ev(v.tensor_scalar(out=mcols[:, ti * 8:(ti + 1) * 8],
                                             in0=lgt[:, :], scalar1=lmx[:, 0:1],
                                             scalar2=None, op0=ALU.is_ge), pre + "rv", 1)

            else:
                wr1 = p.sb(pre + "wr1", [128, 8 * 8], F32)
                pcol = p.sb(pre + "pc", [128, 8], F32)
                lgt = p.sb(pre + "lg", [1, 32], F32)
                onesf = p.sb(pre + "of", [1, 128], F32)
                wcols = p.sb(pre + "wc", [128, NSEQ * 8], F32)
                plg = p.ps(pre + "plg", [128, 512])
                pool1 = p.dram("pool1")

                @blk.gpsimd
                def _(g):
                    p.ev(g.memset(onesf[:, :], 1.0), pre + "g", 1)
                    ap = p.dram("L1_ffn_Wr").rearrange("(c p) e -> c p e", p=128)
                    for c in range(8):
                        p.ev(g.dma_start(out=wr1[:, TS(c, 8)], in_=ap[c]), pre + "wl", 16)

                for s in range(NSEQ):
                    @blk.gpsimd
                    def _(g, s=s):
                        p.wt(g, p.now(pre + "rmm"))
                        p.ev(g.dma_start(out=pcol[:, :],
                                         in_=pool1[s].rearrange("(c p) -> p c", p=128),
                                         allow_slow_non_contiguous=True), pre + "pl", 16)

                    @blk.tensor
                    def _(t, s=s):
                        p.wt(t, p.now(pre + "wl"), p.now(pre + "pl"), p.now(pre + "re"))
                        for c in range(8):
                            inst = t.matmul(plg[0:1, 0:8], pcol[:, c:c + 1],
                                            wr1[:, TS(c, 8)], start=(c == 0), stop=(c == 7))
                        p.ev(inst, pre + "rmm", 1)

                    @blk.scalar
                    def _(a, s=s):
                        p.wt(a, p.now(pre + "rmm"), p.now(pre + "rv"))
                        p.ev(a.activation(out=lgt[:, 0:8], in_=plg[0:1, 0:8], func=AF.Copy),
                             pre + "re", 1)

                    @blk.vector
                    def _(v, s=s):
                        p.wt(v, p.now(pre + "re"))
                        e = p.ev(v.reduce_max(lgt[:, 8:9], lgt[:, 0:8], axis=AX.X),
                                 pre + "rv", 1)
                        p.wt(v, e)
                        e = p.ev(v.tensor_scalar(out=lgt[:, 16:24], in0=lgt[:, 0:8],
                                                 scalar1=lgt[:, 8:9], scalar2=None,
                                                 op0=ALU.is_ge), pre + "rv", 1)
                        p.wt(v, e)
                        e = p.ev(v.scalar_tensor_tensor(out=lgt[:, 24:32], in0=lgt[:, 16:24],
                                                        scalar=-1e30, in1=lgt[:, 0:8],
                                                        op0=ALU.mult, op1=ALU.add),
                                 pre + "rv", 1)
                        p.wt(v, e)
                        e = p.ev(v.reduce_max(lgt[:, 9:10], lgt[:, 24:32], axis=AX.X),
                                 pre + "rv", 1)
                        p.wt(v, e)
                        e = p.ev(v.tensor_scalar(out=lgt[:, 24:32], in0=lgt[:, 24:32],
                                                 scalar1=lgt[:, 9:10], scalar2=None,
                                                 op0=ALU.is_ge), pre + "rv", 1)
                        p.wt(v, e)
                        e = p.ev(v.tensor_tensor(out=lgt[:, 10:11], in0=lgt[:, 9:10],
                                                 in1=lgt[:, 8:9], op=ALU.subtract),
                                 pre + "rv", 1)

                    @blk.scalar
                    def _(a, s=s):
                        p.wt(a, p.now(pre + "rv"))
                        p.ev(a.activation(out=lgt[:, 11:12], in_=lgt[:, 10:11], func=AF.Exp),
                             pre + "re", 1)

                    @blk.vector
                    def _(v, s=s):
                        p.wt(v, p.now(pre + "re"))
                        e = p.ev(v.tensor_scalar(out=lgt[:, 12:13], in0=lgt[:, 11:12],
                                                 scalar1=1.0, scalar2=None, op0=ALU.add),
                                 pre + "rv", 1)
                        p.wt(v, e)
                        e = p.ev(v.reciprocal(lgt[:, 13:14], lgt[:, 12:13]), pre + "rv", 1)
                        p.wt(v, e)
                        e = p.ev(v.tensor_tensor(out=lgt[:, 14:15], in0=lgt[:, 11:12],
                                                 in1=lgt[:, 13:14], op=ALU.mult), pre + "rv", 1)
                        p.wt(v, e)
                        e = p.ev(v.tensor_scalar(out=lgt[:, 16:24], in0=lgt[:, 16:24],
                                                 scalar1=lgt[:, 13:14], scalar2=None,
                                                 op0=ALU.mult), pre + "rv", 1)
                        p.wt(v, e)
                        e = p.ev(v.scalar_tensor_tensor(out=lgt[:, 24:32], in0=lgt[:, 24:32],
                                                        scalar=lgt[:, 14:15],
                                                        in1=lgt[:, 16:24],
                                                        op0=ALU.mult, op1=ALU.add),
                                 pre + "rv", 1)

                    @blk.tensor
                    def _(t, s=s):
                        p.wt(t, p.now(pre + "rv"), p.now(pre + "re"))
                        inst = t.matmul(plg[:, 8:16], onesf[:, :], lgt[:, 24:32],
                                        start=True, stop=True)
                        p.ev(inst, pre + "rmm", 1)

                    @blk.scalar
                    def _(a, s=s):
                        p.wt(a, p.now(pre + "rmm"))
                        p.ev(a.activation(out=wcols[:, TS(s, 8)], in_=plg[:, 8:16],
                                          func=AF.Copy), pre + "re", 1)

        with nc.Block(pre + "E") as blk:
            if layer == 0:
                w1bd = p.dram("w1b")
                w2bd = p.dram("w2b")
            acc_ev = {}
            xl_now = [None]
            hch_ev = [None, None]
            ych_ev = [None, None]
            hbl_ev = [None] * 32
            yst_ev = [None] * 8
            hmm_now = [None]
            ymm_now = [None]
            for e in range(NE):
                @blk.sync
                def _(sy, e=e):
                    if e >= 1:
                        p.wt(sy, p.now(pre + "hmm"))
                    if layer == 0:
                        for dc in range(8):
                            for fg in range(8):
                                p.ev(sy.dma_start(
                                    out=w1t[:, dc * FFD + fg * 512: dc * FFD + (fg + 1) * 512],
                                    in_=w1bd[e * FFD + fg * 512:e * FFD + (fg + 1) * 512,
                                             TS(dc, 128)], transpose=True), pre + "w1l", 16)

                @blk.sync
                def _(sy, e=e):
                    if e >= 1:
                        p.wt(sy, p.now(pre + "ymm"))
                    if layer == 0:
                        for fc in range(32):
                            for dh in range(2):
                                p.ev(sy.dma_start(
                                    out=w2t[:, fc * 1024 + dh * 512: fc * 1024 + (dh + 1) * 512],
                                    in_=w2bd[e * D + dh * 512:e * D + (dh + 1) * 512,
                                             TS(fc, 128)], transpose=True), pre + "w2l", 16)

                @blk.gpsimd
                def _(g, e=e):
                    if e >= 1:
                        p.wt(g, p.now(pre + "hmm"), p.now(pre + "ymm"))
                    if layer == 1:
                        ap = p.dram("L1_ffn_W1")[e].rearrange("(dc p) f -> dc p f", p=128)
                        for dc in range(8):
                            p.ev(g.dma_start(out=w1t[:, TS(dc, FFD)], in_=ap[dc]),
                                 pre + "w1l", 16)
                        ap2 = p.dram("L1_ffn_W2")[e].rearrange("(fc p) d -> fc p d", p=128)
                        for fc in range(32):
                            p.ev(g.dma_start(out=w2t[:, TS(fc, D)], in_=ap2[fc]),
                                 pre + "w2l", 16)
                    p.ev(g.dma_start(out=b1e[:, :],
                                     in_=(p.dram("L0_ffn_b1") if layer == 0 else
                                          p.dram("L1_ffn_b1"))[e][None, :]), pre + "bl", 16)
                    p.ev(g.dma_start(out=b2e[:, :],
                                     in_=(p.dram("L0_ffn_b2") if layer == 0 else
                                          p.dram("L1_ffn_b2"))[e][None, :]), pre + "bl", 16)
                wle = [p.now(pre + "w1l"), p.now(pre + "w2l"), p.now(pre + "bl"),
                       p.now(pre + "g")]

                for blkid in range(NBLKC):
                    s = blkid // NBLK
                    t0 = blkid * BLK

                    @blk.sync
                    def _(sy, e=e, blkid=blkid, t0=t0):
                        p.wt(sy, p.now(pre + "hmm"))
                        for c in range(8):
                            ev = p.ev(sy.dma_start(
                                out=xnf[:, c * BLK:(c + 1) * BLK],
                                in_=xn[t0:t0 + BLK, TS(c, 128)], transpose=True),
                                pre + "xl", 16)
                        xl_now[0] = ev

                    for fc in range(32):
                        @blk.tensor
                        def _(t, e=e, fc=fc):
                            if fc == 0:
                                p.wt(t, *wle)
                                p.wt(t, xl_now[0])
                            p.wt(t, hch_ev[fc % 2])
                            for dc in range(8):
                                t.matmul(ph_[fc % 2][:, :],
                                         w1t[:, dc * FFD + fc * 128: dc * FFD + (fc + 1) * 128],
                                         xnf[:, TS(dc, BLK)], start=(dc == 0), stop=False)
                            inst = t.matmul(ph_[fc % 2][:, :], b1e[:, TS(fc, 128)],
                                            ones5[:, :], start=False, stop=True)
                            hmm_now[0] = p.ev(inst, pre + "hmm", 1)

                        @blk.scalar
                        def _(a, fc=fc):
                            p.wt(a, hmm_now[0], hbl_ev[fc])
                            hch_ev[fc % 2] = p.ev(a.activation(
                                out=hb[:, TS(fc, BLK)], in_=ph_[fc % 2][:, :],
                                func=(AF.Relu if layer == 0 else AF.Gelu)), pre + "he", 1)

                    for tt in range(4):
                        for half in range(2):
                            pid = (2 * tt + half) % 2

                            @blk.tensor
                            def _(t, e=e, tt=tt, half=half, pid=pid):
                                if tt == 0 and half == 0:
                                    p.wt(t, p.now(pre + "he"))
                                p.wt(t, ych_ev[pid])
                                for fc in range(32):
                                    t.matmul(
                                        py_[pid][:, :],
                                        hb[:, fc * BLK + tt * 128: fc * BLK + (tt + 1) * 128],
                                        w2t[:, fc * 1024 + half * 512:
                                            fc * 1024 + (half + 1) * 512] if layer == 0 else
                                        w2t[:, fc * D + half * 512: fc * D + (half + 1) * 512],
                                        start=(fc == 0), stop=False)
                                inst = t.matmul(py_[pid][:, :], ones1[:, :],
                                                b2e[:, TS(half, 512)], start=False, stop=True)
                                ymm_now[0] = p.ev(inst, pre + "ymm", 1)
                                if tt == 3 and half == 1:
                                    for fc in range(32):
                                        hbl_ev[fc] = ymm_now[0]

                            @blk.vector
                            def _(v, e=e, blkid=blkid, s=s, tt=tt, half=half, pid=pid):
                                p.wt(v, ymm_now[0], yst_ev[2 * tt + half])
                                if layer == 0:
                                    sc = mcols[:, (blkid * 4 + tt) * 8 + e:
                                               (blkid * 4 + tt) * 8 + e + 1]
                                else:
                                    sc = wcols[:, s * 8 + e:s * 8 + e + 1]
                                ych_ev[pid] = p.ev(v.tensor_scalar(
                                    out=yst[:, tt * D + half * 512: tt * D + (half + 1) * 512],
                                    in0=py_[pid][:, :], scalar1=sc, scalar2=None,
                                    op0=ALU.mult), pre + "yv", 1)

                    yv_done = p.now(pre + "yv")

                    @blk.gpsimd
                    def _(g, e=e, blkid=blkid):
                        p.wt(g, yv_done)
                        p.wt(g, acc_ev.get(blkid))
                        if e == 0:
                            p.wt(g, p.now(pre + "bc"))
                        for tt in range(4):
                            ev = p.ev(g.dma_start(
                                out=outs_t[blkid * 4 + tt],
                                in_=yst[:, TS(tt, D)], accum_op=ALU.add),
                                pre + f"ac{blkid}", 16)
                        for q in range(8):
                            yst_ev[q] = ev
                        acc_ev[blkid] = ev

        with nc.Block(pre + "F") as blk:
            p.fence(blk)


def ph_ffn0(p: P):
    _ph_ffn(p, 0)


def ph_ffn1(p: P):
    _ph_ffn(p, 1)


def ph_attn1(p: P):
    nc = p.nc
    xn = p.dram("xn1")
    x1_t = p.dram("x1").rearrange("(n p) d -> n p d", p=128)
    xm_t = p.dram("xmid1").rearrange("(n p) d -> n p d", p=128)
    th1 = p.dram("th1")
    th1_t = th1.rearrange("(n p) d -> n p d", p=128)
    ph1_t = p.dram("ph1").rearrange("(n p) d -> n p d", p=128)
    g1_t = p.dram("g1").rearrange("(n p) d -> n p d", p=128)
    with ExitStack() as stk:
        p.stack = stk
        wth = p.sb("a1wth", [128, 8 * LAT], BF16)
        wph = p.sb("a1wph", [128, 8 * LAT], BF16)
        wg = p.sb("a1wg", [128, 8 * D], BF16)
        xnf = p.sb("a1xnf", [128, 2 * 8 * 128], BF16)
        thb = p.sb("a1thb", [128, 2 * LAT], BF16)
        phb = p.sb("a1phb", [128, 2 * LAT], BF16)
        gb = p.sb("a1gb", [128, 2 * D], BF16)
        phl = p.sb("a1phl", [128, 2 * LAT], BF16)
        gl = p.sb("a1gl", [128, 2 * D], BF16)
        msb = p.sb("a1msb", [128, 2 * D], BF16)
        thf = p.sb("a1thf", [128, 2 * LAT], BF16)
        x1b = p.sb("a1x1", [128, 2 * D], F32)
        xmb = p.sb("a1xm", [128, 2 * D], F32)
        stkP = ExitStack()
        pth = stkP.enter_context(nc.psum_tensor("a1pth", [128, 512], F32))
        pph = stkP.enter_context(nc.psum_tensor("a1pph", [128, 512], F32))
        pg = stkP.enter_context(nc.psum_tensor("a1pg", [128, D], F32))

        with nc.Block("a1P") as blk:
            @blk.sync
            def _(sy):
                wthb, wphb, wgb = p.dram("wthb"), p.dram("wphb"), p.dram("wgb")
                for c in range(8):
                    p.ev(sy.dma_start(out=wth[:, TS(c, LAT)],
                                      in_=wthb[0:LAT, TS(c, 128)], transpose=True),
                         "a1wl", 16)
                    p.ev(sy.dma_start(out=wph[:, TS(c, LAT)],
                                      in_=wphb[0:LAT, TS(c, 128)], transpose=True),
                         "a1wl", 16)
                    for hf in range(2):
                        p.ev(sy.dma_start(out=wg[:, c * D + hf * 512: c * D + (hf + 1) * 512],
                                          in_=wgb[hf * 512:(hf + 1) * 512, TS(c, 128)],
                                          transpose=True), "a1wl", 16)
            wl_ev = p.now("a1wl")
            xnf_ev = [None, None]
            xl_evs = [None, None]
            st_ev = [None, None]
            pev = [None]
            for ti in range(NT):
                bb = ti % 2

                @blk.sync
                def _(sy, ti=ti, bb=bb):
                    p.wt(sy, xnf_ev[bb])
                    for c in range(8):
                        ev = p.ev(sy.dma_start(
                            out=xnf[:, bb * 8 * 128 + c * 128: bb * 8 * 128 + (c + 1) * 128],
                            in_=xn[ti * 128:(ti + 1) * 128, TS(c, 128)], transpose=True),
                            f"a1xl{bb}", 16)
                    xl_evs[bb] = ev

                @blk.tensor
                def _(t, ti=ti, bb=bb):
                    p.wt(t, wl_ev, xl_evs[bb], pev[0])
                    for c in range(8):
                        t.matmul(pth[:, 0:LAT],
                                 xnf[:, bb * 8 * 128 + c * 128: bb * 8 * 128 + (c + 1) * 128],
                                 wth[:, TS(c, LAT)], start=(c == 0), stop=(c == 7))
                    for c in range(8):
                        t.matmul(pph[:, 0:LAT],
                                 xnf[:, bb * 8 * 128 + c * 128: bb * 8 * 128 + (c + 1) * 128],
                                 wph[:, TS(c, LAT)], start=(c == 0), stop=(c == 7))
                    for hf in range(2):
                        for c in range(8):
                            inst = t.matmul(
                                pg[:, TS(hf, 512)],
                                xnf[:, bb * 8 * 128 + c * 128: bb * 8 * 128 + (c + 1) * 128],
                                wg[:, c * D + hf * 512: c * D + (hf + 1) * 512],
                                start=(c == 0), stop=(c == 7))
                    xnf_ev[bb] = p.ev(inst, "a1mm", 1)

                @blk.scalar
                def _(a, ti=ti, bb=bb):
                    p.wt(a, p.now("a1mm"), st_ev[bb])
                    p.ev(a.activation(out=thb[:, TS(bb, LAT)], in_=pth[:, 0:LAT],
                                      func=AF.Copy), "a1e", 1)
                    p.ev(a.activation(out=phb[:, TS(bb, LAT)], in_=pph[:, 0:LAT],
                                      func=AF.Copy), "a1e", 1)
                    pev[0] = p.ev(a.activation(out=gb[:, TS(bb, D)], in_=pg[:, :],
                                               func=AF.Copy), "a1e", 1)
                    p.wt(a, pev[0])
                    p.ev(a.dma_start(out=th1_t[ti], in_=thb[:, TS(bb, LAT)]),
                         f"a1st{bb}", 16)
                    p.ev(a.dma_start(out=ph1_t[ti], in_=phb[:, TS(bb, LAT)]),
                         f"a1st{bb}", 16)
                    st_ev[bb] = p.ev(a.dma_start(out=g1_t[ti], in_=gb[:, TS(bb, D)]),
                                     f"a1st{bb}", 16)

        stkP.close()
        stkM = ExitStack()
        pM = [stkM.enter_context(nc.psum_tensor(f"a1pM{j}", [128, 512], F32))
              for j in range(4)]
        pout = stkM.enter_context(nc.psum_tensor("a1po", [128, D], F32))

        mev_ev = [None]
        for s in range(NSEQ):
            with nc.Block(f"a1M{s}") as blk:
                pl_ev = [p.now("a1Mm"), p.now("a1Mm")]
                ml_evs = [None, None]
                for i in range(TPS):
                    bb = i % 2
                    ti = s * TPS + i

                    @blk.sync
                    def _(sy, ti=ti, bb=bb, i=i):
                        if i == 0:
                            p.wt(sy, p.now("a1st0"), p.now("a1st1"))
                        p.wt(sy, pl_ev[bb])
                        p.ev(sy.dma_start(out=phl[:, TS(bb, LAT)], in_=ph1_t[ti]),
                             f"a1pl{bb}", 16)
                        ml_evs[bb] = p.ev(sy.dma_start(out=gl[:, TS(bb, D)], in_=g1_t[ti]),
                                          f"a1pl{bb}", 16)

                    @blk.tensor
                    def _(t, i=i, bb=bb, s=s):
                        p.wt(t, ml_evs[bb])
                        if i == 0 and s == 1:
                            p.wt(t, mev_ev[0])
                        for lc in range(2):
                            for oh in range(2):
                                inst = t.matmul(
                                    pM[2 * lc + oh][:, :],
                                    phl[:, bb * LAT + lc * 128: bb * LAT + (lc + 1) * 128],
                                    gl[:, bb * D + oh * 512: bb * D + (oh + 1) * 512],
                                    start=(i == 0), stop=(i == TPS - 1))
                        pl_ev[bb] = p.ev(inst, "a1Mm", 1)

                @blk.vector
                def _(v, s=s):
                    p.wt(v, p.now("a1Mm"))
                    if s == 1:
                        p.wt(v, p.now("a1om"))
                    for lc in range(2):
                        for oh in range(2):
                            e = p.ev(v.tensor_scalar(
                                out=msb[:, lc * D + oh * 512: lc * D + (oh + 1) * 512],
                                in0=pM[2 * lc + oh][:, :], scalar1=float(L1SCALE),
                                scalar2=None, op0=ALU.mult), "a1Mv", 1)
                    mev_ev[0] = e

            with nc.Block(f"a1O{s}") as blk:
                tf_ev = [p.now("a1om"), p.now("a1om")]
                x1_ev = [p.now("a1ov"), p.now("a1ov")]
                xs_ev = [p.now("a1os0"), p.now("a1os1")]
                tl_evs = [None, None]
                xl2_evs = [None, None]
                po_ev = p.now("a1ov")
                for i in range(TPS):
                    bb = i % 2
                    ti = s * TPS + i

                    @blk.sync
                    def _(sy, ti=ti, bb=bb, i=i):
                        if i == 0:
                            p.wt(sy, p.now("a1st0"), p.now("a1st1"))
                        p.wt(sy, tf_ev[bb])
                        for lc in range(2):
                            ev = p.ev(sy.dma_start(
                                out=thf[:, bb * LAT + lc * 128: bb * LAT + (lc + 1) * 128],
                                in_=th1[ti * 128:(ti + 1) * 128, TS(lc, 128)],
                                transpose=True), f"a1tl{bb}", 16)
                        tl_evs[bb] = ev

                    @blk.scalar
                    def _(a, ti=ti, bb=bb):
                        p.wt(a, x1_ev[bb])
                        xl2_evs[bb] = p.ev(a.dma_start(out=x1b[:, TS(bb, D)], in_=x1_t[ti]),
                                           f"a1xr{bb}", 16)

                    @blk.tensor
                    def _(t, i=i, bb=bb, s=s):
                        p.wt(t, tl_evs[bb], po_ev)
                        if i == 0:
                            p.wt(t, p.now("a1Mv"))
                        for half in range(2):
                            for lc in range(2):
                                inst = t.matmul(
                                    pout[:, TS(half, 512)],
                                    thf[:, bb * LAT + lc * 128: bb * LAT + (lc + 1) * 128],
                                    msb[:, lc * D + half * 512: lc * D + (half + 1) * 512],
                                    start=(lc == 0), stop=(lc == 1))
                        tf_ev[bb] = p.ev(inst, "a1om", 1)

                    @blk.vector
                    def _(v, i=i, bb=bb):
                        p.wt(v, p.now("a1om"), xl2_evs[bb], xs_ev[bb])
                        e = p.ev(v.tensor_tensor(out=xmb[:, TS(bb, D)], in0=pout[:, :],
                                                 in1=x1b[:, TS(bb, D)], op=ALU.add),
                                 "a1ov", 1)
                        x1_ev[bb] = e
                    po_ev = p.now("a1ov")

                    @blk.scalar
                    def _(a, ti=ti, bb=bb):
                        p.wt(a, p.now("a1ov"))
                        xs_ev[bb] = p.ev(a.dma_start(out=xm_t[ti], in_=xmb[:, TS(bb, D)]),
                                         f"a1os{bb}", 16)

        stkM.close()
        with nc.Block("a1F") as blk:
            p.fence(blk)


def _phase_builders():
    return {
        "cast0": ph_cast0,
        "ln0": lambda p: ph_ln(p, "xin", "xn0", pooled=False, lo_name="xnlo0"),
        "ln1": lambda p: ph_ln(p, "x1", "xn1", pooled=True),
        "attn0": ph_attn0,
        "ffn0": ph_ffn0,
        "attn1": ph_attn1,
        "ffn1": ph_ffn1,
    }


def build_launch(phases, before, needed_later):
    nc = bass.Bass("TRN2", target_bir_lowering=False)
    p = P(nc, phases, before, needed_later)
    builders = _phase_builders()
    for ph in phases:
        builders[ph](p)
    return nc, p


LAUNCHES = [["cast0", "ln0", "attn0", "ffn0", "ln1", "attn1", "ffn1"]]


def kernel(**inputs):
    x = np.asarray(inputs["L0_attn_x"])
    per_core = {"xin": [np.ascontiguousarray(x[NSEQ * c:NSEQ * (c + 1)].reshape(TC, D))
                        for c in range(NCORES)]}
    produced = set()
    out = None
    for li, phases in enumerate(LAUNCHES):
        later_reads = set()
        for phs in LAUNCHES[li + 1:]:
            for ph in phs:
                later_reads |= set(PHASE_IO[ph][0])
        writes, reads = set(), set()
        for ph in phases:
            reads |= set(PHASE_IO[ph][0])
            writes |= set(PHASE_IO[ph][1])
        needed_later = (writes & later_reads) | ({"out"} & writes)
        nc, p = build_launch(phases, produced, needed_later)
        in_maps = []
        for c in range(NCORES):
            m = {}
            for name in reads:
                if name == "xin" or name in SCRATCH:
                    if name in per_core:
                        m[name] = per_core[name][c]
                else:
                    m[name] = np.ascontiguousarray(np.asarray(inputs[name]))
            in_maps.append(m)
        res = run_bass_kernel_spmd(nc, in_maps, core_ids=list(range(NCORES)))
        for name in writes:
            if name == "out":
                out = [r["out"] for r in res.results]
            elif name in needed_later:
                per_core[name] = [r[name] for r in res.results]
        produced |= writes
    full = np.concatenate([o.reshape(NSEQ, SL, D) for o in out], axis=0)
    return np.ascontiguousarray(full.astype(np.float32))


# version 05
# speedup vs baseline: 1.3246x; 1.3218x over previous
import numpy as np
import ml_dtypes
from contextlib import ExitStack
import concourse.bass as bass
import concourse.mybir as mybir
from concourse.bass_utils import run_bass_kernel_spmd
from concourse.bass_core_rust import IndirectOffsetOnAxis

F32 = mybir.dt.float32
BF16 = mybir.dt.bfloat16
U32 = mybir.dt.uint32
AF = mybir.ActivationFunctionType
ALU = mybir.AluOpType
AX = mybir.AxisListType
TS = bass.ts

B, SL, D = 16, 4096, 1024
NCORES = 8
NSEQ = B // NCORES
HEADS, DH, HID = 8, 64, 512
NE, FFD = 8, 4096
LAT = 256
EPS = 1e-5
SCALE = DH ** -0.5
L1SCALE = 1.0 / float(np.sqrt(D))
BLK = 512


def _derive():
    global TC, NT, TPS, NBLK, NBLKC, SCRATCH, INPUT_SHAPES, OUT_SHAPE
    TC = NSEQ * SL
    NT = TC // 128
    TPS = SL // 128
    NBLK = SL // BLK
    NBLKC = TC // BLK
    SCRATCH = {
        "xn0":   ([TC, D], BF16),
        "xnlo0": ([TC, D], BF16),
        "wrlob": ([128, D], BF16),
        "xmid0": ([TC, D], F32),
        "x1":    ([TC, D], F32),
        "xn1":   ([TC, D], BF16),
        "xmid1": ([TC, D], F32),
        "q0":    ([TC, HID], BF16),
        "v0":    ([TC, HID], BF16),
        "k0":    ([NSEQ, HID, SL], F32),
        "ek0":   ([NSEQ, HID, SL], BF16),
        "w1b":   ([NE * FFD, D], BF16),
        "w2b":   ([NE * D, FFD], BF16),
        "wthb":  ([LAT, D], BF16),
        "wphb":  ([LAT, D], BF16),
        "wgb":   ([D, D], BF16),
        "wrb":   ([128, D], BF16),
        "pool1": ([NSEQ, D], F32),
        "zsc":   ([128, 4], F32),
        "ctxs":  ([2, 64, 4, DH], BF16),
        "th1":   ([TC, LAT], BF16),
        "ph1":   ([TC, LAT], BF16),
        "g1":    ([TC, D], BF16),
        "mrows": ([NSEQ * 2, NE], F32),
        "b2w":   ([NSEQ * 2, D], BF16),
    }
    INPUT_SHAPES = {
        "xin": ([TC, D], F32),
        "L0_attn_Wq": ([D, HID], F32), "L0_attn_Wk": ([D, HID], F32),
        "L0_attn_Wv": ([D, HID], F32), "L0_attn_Wo": ([HID, D], F32),
        "L0_attn_bo": ([D], F32),
        "L0_ffn_Wr": ([NE, D], F32), "L0_ffn_br": ([NE], F32),
        "L0_ffn_W1": ([NE, FFD, D], F32), "L0_ffn_b1": ([NE, FFD], F32),
        "L0_ffn_W2": ([NE, D, FFD], F32), "L0_ffn_b2": ([NE, D], F32),
        "L1_attn_W_theta": ([LAT, D], F32), "L1_attn_W_phi": ([LAT, D], F32),
        "L1_attn_W_g": ([D, D], F32),
        "L1_ffn_Wr": ([D, NE], F32),
        "L1_ffn_W1": ([NE, D, FFD], F32), "L1_ffn_b1": ([NE, FFD], F32),
        "L1_ffn_W2": ([NE, FFD, D], F32), "L1_ffn_b2": ([NE, D], F32),
    }
    OUT_SHAPE = ([TC, D], F32)


def set_dims(nseq=2, sl=4096):
    global NSEQ, SL
    NSEQ, SL = nseq, sl
    _derive()


_derive()

PHASE_IO = {
    "cast0": (["L0_ffn_W1", "L0_ffn_W2", "L1_attn_W_theta", "L1_attn_W_phi",
               "L1_attn_W_g", "L0_ffn_Wr"],
              ["w1b", "w2b", "wthb", "wphb", "wgb", "wrb", "wrlob"]),
    "ln0":   (["xin"], ["xn0", "xnlo0"]),
    "attn0": (["xn0", "xin", "L0_attn_Wq", "L0_attn_Wk", "L0_attn_Wv",
               "L0_attn_Wo", "L0_attn_bo"],
              ["xmid0", "q0", "v0", "k0", "ek0"]),
    "ffn0":  (["xn0", "xnlo0", "xmid0", "w1b", "w2b", "wrb", "wrlob",
               "L0_ffn_br", "L0_ffn_b1", "L0_ffn_b2"], ["x1"]),
    "ln1":   (["x1"], ["xn1", "pool1"]),
    "attn1": (["xn1", "x1", "wthb", "wphb", "wgb"],
              ["th1", "ph1", "g1", "xmid1"]),
    "ffn1":  (["xn1", "xmid1", "pool1", "L1_ffn_Wr", "L1_ffn_W1", "L1_ffn_b1",
               "L1_ffn_W2", "L1_ffn_b2"], ["out"]),
}


class P:

    def __init__(self, nc, launch_phases, before, needed_later):
        self.nc = nc
        self.cnt = {}
        self._sems = {}
        self._slots = {}
        self._es = ExitStack()
        self.stack = None
        self.dram_cache = {}
        self.produced_here = set()
        for ph in launch_phases:
            self.produced_here |= set(PHASE_IO[ph][1])
        self.before = before
        self.needed_later = needed_later
        self.dma_ch = set()

    def dram(self, name):
        if name in self.dram_cache:
            return self.dram_cache[name]
        if name == "out":
            shape, dt = OUT_SHAPE
            kind = "ExternalOutput"
        elif name in INPUT_SHAPES:
            shape, dt = INPUT_SHAPES[name]
            kind = "ExternalInput"
        else:
            shape, dt = SCRATCH[name]
            if name in self.produced_here:
                kind = "ExternalOutput" if name in self.needed_later else "Internal"
            elif name in self.before:
                kind = "ExternalInput"
            else:
                kind = "Internal"
        ap = self.nc.dram_tensor(name, shape, dt, kind=kind).ap()
        self.dram_cache[name] = ap
        return ap

    SEMPOOL = 90

    def _slot(self, name):
        if name not in self._slots:
            self._slots[name] = len(self._slots) % self.SEMPOOL
        return self._slots[name]

    def sem(self, slot):
        if slot not in self._sems:
            self._sems[slot] = self._es.enter_context(self.nc.semaphore(f"s{slot}"))
            self.cnt[slot] = 0
        return self._sems[slot]

    def ev(self, inst, ch, n=1):
        slot = self._slot(ch)
        inst.then_inc(self.sem(slot), n)
        self.cnt[slot] += n
        if n >= 16:
            self.dma_ch.add(ch)
        return (slot, self.cnt[slot])

    def wt(self, eng, *evts):
        for e in evts:
            if e is None:
                continue
            if isinstance(e, list):
                self.wt(eng, *e)
                continue
            slot, v = e
            if v > 0:
                eng.wait_ge(self.sem(slot), v)

    def now(self, ch):
        slot = self._slot(ch)
        return (slot, self.cnt.get(slot, 0))

    def fence(self, blk):
        chans = sorted(self.dma_ch)

        @blk.sync
        def _(s):
            for ch in chans:
                self.wt(s, self.now(ch))

    def sb(self, name, shape, dtype):
        return self.stack.enter_context(self.nc.sbuf_tensor(name, shape, dtype))

    def ps(self, name, shape, dtype=F32):
        return self.stack.enter_context(self.nc.psum_tensor(name, shape, dtype))


def ph_cast0(p: P):
    nc = p.nc
    jobs = [("L0_ffn_W1", "w1b", NE * FFD, D), ("L0_ffn_W2", "w2b", NE * D, FFD),
            ("L1_attn_W_theta", "wthb", LAT, D), ("L1_attn_W_phi", "wphb", LAT, D),
            ("L1_attn_W_g", "wgb", D, D)]
    with ExitStack() as stk:
        p.stack = stk
        buf = p.sb("c0buf", [128, 2 * FFD], BF16)
        wr = p.sb("c0wr", [128, D], BF16)
        wrf = p.sb("c0wrf", [128, D], F32)
        wrh32 = p.sb("c0wrh", [128, D], F32)
        wrlo_sb = p.sb("c0wrl", [128, D], BF16)
        with nc.Block("cast0") as blk:
            seq = []
            for src, dst, rows, w in jobs:
                sap = p.dram(src)
                sflat = sap.rearrange(" ".join(f"a{i}" for i in range(sap.ndim)) +
                                      " -> (" + " ".join(f"a{i}" for i in range(sap.ndim))
                                      + ")").rearrange("(n p w) -> n p w", p=128, w=w)
                dflat = p.dram(dst).rearrange("a b -> (a b)").rearrange(
                    "(n p w) -> n p w", p=128, w=w)
                for i in range(sflat.shape[0]):
                    seq.append((sflat[i], dflat[i], w))
            st_ev = [None, None]
            ld_ev = [None, None]
            for i, (s_ap, d_ap, w) in enumerate(seq):
                b = i % 2

                @blk.gpsimd
                def _(g, b=b, s_ap=s_ap, w=w):
                    p.wt(g, st_ev[b])
                    ld_ev[b] = p.ev(g.dma_start(out=buf[:, TS(b, FFD)][:, 0:w],
                                                in_=s_ap), f"c0ld{b}", 16)

                @blk.scalar
                def _(s, b=b, d_ap=d_ap, w=w):
                    p.wt(s, ld_ev[b])
                    st_ev[b] = p.ev(s.dma_start(out=d_ap, in_=buf[:, TS(b, FFD)][:, 0:w]),
                                    f"c0st{b}", 16)

            @blk.gpsimd
            def _(g):
                mev = p.ev(g.memset(wrf[:, :], 0.0), "c0v", 1)
                p.wt(g, mev)
                ld_ev[0] = p.ev(g.dma_start(out=wrf[0:8, :], in_=p.dram("L0_ffn_Wr")),
                                "c0ldw", 16)

            @blk.scalar
            def _(s):
                p.wt(s, ld_ev[0])
                e1 = p.ev(s.activation(out=wr[:, :], in_=wrf[:, :], func=AF.Copy),
                          "c0we", 1)
                p.wt(s, e1)
                p.ev(s.activation(out=wrh32[:, :], in_=wr[:, :], func=AF.Copy),
                     "c0we", 1)

            @blk.vector
            def _(v):
                p.wt(v, p.now("c0we"))
                p.ev(v.tensor_tensor(out=wrlo_sb[:, :], in0=wrf[:, :], in1=wrh32[:, :],
                                     op=ALU.subtract), "c0wv", 1)

            @blk.scalar
            def _(s):
                p.wt(s, p.now("c0wv"))
                p.ev(s.dma_start(out=p.dram("wrb"), in_=wr[:, :]), "c0stw", 16)
                p.ev(s.dma_start(out=p.dram("wrlob"), in_=wrlo_sb[:, :]), "c0stw", 16)
            p.fence(blk)


def ph_ln(p: P, x_name, xn_name, pooled=False, lo_name=None):
    nc = p.nc
    x_t = p.dram(x_name).rearrange("(n p) d -> n p d", p=128)
    xn_t = p.dram(xn_name).rearrange("(n p) d -> n p d", p=128)
    if lo_name is not None:
        xnlo_t = p.dram(lo_name).rearrange("(n p) d -> n p d", p=128)
    pre = "L" + xn_name
    LD, LV, LA, ST, PEc = pre + "ld", pre + "v", pre + "a", pre + "st", pre + "pe"
    LAc, LV2, ST2 = pre + "ac", pre + "v2", pre + "s2"
    with ExitStack() as stk:
        p.stack = stk
        xb = p.sb(pre + "x", [128, 2 * D], F32)
        ob = p.sb(pre + "o", [128, 2 * D], BF16)
        sq = p.sb(pre + "sq", [128, D], F32)
        st_ = p.sb(pre + "sv", [128, 8], F32)
        cst = p.sb(pre + "cst", [128, 2], F32)
        if lo_name is not None:
            ob32 = p.sb(pre + "o32", [128, 2 * D], F32)
            obf = p.sb(pre + "obf", [128, 2 * D], F32)
            lob = p.sb(pre + "lob", [128, 2 * D], BF16)
        if pooled:
            onec = p.sb(pre + "one", [128, 1], BF16)
            poolp = [p.ps(pre + f"pp{h}", [1, 512]) for h in range(2 * NSEQ)]
            poolr = p.sb(pre + "pr", [1, NSEQ * D], F32)
        cons_ev = [None, None]
        stor_ev = [None, None]
        pe_ev = [None, None]
        ld_evs = [None, None]
        ocp_ev = [None, None]
        losub_ev = [None, None]
        lost_ev = [None, None]
        with nc.Block(pre) as blk:
            @blk.gpsimd
            def _(g):
                p.ev(g.memset(cst[:, 0:1], float(EPS)), LA, 1)
                p.ev(g.memset(cst[:, 1:2], 1.0 / D), LA, 1)
                if pooled:
                    p.ev(g.memset(onec[:, :], 1.0), LV, 1)
            for i in range(NT):
                b = i % 2
                sl = slice(b * D, (b + 1) * D)

                @blk.sync
                def _(s, i=i, b=b, sl=sl):
                    p.wt(s, cons_ev[b], pe_ev[b])
                    ld_evs[b] = p.ev(s.dma_start(out=xb[:, sl], in_=x_t[i]), LD + str(b), 16)

                @blk.vector
                def _(v, i=i, b=b, sl=sl):
                    p.wt(v, ld_evs[b])
                    e1 = p.ev(v.reduce_sum(st_[:, b:b + 1], xb[:, sl], axis=AX.X), LV, 1)
                    p.wt(v, e1)
                    p.ev(v.tensor_scalar(out=st_[:, 2 + b:3 + b], in0=st_[:, b:b + 1],
                                         scalar1=-1.0 / D, scalar2=None, op0=ALU.mult),
                         LV, 1)

                @blk.scalar
                def _(a, i=i, b=b, sl=sl):
                    p.wt(a, p.now(LV), p.now(LA))
                    e1 = p.ev(a.activation(out=sq[:, :], in_=xb[:, sl], func=AF.Square,
                                           bias=st_[:, 2 + b:3 + b], accum_out=st_[:, 4:5]),
                              LA, 1)
                    p.wt(a, e1)
                    p.ev(a.activation(out=st_[:, 5:6], in_=st_[:, 4:5], func=AF.Sqrt,
                                      scale=cst[:, 1:2], bias=cst[:, 0:1]), LA, 1)

                if lo_name is None:
                    @blk.vector
                    def _(v, i=i, b=b, sl=sl):
                        p.wt(v, p.now(LA), p.now(LV))
                        e1 = p.ev(v.reciprocal(st_[:, 6:7], st_[:, 5:6]), LV, 1)
                        p.wt(v, e1, stor_ev[b])
                        cons_ev[b] = p.ev(v.tensor_scalar(out=ob[:, sl], in0=xb[:, sl],
                                                          scalar1=st_[:, 2 + b:3 + b],
                                                          scalar2=st_[:, 6:7],
                                                          op0=ALU.add, op1=ALU.mult), LV, 1)

                    @blk.scalar
                    def _(s, i=i, b=b, sl=sl):
                        p.wt(s, p.now(LV))
                        stor_ev[b] = p.ev(s.dma_start(out=xn_t[i], in_=ob[:, sl]),
                                          ST + str(b), 16)
                else:
                    @blk.vector
                    def _(v, i=i, b=b, sl=sl):
                        p.wt(v, p.now(LA), p.now(LV))
                        e1 = p.ev(v.reciprocal(st_[:, 6:7], st_[:, 5:6]), LV, 1)
                        p.wt(v, e1, ocp_ev[b], losub_ev[b])
                        cons_ev[b] = p.ev(v.tensor_scalar(out=ob32[:, sl], in0=xb[:, sl],
                                                          scalar1=st_[:, 2 + b:3 + b],
                                                          scalar2=st_[:, 6:7],
                                                          op0=ALU.add, op1=ALU.mult), LV, 1)

                    @blk.scalar
                    def _(s, i=i, b=b, sl=sl):
                        p.wt(s, cons_ev[b], stor_ev[b])
                        e1 = p.ev(s.activation(out=ob[:, sl], in_=ob32[:, sl],
                                               func=AF.Copy), LAc, 1)
                        p.wt(s, e1)
                        ocp_ev[b] = p.ev(s.activation(out=obf[:, sl], in_=ob[:, sl],
                                                      func=AF.Copy), LAc, 1)
                        stor_ev[b] = p.ev(s.dma_start(out=xn_t[i], in_=ob[:, sl]),
                                          ST + str(b), 16)

                    @blk.vector
                    def _(v, i=i, b=b, sl=sl):
                        p.wt(v, ocp_ev[b], lost_ev[b])
                        losub_ev[b] = p.ev(v.tensor_tensor(out=lob[:, sl],
                                                           in0=ob32[:, sl],
                                                           in1=obf[:, sl],
                                                           op=ALU.subtract), LV2, 1)

                    @blk.scalar
                    def _(s, i=i, b=b, sl=sl):
                        p.wt(s, losub_ev[b])
                        lost_ev[b] = p.ev(s.dma_start(out=xnlo_t[i], in_=lob[:, sl]),
                                          ST2 + str(b), 16)

                if pooled:
                    @blk.tensor
                    def _(t, i=i, b=b, sl=sl):
                        p.wt(t, p.now(LV))
                        sidx = i // TPS
                        first = (i % TPS) == 0
                        last = (i % TPS) == TPS - 1
                        t.matmul(poolp[2 * sidx][:, :], onec[:, :],
                                 ob[:, sl][:, 0:512], start=first, stop=last)
                        i2 = t.matmul(poolp[2 * sidx + 1][:, :], onec[:, :],
                                      ob[:, sl][:, 512:1024], start=first, stop=last)
                        pe_ev[b] = p.ev(i2, PEc, 1)

            if pooled:
                @blk.vector
                def _(v):
                    p.wt(v, p.now(PEc), p.now(LV))
                    for s in range(NSEQ):
                        for h in range(2):
                            p.ev(v.tensor_scalar(
                                out=poolr[:, s * D + h * 512: s * D + (h + 1) * 512],
                                in0=poolp[2 * s + h][:, :], scalar1=1.0 / SL,
                                scalar2=None, op0=ALU.mult), LV, 1)

                @blk.scalar
                def _(s):
                    p.wt(s, p.now(LV))
                    p.ev(s.dma_start(out=p.dram("pool1").rearrange("s d -> (s d)")[None, :],
                                     in_=poolr[:, :]), ST, 16)
            p.fence(blk)


def ph_attn0(p: P):
    nc = p.nc
    xn = p.dram("xn0")
    xin_t = p.dram("xin").rearrange("(n p) d -> n p d", p=128)
    xmid_t = p.dram("xmid0").rearrange("(n p) d -> n p d", p=128)
    q0 = p.dram("q0")
    q0_t = q0.rearrange("(n p) d -> n p d", p=128)
    v0_t = p.dram("v0").rearrange("(n p) d -> n p d", p=128)
    k0 = p.dram("k0")
    ek0 = p.dram("ek0")
    with ExitStack() as stk:
        p.stack = stk
        wq = p.sb("a0wq", [128, 8 * HID], BF16)
        wk = p.sb("a0wk", [128, 8 * HID], BF16)
        wv = p.sb("a0wv", [128, 8 * HID], BF16)
        wo = p.sb("a0wo", [128, 4 * D], BF16)
        bo = p.sb("a0bo", [1, D], BF16)
        ones1 = p.sb("a0one", [1, 128], BF16)
        xnf = p.sb("a0xnf", [128, 2 * 8 * BLK], BF16)
        kev = p.sb("a0kev", [128, 4 * BLK], F32)
        qf = p.sb("a0qf", [128, HID], F32)
        qe = p.sb("a0qe", [128, HID], F32)
        qs = p.sb("a0qs", [128, 2 * HID], BF16)
        vb = p.sb("a0vb", [128, 2 * HID], BF16)
        mq = p.sb("a0mq", [128, 3 * HEADS], F32)
        kst = p.sb("a0kst", [128, 32], F32)
        tm1 = p.sb("a0tm1", [128, 4], F32)
        ekb = p.sb("a0ekb", [128, 2 * BLK], BF16)
        zp = p.sb("a0zp", [128, 2], F32)
        ektm = p.sb("a0ektm", [128, 2 * HID], BF16)
        vtm = p.sb("a0vtm", [128, 2 * HID], BF16)
        ctxn = p.sb("a0ctxn", [64, HID], BF16)
        ctx2 = p.sb("a0ctx2", [128, 256], BF16)
        zlo = p.sb("a0zlo", [64, 8], F32)
        qfm = p.sb("a0qfm", [128, 2 * HID], BF16)
        af = p.sb("a0af", [128, HID], BF16)
        xmb = p.sb("a0xmb", [128, 2 * D], F32)
        xr = p.sb("a0xr", [128, 2 * D], F32)
        stkA = ExitStack()
        pk = [stkA.enter_context(nc.psum_tensor(f"a0pk{j}", [128, BLK], F32))
              for j in range(4)]
        pq = stkA.enter_context(nc.psum_tensor("a0pq", [128, BLK], F32))
        pv = stkA.enter_context(nc.psum_tensor("a0pv", [128, BLK], F32))

        with nc.Block("a0A") as blk:
            @blk.gpsimd
            def _(g):
                for name, dst, wdt, nch in [("L0_attn_Wq", wq, HID, 8),
                                            ("L0_attn_Wk", wk, HID, 8),
                                            ("L0_attn_Wv", wv, HID, 8)]:
                    ap = p.dram(name).rearrange("(c p) h -> c p h", p=128)
                    for c in range(nch):
                        p.ev(g.dma_start(out=dst[:, TS(c, wdt)], in_=ap[c]), "a0wl", 16)
                apo = p.dram("L0_attn_Wo").rearrange("(c p) d -> c p d", p=128)
                for c in range(4):
                    p.ev(g.dma_start(out=wo[:, TS(c, D)], in_=apo[c]), "a0wl", 16)
                p.ev(g.dma_start(out=bo[:, :], in_=p.dram("L0_attn_bo")[None, :]),
                     "a0wl", 16)
                p.ev(g.memset(ones1[:, :], 1.0), "a0ini", 1)
                for sq in range(NSEQ):
                    p.ev(g.memset(kst[:, 16 * sq:16 * sq + 4], -30000.0), "a0ini", 1)
                    p.ev(g.memset(kst[:, 16 * sq + 8:16 * sq + 12], 0.0), "a0ini", 1)
            wl_ev = [p.now("a0wl"), p.now("a0ini")]

            xnf_ev = [None, None]
            xl_ev = [None, None]
            kev_ev = [[], [], [], []]
            kps_ev = [None] * 4
            qs_ev = [None, None]
            vb_ev = [None, None]
            pq_ev = None
            pv_ev = None
            for s in range(NSEQ):
                for j in range(NBLK):
                    bb = (s * NBLK + j) % 2
                    t0 = s * SL + j * BLK

                    @blk.sync
                    def _(sy, bb=bb, t0=t0):
                        p.wt(sy, xnf_ev[bb])
                        for c in range(8):
                            ev = p.ev(sy.dma_start(
                                out=xnf[:, bb * 8 * BLK + c * BLK: bb * 8 * BLK + (c + 1) * BLK],
                                in_=xn[t0:t0 + BLK, TS(c, 128)], transpose=True),
                                f"a0xl{bb}", 16)
                        xl_ev[bb] = ev

                    @blk.tensor
                    def _(t, bb=bb):
                        p.wt(t, wl_ev, xl_ev[bb], *[e for e in kps_ev if e])
                        for kc in range(4):
                            for c in range(8):
                                inst = t.matmul(
                                    pk[kc][:, :],
                                    wk[:, c * HID + kc * 128: c * HID + (kc + 1) * 128],
                                    xnf[:, bb * 8 * BLK + c * BLK: bb * 8 * BLK + (c + 1) * BLK],
                                    start=(c == 0), stop=(c == 7))
                        p.ev(inst, "a0kmm", 1)

                    @blk.scalar
                    def _(a, s=s, j=j):
                        p.wt(a, p.now("a0kmm"))
                        for kc in range(4):
                            p.wt(a, kev_ev[kc])
                            e = p.ev(a.activation(out=kev[:, TS(kc, BLK)], in_=pk[kc][:, :],
                                                  func=AF.Copy), "a0ke", 1)
                            kps_ev[kc] = e
                            p.wt(a, e)
                            sev = p.ev(a.dma_start(
                                out=k0[s, kc * 128:(kc + 1) * 128, j * BLK:(j + 1) * BLK],
                                in_=kev[:, TS(kc, BLK)]), f"a0ks{kc}", 16)
                            kev_ev[kc] = [sev]

                    @blk.vector
                    def _(v, s=s):
                        p.wt(v, p.now("a0ke"), p.now("a0kv"))
                        for kc in range(4):
                            e = p.ev(v.reduce_max(tm1[:, kc:kc + 1], kev[:, TS(kc, BLK)],
                                                  axis=AX.X), "a0kv", 1)
                            kev_ev[kc].append(e)
                        p.wt(v, p.now("a0kv"))
                        p.ev(v.tensor_tensor(out=kst[:, 16 * s:16 * s + 4],
                                             in0=kst[:, 16 * s:16 * s + 4],
                                             in1=tm1[:, 0:4], op=ALU.max), "a0kv", 1)

                    for tt in range(4):
                        ti = (t0 + tt * 128) // 128
                        qb = (s * NBLK * 4 + j * 4 + tt) % 2

                        @blk.tensor
                        def _(t, bb=bb, tt=tt):
                            p.wt(t, pq_ev)
                            for c in range(8):
                                inst = t.matmul(
                                    pq[:, :],
                                    xnf[:, bb * 8 * BLK + c * BLK + tt * 128:
                                        bb * 8 * BLK + c * BLK + (tt + 1) * 128],
                                    wq[:, TS(c, HID)], start=(c == 0), stop=(c == 7))
                            p.ev(inst, "a0qmm", 1)

                        @blk.scalar
                        def _(a):
                            p.wt(a, p.now("a0qmm"), p.now("a0qv"))
                            p.ev(a.activation(out=qf[:, :], in_=pq[:, :][:, 0:HID],
                                              func=AF.Copy), "a0qe", 1)
                        pq_ev = p.now("a0qe")

                        @blk.vector
                        def _(v):
                            p.wt(v, p.now("a0qe"))
                            q3 = qf[:, :].rearrange("p (h d) -> p h d", d=DH)
                            e1 = p.ev(v.reduce_max(mq[:, 0:HEADS], q3, axis=AX.X), "a0qv", 1)
                            p.wt(v, e1)
                            mqb = mq[:, 0:HEADS].unsqueeze(2).broadcast_to([128, HEADS, DH])
                            p.ev(v.tensor_tensor(
                                out=qe[:, :].rearrange("p (h d) -> p h d", d=DH),
                                in0=q3, in1=mqb, op=ALU.subtract), "a0qv", 1)

                        @blk.scalar
                        def _(a):
                            p.wt(a, p.now("a0qv"))
                            p.ev(a.activation(out=qe[:, :], in_=qe[:, :], func=AF.Exp),
                                 "a0qe", 1)

                        @blk.vector
                        def _(v, qb=qb):
                            p.wt(v, p.now("a0qe"))
                            q3 = qe[:, :].rearrange("p (h d) -> p h d", d=DH)
                            e1 = p.ev(v.reduce_sum(mq[:, HEADS:2 * HEADS], q3, axis=AX.X),
                                      "a0qv", 1)
                            p.wt(v, e1)
                            e2 = p.ev(v.reciprocal(mq[:, 2 * HEADS:3 * HEADS],
                                                   mq[:, HEADS:2 * HEADS]), "a0qv", 1)
                            p.wt(v, e2, qs_ev[qb])
                            rqb = mq[:, 2 * HEADS:3 * HEADS].unsqueeze(2).broadcast_to(
                                [128, HEADS, DH])
                            p.ev(v.tensor_tensor(
                                out=qs[:, TS(qb, HID)].rearrange("p (h d) -> p h d", d=DH),
                                in0=q3, in1=rqb, op=ALU.mult), "a0qv", 1)

                        @blk.scalar
                        def _(a, ti=ti, qb=qb):
                            p.wt(a, p.now("a0qv"))
                            qs_ev[qb] = p.ev(a.dma_start(out=q0_t[ti],
                                                         in_=qs[:, TS(qb, HID)]),
                                             f"a0qss{qb}", 16)

                        @blk.tensor
                        def _(t, bb=bb, tt=tt):
                            p.wt(t, pv_ev)
                            for c in range(8):
                                inst = t.matmul(
                                    pv[:, :],
                                    xnf[:, bb * 8 * BLK + c * BLK + tt * 128:
                                        bb * 8 * BLK + c * BLK + (tt + 1) * 128],
                                    wv[:, TS(c, HID)], start=(c == 0), stop=(c == 7))
                            ev = p.ev(inst, "a0vmm", 1)
                            if tt == 3:
                                xnf_ev[bb] = ev

                        @blk.scalar
                        def _(a, ti=ti, qb=qb):
                            p.wt(a, p.now("a0vmm"), vb_ev[qb])
                            e = p.ev(a.activation(out=vb[:, TS(qb, HID)],
                                                  in_=pv[:, :][:, 0:HID], func=AF.Copy),
                                     "a0ve", 1)
                            p.wt(a, e)
                            vb_ev[qb] = p.ev(a.dma_start(out=v0_t[ti],
                                                         in_=vb[:, TS(qb, HID)]),
                                             f"a0vss{qb}", 16)
                        pv_ev = p.now("a0ve")

        stkA.close()
        stkB = ExitStack()
        pctx = stkB.enter_context(nc.psum_tensor("a0pctx", [64, HID], F32))
        patt = stkB.enter_context(nc.psum_tensor("a0patt", [128, BLK], F32))
        py = stkB.enter_context(nc.psum_tensor("a0py", [128, D], F32))

        ctx_ev = [None]
        for s in range(NSEQ):
            with nc.Block(f"a0B{s}") as blk:
                @blk.vector
                def _(v, s=s):
                    p.ev(v.tensor_scalar(out=kst[:, 16 * s + 4:16 * s + 8],
                                         in0=kst[:, 16 * s:16 * s + 4], scalar1=-1.0,
                                         scalar2=None, op0=ALU.mult), "b0v", 1)

                kl_ev = [None, None]
                exp_ev = [None, None]
                eks_ev = [p.now("b0es0"), p.now("b0es1")]
                zp_ev = [None, None]
                for kc in range(4):
                    for j in range(NBLK):
                        bb = (kc * NBLK + j) % 2

                        @blk.sync
                        def _(sy, s=s, kc=kc, j=j, bb=bb):
                            if s == 0 and kc == 0 and j == 0:
                                for kk in range(4):
                                    p.wt(sy, p.now(f"a0ks{kk}"))
                                p.wt(sy, p.now("a0vss0"), p.now("a0vss1"))
                            p.wt(sy, exp_ev[bb])
                            kl_ev[bb] = p.ev(sy.dma_start(
                                out=kev[:, TS(bb, BLK)],
                                in_=k0[s, kc * 128:(kc + 1) * 128, j * BLK:(j + 1) * BLK]),
                                f"b0kl{bb}", 16)

                        @blk.scalar
                        def _(a, s=s, kc=kc, j=j, bb=bb):
                            p.wt(a, kl_ev[bb], zp_ev[bb], eks_ev[bb], p.now("b0v"))
                            e = p.ev(a.activation(out=ekb[:, TS(bb, BLK)],
                                                  in_=kev[:, TS(bb, BLK)], func=AF.Exp,
                                                  bias=kst[:, 16 * s + 4 + kc:
                                                           16 * s + 5 + kc],
                                                  accum_out=zp[:, bb:bb + 1]), "b0e", 1)
                            exp_ev[bb] = e
                            p.wt(a, e)
                            eks_ev[bb] = p.ev(a.dma_start(
                                out=ek0[s, kc * 128:(kc + 1) * 128, j * BLK:(j + 1) * BLK],
                                in_=ekb[:, TS(bb, BLK)]), f"b0es{bb}", 16)

                        @blk.vector
                        def _(v, s=s, kc=kc, bb=bb):
                            p.wt(v, p.now("b0e"), p.now("b0v"))
                            zp_ev[bb] = p.ev(v.tensor_tensor(
                                out=kst[:, 16 * s + 8 + kc:16 * s + 9 + kc],
                                in0=kst[:, 16 * s + 8 + kc:16 * s + 9 + kc],
                                in1=zp[:, bb:bb + 1], op=ALU.add), "b0v", 1)

                @blk.vector
                def _(v, s=s):
                    p.wt(v, p.now("b0v"))
                    p.ev(v.reciprocal(kst[:, 16 * s + 12:16 * s + 16],
                                      kst[:, 16 * s + 8:16 * s + 12]), "b0v", 1)

                @blk.scalar
                def _(a, s=s):
                    p.wt(a, p.now("b0v"))
                    if s == 1:
                        p.wt(a, p.now("b0zl0"), p.now("b0cv"))
                    e = p.ev(a.dma_start(out=p.dram("zsc"),
                                         in_=kst[:, 16 * s + 12:16 * s + 16]),
                             f"b0zs{s}", 16)
                    p.wt(a, e)
                    p.ev(a.dma_start(
                        out=zlo[:, :].rearrange("d (hp kc) -> d hp kc", hp=2),
                        in_=p.dram("zsc").rearrange("(hp d) kc -> d hp kc", hp=2)),
                        f"b0zl{s}", 16)

                et_ev = [p.now("b0mm"), p.now("b0mm")]
                vt_ev = [p.now("b0mm"), p.now("b0mm")]
                el_evs = [None, None]
                vl_evs = [None, None]
                for i in range(TPS):
                    bb = i % 2

                    @blk.sync
                    def _(sy, s=s, i=i, bb=bb):
                        if i == 0:
                            p.wt(sy, p.now("b0es0"), p.now("b0es1"))
                        p.wt(sy, et_ev[bb])
                        el_evs[bb] = p.ev(sy.dma_start(
                            out=ektm[:, TS(bb, HID)],
                            in_=ek0[s, 0:HID, i * 128:(i + 1) * 128],
                            transpose=True), f"b0el{bb}", 16)

                    @blk.scalar
                    def _(a, s=s, i=i, bb=bb):
                        p.wt(a, vt_ev[bb])
                        vl_evs[bb] = p.ev(a.dma_start(out=vtm[:, TS(bb, HID)],
                                                      in_=v0_t[(s * SL) // 128 + i]),
                                          f"b0vl{bb}", 16)

                    @blk.tensor
                    def _(t, s=s, i=i, bb=bb):
                        p.wt(t, el_evs[bb], vl_evs[bb])
                        if i == 0:
                            p.wt(t, ctx_ev[0])
                        for h in range(HEADS):
                            inst = t.matmul(
                                pctx[:, TS(h, DH)],
                                ektm[:, bb * HID + h * DH: bb * HID + (h + 1) * DH],
                                vtm[:, bb * HID + h * DH: bb * HID + (h + 1) * DH],
                                start=(i == 0 and h == 0),
                                stop=(i == TPS - 1 and h == HEADS - 1))
                        e = p.ev(inst, "b0mm", 1)
                        et_ev[bb] = e
                        vt_ev[bb] = e

                @blk.vector
                def _(v, s=s):
                    p.wt(v, p.now("b0mm"), p.now("b0v"), p.now(f"b0zl{s}"))
                    for h in range(HEADS):
                        zc = zlo[:, (h % 2) * 4 + h // 2: (h % 2) * 4 + h // 2 + 1]
                        e = p.ev(v.tensor_scalar(out=ctxn[:, TS(h, DH)],
                                                 in0=pctx[:, TS(h, DH)],
                                                 scalar1=zc,
                                                 scalar2=float(SCALE), op0=ALU.mult,
                                                 op1=ALU.mult), "b0cv", 1)
                    ctx_ev[0] = e

                @blk.scalar
                def _(a, s=s):
                    p.wt(a, p.now("b0cv"))
                    if s == 1:
                        p.wt(a, p.now("c0am"))
                    e = p.ev(a.dma_start(
                        out=p.dram("ctxs").rearrange("hp d hc e -> d hc hp e"),
                        in_=ctxn[:, :].rearrange("p (hc hp e) -> p hc hp e",
                                                 hp=2, e=DH)),
                        f"b0cs{s}", 16)
                    p.wt(a, e)
                    p.ev(a.dma_start(
                        out=ctx2[:, :].rearrange("p (hc e) -> p hc e", e=DH),
                        in_=p.dram("ctxs").rearrange("hp d hc e -> (hp d) hc e")),
                        f"b0c2{s}", 16)

            with nc.Block(f"a0C{s}") as blk:
                qf_ev = [p.now("c0am"), p.now("c0am")]
                xr_ev = [p.now("c0v"), p.now("c0v")]
                af_hold = [p.now("c0ym")]
                xm_ev = [p.now("c0xs0"), p.now("c0xs1")]
                ql_evs = [None, None]
                xl_evs = [None, None]
                patt_ev = p.now("c0ae")
                py_ev = p.now("c0v")
                for i in range(TPS):
                    bb = i % 2
                    ti = (s * SL) // 128 + i

                    @blk.sync
                    def _(sy, s=s, i=i, bb=bb, ti=ti):
                        if i == 0 and s == 0:
                            p.wt(sy, p.now("a0qss0"), p.now("a0qss1"))
                        p.wt(sy, qf_ev[bb])
                        for c in range(4):
                            e = p.ev(sy.dma_start(
                                out=qfm[:, bb * HID + c * 128: bb * HID + (c + 1) * 128],
                                in_=q0[ti * 128:(ti + 1) * 128, TS(c, 128)],
                                transpose=True), f"c0ql{bb}", 16)
                        ql_evs[bb] = e

                    @blk.scalar
                    def _(a, i=i, bb=bb, ti=ti):
                        p.wt(a, xr_ev[bb])
                        xl_evs[bb] = p.ev(a.dma_start(out=xr[:, TS(bb, D)], in_=xin_t[ti]),
                                          f"c0xl{bb}", 16)

                    @blk.tensor
                    def _(t, i=i, bb=bb, s=s):
                        p.wt(t, ql_evs[bb], patt_ev)
                        if i == 0:
                            p.wt(t, p.now(f"b0c2{s}"))
                        for c in range(4):
                            for hh in range(2):
                                pb = hh * 64
                                inst = t.matmul(
                                    patt[pb:pb + 64, TS(c, 128)],
                                    ctx2[pb:pb + 64, TS(c, DH)],
                                    qfm[:, bb * HID + c * 128:
                                        bb * HID + (c + 1) * 128][pb:pb + 64, :],
                                    start=(c == 0), stop=(c == 3),
                                    skip_group_check=True)
                        qf_ev[bb] = p.ev(inst, "c0am", 1)

                    @blk.scalar
                    def _(a, i=i, bb=bb):
                        p.wt(a, p.now("c0am"), af_hold[0])
                        p.ev(a.activation(out=af[:, :], in_=patt[:, :],
                                          func=AF.Copy), "c0ae", 1)
                    patt_ev = p.now("c0ae")

                    @blk.tensor
                    def _(t, i=i, bb=bb):
                        p.wt(t, p.now("c0ae"), py_ev)
                        for half in range(2):
                            for c in range(4):
                                t.matmul(py[:, TS(half, 512)],
                                         af[:, TS(c, 128)],
                                         wo[:, c * D + half * 512: c * D + (half + 1) * 512],
                                         start=(c == 0), stop=False)
                            inst = t.matmul(py[:, TS(half, 512)], ones1[:, :],
                                            bo[:, TS(half, 512)], start=False, stop=True)
                        af_hold[0] = p.ev(inst, "c0ym", 1)

                    @blk.vector
                    def _(v, i=i, bb=bb):
                        p.wt(v, p.now("c0ym"), xl_evs[bb], xm_ev[bb])
                        e = p.ev(v.tensor_tensor(out=xmb[:, TS(bb, D)], in0=py[:, :],
                                                 in1=xr[:, TS(bb, D)], op=ALU.add), "c0v", 1)
                        xr_ev[bb] = e
                    py_ev = p.now("c0v")

                    @blk.scalar
                    def _(a, i=i, bb=bb, ti=ti):
                        p.wt(a, p.now("c0v"))
                        xm_ev[bb] = p.ev(a.dma_start(out=xmid_t[ti], in_=xmb[:, TS(bb, D)]),
                                         f"c0xs{bb}", 16)

        stkB.close()
        with nc.Block("a0F") as blk:
            p.fence(blk)


def _ph_ffn(p: P, layer):
    nc = p.nc
    if layer == 0:
        xn, base, outs, pre = p.dram("xn0"), p.dram("xmid0"), p.dram("x1"), "f0"
    else:
        xn, base, outs, pre = p.dram("xn1"), p.dram("xmid1"), p.dram("out"), "f1"
    outs_t = outs.rearrange("(n p) d -> n p d", p=128)
    with ExitStack() as stk:
        p.stack = stk
        w1t = p.sb(pre + "w1", [128, 8 * FFD], BF16)
        w2t = p.sb(pre + "w2", [128, 8 * FFD], BF16)
        xnf = p.sb(pre + "xnf", [128, 8 * BLK], BF16)
        hb = p.sb(pre + "hb", [128, 32 * BLK], BF16)
        yst = p.sb(pre + "yst", [128, 4 * D], F32)
        ones1 = p.sb(pre + "o1", [1, 128], BF16)
        if layer == 0:
            ones5 = p.sb(pre + "o5", [1, BLK], BF16)
            b1e = p.sb(pre + "b1", [1, FFD], BF16)
            b2e = p.sb(pre + "b2", [1, D], BF16)
        ph_ = [p.ps(pre + "ph0", [128, BLK]), p.ps(pre + "ph1", [128, BLK])]
        py_ = [p.ps(pre + "py0", [128, BLK]), p.ps(pre + "py1", [128, BLK])]

        with nc.Block(pre + "R") as blk:
            @blk.gpsimd
            def _(g):
                p.ev(g.memset(ones1[:, :], 1.0), pre + "g", 1)
                if layer == 0:
                    p.ev(g.memset(ones5[:, :], 1.0), pre + "g", 1)
                p.ev(g.dma_start(out=outs, in_=base), pre + "bc", 16)

            if layer == 0:
                rxn = p.sb(pre + "rxn", [128, 8 * 128], BF16)
                rxnlo = p.sb(pre + "rxl", [128, 8 * 128], BF16)
                wrt = p.sb(pre + "wrt", [128, 8 * 128], BF16)
                wrtlo = p.sb(pre + "wrtl", [128, 8 * 128], BF16)
                brr = p.sb(pre + "brr", [1, 8], BF16)
                brf = p.sb(pre + "brf", [1, 8], F32)
                brh = p.sb(pre + "brh", [1, 8], F32)
                brlo = p.sb(pre + "brlo", [1, 8], BF16)
                lgt = p.sb(pre + "lgt", [128, 8], F32)
                lmx = p.sb(pre + "lmx", [128, 1], F32)
                mcols = p.sb(pre + "mc", [128, NT * 8], F32)
                plg = p.ps(pre + "plg", [128, 512])
                wrbd = p.dram("wrb")
                wrlod = p.dram("wrlob")
                xnlo = p.dram("xnlo0")

                @blk.sync
                def _(sy):
                    for c in range(8):
                        p.ev(sy.dma_start(out=wrt[:, TS(c, 128)],
                                          in_=wrbd[:, TS(c, 128)], transpose=True),
                             pre + "wl", 16)
                        p.ev(sy.dma_start(out=wrtlo[:, TS(c, 128)],
                                          in_=wrlod[:, TS(c, 128)], transpose=True),
                             pre + "wl", 16)

                @blk.gpsimd
                def _(g):
                    p.ev(g.dma_start(out=brr[:, :], in_=p.dram("L0_ffn_br")[None, :]),
                         pre + "wlb", 16)
                    p.ev(g.dma_start(out=brf[:, :], in_=p.dram("L0_ffn_br")[None, :]),
                         pre + "wlb", 16)

                @blk.scalar
                def _(a):
                    p.wt(a, p.now(pre + "wlb"))
                    p.ev(a.activation(out=brh[:, :], in_=brr[:, :], func=AF.Copy),
                         pre + "ba", 1)

                @blk.vector
                def _(v):
                    p.wt(v, p.now(pre + "ba"))
                    p.ev(v.tensor_tensor(out=brlo[:, :], in0=brf[:, :], in1=brh[:, :],
                                         op=ALU.subtract), pre + "bv", 1)
                rt_wl = [p.now(pre + "wl"), p.now(pre + "wlb"), p.now(pre + "bv")]

                xe_ev = [None, None]
                rxl_evs = [None, None]
                lg_ev = None
                for ti in range(NT):
                    bb = ti % 2

                    @blk.sync
                    def _(sy, ti=ti):
                        p.wt(sy, xe_ev[0])
                        for c in range(8):
                            p.ev(sy.dma_start(
                                out=rxn[:, TS(c, 128)],
                                in_=xn[ti * 128:(ti + 1) * 128, TS(c, 128)],
                                transpose=True), pre + "rxl", 16)
                            e = p.ev(sy.dma_start(
                                out=rxnlo[:, TS(c, 128)],
                                in_=xnlo[ti * 128:(ti + 1) * 128, TS(c, 128)],
                                transpose=True), pre + "rxl", 16)
                        rxl_evs[0] = e

                    @blk.tensor
                    def _(t, ti=ti):
                        p.wt(t, rt_wl, rxl_evs[0], lg_ev)
                        for c in range(8):
                            t.matmul(plg[:, 0:8],
                                     rxn[:, TS(c, 128)],
                                     wrt[:, c * 128:c * 128 + 8],
                                     start=(c == 0), stop=False)
                        for c in range(8):
                            t.matmul(plg[:, 0:8],
                                     rxnlo[:, TS(c, 128)],
                                     wrt[:, c * 128:c * 128 + 8],
                                     start=False, stop=False)
                        for c in range(8):
                            t.matmul(plg[:, 0:8],
                                     rxn[:, TS(c, 128)],
                                     wrtlo[:, c * 128:c * 128 + 8],
                                     start=False, stop=False)
                        t.matmul(plg[:, 0:8], ones1[:, :], brr[:, :],
                                 start=False, stop=False)
                        inst = t.matmul(plg[:, 0:8], ones1[:, :], brlo[:, :],
                                        start=False, stop=True)
                        xe_ev[0] = p.ev(inst, pre + "rmm", 1)

                    @blk.scalar
                    def _(a, ti=ti):
                        p.wt(a, p.now(pre + "rmm"), p.now(pre + "rv"))
                        p.ev(a.activation(out=lgt[:, :], in_=plg[:, 0:8], func=AF.Copy),
                             pre + "re", 1)
                    lg_ev = p.now(pre + "re")

                    @blk.vector
                    def _(v, ti=ti):
                        p.wt(v, p.now(pre + "re"))
                        e = p.ev(v.reduce_max(lmx[:, :], lgt[:, :], axis=AX.X), pre + "rv", 1)
                        p.wt(v, e)
                        p.ev(v.tensor_scalar(out=mcols[:, ti * 8:(ti + 1) * 8],
                                             in0=lgt[:, :], scalar1=lmx[:, 0:1],
                                             scalar2=None, op0=ALU.is_ge), pre + "rv", 1)

            else:
                wr1 = p.sb(pre + "wr1", [128, 8 * 8], F32)
                pcol = p.sb(pre + "pc", [128, 8], F32)
                lgt = p.sb(pre + "lg", [1, 32], F32)
                onesf = p.sb(pre + "of", [1, 128], F32)
                siota = p.sb(pre + "si", [1, 8], F32)
                seltmp = p.sb(pre + "stm", [1, 8], F32)
                selr = p.sb(pre + "slr", [1, NSEQ * 4], F32)
                sel = p.sb(pre + "sel", [128, 8], F32)
                maskT = p.sb(pre + "mkT", [8, NSEQ * 2], BF16)
                maskTf = p.sb(pre + "mkF", [8, NSEQ * 2], F32)
                mones = p.sb(pre + "mo", [8, NSEQ * 2 * 512], BF16)
                b1all = p.sb(pre + "b1a", [8, FFD], BF16)
                b2all = p.sb(pre + "b2a", [8, D], BF16)
                b2row = p.sb(pre + "b2r", [1, 2 * D], BF16)
                b2st = p.sb(pre + "b2t", [1, 512], BF16)
                iotW1 = p.sb(pre + "io1", [128, 16], F32)
                iotW2 = p.sb(pre + "io2", [128, 32], F32)
                ebase = p.sb(pre + "eb", [128, 2], F32)
                idxt = p.sb(pre + "ixt", [128, 32], F32)
                idxw1a = p.sb(pre + "ix1", [128, NSEQ * 2 * 16], U32)
                idxw2a = p.sb(pre + "ix2", [128, NSEQ * 2 * 32], U32)
                plg = p.ps(pre + "plg", [128, 512])
                pb2 = p.ps(pre + "pb2", [1, 512])
                pool1 = p.dram("pool1")
                mrows = p.dram("mrows")

                @blk.gpsimd
                def _(g):
                    p.ev(g.memset(onesf[:, :], 1.0), pre + "g", 1)
                    p.ev(g.memset(mones[:, :], 1.0), pre + "g", 1)
                    p.ev(g.iota(siota[:, :], pattern=[[1, 8]], base=0,
                                channel_multiplier=0,
                                allow_small_or_imprecise_dtypes=True), pre + "g", 1)
                    p.ev(g.iota(iotW1[:, :].rearrange("p (a b) -> p a b", b=2),
                                pattern=[[256, 8], [1, 2]], base=0,
                                channel_multiplier=2,
                                allow_small_or_imprecise_dtypes=True), pre + "g", 1)
                    p.ev(g.iota(iotW2[:, :], pattern=[[128, 32]], base=0,
                                channel_multiplier=1,
                                allow_small_or_imprecise_dtypes=True), pre + "g", 1)
                    ap = p.dram("L1_ffn_Wr").rearrange("(c p) e -> c p e", p=128)
                    for c in range(8):
                        p.ev(g.dma_start(out=wr1[:, TS(c, 8)], in_=ap[c]), pre + "wl", 16)
                    p.ev(g.dma_start(out=b1all[:, :], in_=p.dram("L1_ffn_b1")),
                         pre + "wl", 16)
                    p.ev(g.dma_start(out=b2all[:, :], in_=p.dram("L1_ffn_b2")),
                         pre + "wl", 16)

                for s in range(NSEQ):
                    @blk.gpsimd
                    def _(g, s=s):
                        p.wt(g, p.now(pre + "rmm"))
                        p.ev(g.dma_start(out=pcol[:, :],
                                         in_=pool1[s].rearrange("(c p) -> p c", p=128),
                                         allow_slow_non_contiguous=True), pre + "pl", 16)

                    @blk.tensor
                    def _(t, s=s):
                        p.wt(t, p.now(pre + "wl"), p.now(pre + "pl"), p.now(pre + "re"))
                        for c in range(8):
                            inst = t.matmul(plg[0:1, 0:8], pcol[:, c:c + 1],
                                            wr1[:, TS(c, 8)], start=(c == 0), stop=(c == 7))
                        p.ev(inst, pre + "rmm", 1)

                    @blk.scalar
                    def _(a, s=s):
                        p.wt(a, p.now(pre + "rmm"), p.now(pre + "rv"))
                        p.ev(a.activation(out=lgt[:, 0:8], in_=plg[0:1, 0:8], func=AF.Copy),
                             pre + "re", 1)

                    @blk.vector
                    def _(v, s=s):
                        p.wt(v, p.now(pre + "re"), p.now(pre + "ms"))
                        e = p.ev(v.reduce_max(lgt[:, 8:9], lgt[:, 0:8], axis=AX.X),
                                 pre + "rv", 1)
                        p.wt(v, e)
                        e = p.ev(v.tensor_scalar(out=lgt[:, 16:24], in0=lgt[:, 0:8],
                                                 scalar1=lgt[:, 8:9], scalar2=None,
                                                 op0=ALU.is_ge), pre + "rv", 1)
                        p.wt(v, e)
                        e = p.ev(v.scalar_tensor_tensor(out=lgt[:, 24:32], in0=lgt[:, 16:24],
                                                        scalar=-1e30, in1=lgt[:, 0:8],
                                                        op0=ALU.mult, op1=ALU.add),
                                 pre + "rv", 1)
                        p.wt(v, e)
                        e = p.ev(v.reduce_max(lgt[:, 9:10], lgt[:, 24:32], axis=AX.X),
                                 pre + "rv", 1)
                        p.wt(v, e)
                        e = p.ev(v.tensor_scalar(out=lgt[:, 24:32], in0=lgt[:, 24:32],
                                                 scalar1=lgt[:, 9:10], scalar2=None,
                                                 op0=ALU.is_ge), pre + "rv", 1)
                        p.wt(v, e)
                        e = p.ev(v.tensor_tensor(out=lgt[:, 10:11], in0=lgt[:, 9:10],
                                                 in1=lgt[:, 8:9], op=ALU.subtract),
                                 pre + "rv", 1)

                    @blk.scalar
                    def _(a, s=s):
                        p.wt(a, p.now(pre + "rv"))
                        p.ev(a.activation(out=lgt[:, 11:12], in_=lgt[:, 10:11], func=AF.Exp),
                             pre + "re", 1)

                    @blk.vector
                    def _(v, s=s):
                        p.wt(v, p.now(pre + "re"), p.now(pre + "g"))
                        e = p.ev(v.tensor_scalar(out=lgt[:, 12:13], in0=lgt[:, 11:12],
                                                 scalar1=1.0, scalar2=None, op0=ALU.add),
                                 pre + "rv", 1)
                        p.wt(v, e)
                        e = p.ev(v.reciprocal(lgt[:, 13:14], lgt[:, 12:13]), pre + "rv", 1)
                        p.wt(v, e)
                        e = p.ev(v.tensor_tensor(out=lgt[:, 14:15], in0=lgt[:, 11:12],
                                                 in1=lgt[:, 13:14], op=ALU.mult), pre + "rv", 1)
                        p.wt(v, e)
                        e = p.ev(v.tensor_tensor(out=seltmp[:, :], in0=lgt[:, 16:24],
                                                 in1=siota[:, :], op=ALU.mult), pre + "rv", 1)
                        p.wt(v, e)
                        e = p.ev(v.reduce_sum(selr[:, 4 * s:4 * s + 1], seltmp[:, :],
                                              axis=AX.X), pre + "rv", 1)
                        p.wt(v, e)
                        e = p.ev(v.tensor_tensor(out=seltmp[:, :], in0=lgt[:, 24:32],
                                                 in1=siota[:, :], op=ALU.mult), pre + "rv", 1)
                        p.wt(v, e)
                        e = p.ev(v.reduce_sum(selr[:, 4 * s + 1:4 * s + 2], seltmp[:, :],
                                              axis=AX.X), pre + "rv", 1)
                        p.wt(v, e)
                        e = p.ev(v.tensor_copy(out=selr[:, 4 * s + 2:4 * s + 3],
                                               in_=lgt[:, 13:14]), pre + "rv", 1)
                        p.wt(v, e)
                        e = p.ev(v.tensor_copy(out=selr[:, 4 * s + 3:4 * s + 4],
                                               in_=lgt[:, 14:15]), pre + "rv", 1)

                    @blk.scalar
                    def _(a, s=s):
                        p.wt(a, p.now(pre + "rv"))
                        p.ev(a.dma_start(out=mrows[2 * s][None, :], in_=lgt[:, 16:24]),
                             pre + "ms", 16)
                        p.ev(a.dma_start(out=mrows[2 * s + 1][None, :], in_=lgt[:, 24:32]),
                             pre + "ms", 16)

                @blk.tensor
                def _(t):
                    p.wt(t, p.now(pre + "rv"), p.now(pre + "g"))
                    inst = t.matmul(plg[:, 16:24], onesf[:, :], selr[:, :],
                                    start=True, stop=True)
                    p.ev(inst, pre + "bm", 1)

                @blk.scalar
                def _(a):
                    p.wt(a, p.now(pre + "bm"))
                    p.ev(a.activation(out=sel[:, :], in_=plg[:, 16:24], func=AF.Copy),
                         pre + "sb", 1)

                @blk.gpsimd
                def _(g):
                    p.wt(g, p.now(pre + "ms"))
                    p.ev(g.dma_start(out=maskT[:, :], in_=mrows.rearrange("k e -> e k"),
                                     allow_slow_non_contiguous=True), pre + "mt", 16)
                    p.ev(g.dma_start(out=maskTf[:, :], in_=mrows.rearrange("k e -> e k"),
                                     allow_slow_non_contiguous=True), pre + "mt", 16)

                @blk.vector
                def _(v):
                    p.wt(v, p.now(pre + "mt"), p.now(pre + "g"))
                    for sk in range(NSEQ * 2):
                        p.ev(v.tensor_scalar(out=mones[:, TS(sk, 512)],
                                             in0=mones[:, TS(sk, 512)],
                                             scalar1=maskTf[:, sk:sk + 1], scalar2=None,
                                             op0=ALU.mult), pre + "mo", 1)

                for sk in range(NSEQ * 2):
                    for half in range(2):
                        @blk.tensor
                        def _(t, sk=sk, half=half):
                            p.wt(t, p.now(pre + "mt"), p.now(pre + "wl"),
                                 p.now(pre + "be"))
                            p.ev(t.matmul(pb2[:, :], maskT[:, sk:sk + 1],
                                          b2all[0:8, half * 512:(half + 1) * 512],
                                          start=True, stop=True), pre + "bm", 1)

                        @blk.scalar
                        def _(a, sk=sk, half=half):
                            p.wt(a, p.now(pre + "bm"))
                            ebe = p.ev(a.activation(out=b2st[:, :], in_=pb2[0:1, :],
                                                    func=AF.Copy), pre + "be", 1)
                            p.wt(a, ebe)
                            p.ev(a.dma_start(
                                out=p.dram("b2w")[sk][None, half * 512:(half + 1) * 512],
                                in_=b2st[:, :]), pre + "be", 16)

                @blk.vector
                def _(v):
                    p.wt(v, p.now(pre + "sb"), p.now(pre + "g"))
                    for sk in range(NSEQ * 2):
                        s_, k_ = sk // 2, sk % 2
                        ecol = sel[:, 4 * s_ + k_:4 * s_ + k_ + 1]
                        e = p.ev(v.tensor_scalar(out=ebase[:, 0:1], in0=ecol,
                                                 scalar1=2048.0, scalar2=None,
                                                 op0=ALU.mult), pre + "ix", 1)
                        p.wt(v, e)
                        e = p.ev(v.tensor_scalar(out=idxt[:, 0:16], in0=iotW1[:, :],
                                                 scalar1=ebase[:, 0:1], scalar2=None,
                                                 op0=ALU.add), pre + "ix", 1)
                        p.wt(v, e)
                        e = p.ev(v.tensor_copy(out=idxw1a[:, TS(sk, 16)],
                                               in_=idxt[:, 0:16]), pre + "ix", 1)
                        p.wt(v, e)
                        e = p.ev(v.tensor_scalar(out=ebase[:, 1:2], in0=ecol,
                                                 scalar1=4096.0, scalar2=None,
                                                 op0=ALU.mult), pre + "ix", 1)
                        p.wt(v, e)
                        e = p.ev(v.tensor_scalar(out=idxt[:, :], in0=iotW2[:, :],
                                                 scalar1=ebase[:, 1:2], scalar2=None,
                                                 op0=ALU.add), pre + "ix", 1)
                        p.wt(v, e)
                        e = p.ev(v.tensor_copy(out=idxw2a[:, TS(sk, 32)],
                                               in_=idxt[:, :]), pre + "ix", 1)
                        p.wt(v, e)

        with nc.Block(pre + "E") as blk:
            if layer == 0:
                w1bd = p.dram("w1b")
                w2bd = p.dram("w2b")
                passes = [(e, None, list(range(NBLKC))) for e in range(NE)]
            else:
                w1f = p.dram("L1_ffn_W1").rearrange("e d (x h) -> (e d x) h", h=2048)
                w2f = p.dram("L1_ffn_W2").rearrange("e f d -> (e f) d")
                passes = [(None, 2 * s + k, list(range(s * NBLK, (s + 1) * NBLK)))
                          for s in range(NSEQ) for k in range(2)]
            acc_ev = {}
            xl_now = [None]
            hch_ev = [None, None]
            ych_ev = [None, None]
            hbl_ev = [None] * 32
            yst_ev = [None] * 8
            hmm_now = [None]
            ymm_now = [None]
            for pi, (e, sk, blkids) in enumerate(passes):
                if layer == 0:
                    @blk.sync
                    def _(sy, e=e, pi=pi):
                        if pi >= 1:
                            p.wt(sy, p.now(pre + "hmm"))
                        for dc in range(8):
                            for fg in range(8):
                                p.ev(sy.dma_start(
                                    out=w1t[:, dc * FFD + fg * 512: dc * FFD + (fg + 1) * 512],
                                    in_=w1bd[e * FFD + fg * 512:e * FFD + (fg + 1) * 512,
                                             TS(dc, 128)], transpose=True), pre + "w1l", 16)

                    @blk.sync
                    def _(sy, e=e, pi=pi):
                        if pi >= 1:
                            p.wt(sy, p.now(pre + "ymm"))
                        for fc in range(32):
                            for dh in range(2):
                                p.ev(sy.dma_start(
                                    out=w2t[:, fc * 1024 + dh * 512: fc * 1024 + (dh + 1) * 512],
                                    in_=w2bd[e * D + dh * 512:e * D + (dh + 1) * 512,
                                             TS(fc, 128)], transpose=True), pre + "w2l", 16)

                    @blk.gpsimd
                    def _(g, e=e, pi=pi):
                        if pi >= 1:
                            p.wt(g, p.now(pre + "hmm"), p.now(pre + "ymm"))
                        p.ev(g.dma_start(out=b1e[:, :],
                                         in_=p.dram("L0_ffn_b1")[e][None, :]), pre + "bl", 16)
                        p.ev(g.dma_start(out=b2e[:, :],
                                         in_=p.dram("L0_ffn_b2")[e][None, :]), pre + "bl", 16)
                    wle = [p.now(pre + "w1l"), p.now(pre + "w2l"), p.now(pre + "bl"),
                           p.now(pre + "g")]
                else:
                    @blk.gpsimd
                    def _(g, sk=sk, pi=pi):
                        if pi >= 1:
                            p.wt(g, p.now(pre + "hmm"))
                        for dc in range(8):
                            for half in range(2):
                                c = sk * 16 + dc * 2 + half
                                p.ev(g.indirect_dma_start(
                                    out=w1t[:, dc * FFD + half * 2048:
                                            dc * FFD + half * 2048 + 2048],
                                    out_offset=None, in_=w1f,
                                    in_offset=IndirectOffsetOnAxis(
                                        ap=idxw1a[:, c:c + 1], axis=0)),
                                    pre + "w1l", 16)

                    @blk.gpsimd
                    def _(g, sk=sk, pi=pi):
                        if pi >= 1:
                            p.wt(g, p.now(pre + "ymm"))
                        p.wt(g, p.now(pre + "be"))
                        for fc in range(32):
                            c = sk * 32 + fc
                            p.ev(g.indirect_dma_start(
                                out=w2t[:, TS(fc, D)], out_offset=None, in_=w2f,
                                in_offset=IndirectOffsetOnAxis(
                                    ap=idxw2a[:, c:c + 1], axis=0)),
                                pre + "w2l", 16)
                        p.ev(g.dma_start(out=b2row[:, (pi % 2) * D:(pi % 2 + 1) * D],
                                         in_=p.dram("b2w")[sk][None, :]),
                             pre + "w2l", 16)
                    wle = [p.now(pre + "w1l"), p.now(pre + "w2l"), p.now(pre + "g"),
                           p.now(pre + "wl")]

                for blkid in blkids:
                    s = blkid // NBLK
                    t0 = blkid * BLK

                    @blk.sync
                    def _(sy, blkid=blkid, t0=t0):
                        p.wt(sy, p.now(pre + "hmm"))
                        for c in range(8):
                            ev = p.ev(sy.dma_start(
                                out=xnf[:, c * BLK:(c + 1) * BLK],
                                in_=xn[t0:t0 + BLK, TS(c, 128)], transpose=True),
                                pre + "xl", 16)
                        xl_now[0] = ev

                    for fc in range(32):
                        @blk.tensor
                        def _(t, sk=sk, fc=fc):
                            if fc == 0:
                                p.wt(t, *wle)
                                p.wt(t, xl_now[0])
                            p.wt(t, hch_ev[fc % 2])
                            for dc in range(8):
                                t.matmul(ph_[fc % 2][:, :],
                                         w1t[:, dc * FFD + fc * 128: dc * FFD + (fc + 1) * 128],
                                         xnf[:, TS(dc, BLK)], start=(dc == 0), stop=False)
                            if layer == 0:
                                inst = t.matmul(ph_[fc % 2][:, :], b1e[:, TS(fc, 128)],
                                                ones5[:, :], start=False, stop=True)
                            else:
                                inst = t.matmul(ph_[fc % 2][:, :], b1all[0:8, TS(fc, 128)],
                                                mones[0:8, TS(sk, 512)],
                                                start=False, stop=True)
                            hmm_now[0] = p.ev(inst, pre + "hmm", 1)

                        @blk.scalar
                        def _(a, fc=fc):
                            p.wt(a, hmm_now[0], hbl_ev[fc])
                            hch_ev[fc % 2] = p.ev(a.activation(
                                out=hb[:, TS(fc, BLK)], in_=ph_[fc % 2][:, :],
                                func=(AF.Relu if layer == 0 else AF.Gelu)), pre + "he", 1)

                    for tt in range(4):
                        for half in range(2):
                            pid = (2 * tt + half) % 2

                            @blk.tensor
                            def _(t, pi=pi, tt=tt, half=half, pid=pid):
                                if tt == 0 and half == 0:
                                    p.wt(t, p.now(pre + "he"))
                                p.wt(t, ych_ev[pid])
                                for fc in range(32):
                                    t.matmul(
                                        py_[pid][:, :],
                                        hb[:, fc * BLK + tt * 128: fc * BLK + (tt + 1) * 128],
                                        w2t[:, fc * 1024 + half * 512:
                                            fc * 1024 + (half + 1) * 512] if layer == 0 else
                                        w2t[:, fc * D + half * 512: fc * D + (half + 1) * 512],
                                        start=(fc == 0), stop=False)
                                if layer == 0:
                                    inst = t.matmul(py_[pid][:, :], ones1[:, :],
                                                    b2e[:, TS(half, 512)],
                                                    start=False, stop=True)
                                else:
                                    inst = t.matmul(py_[pid][:, :], ones1[:, :],
                                                    b2row[:, (pi % 2) * D + half * 512:
                                                          (pi % 2) * D + (half + 1) * 512],
                                                    start=False, stop=True)
                                ymm_now[0] = p.ev(inst, pre + "ymm", 1)
                                if tt == 3 and half == 1:
                                    for fc in range(32):
                                        hbl_ev[fc] = ymm_now[0]

                            @blk.vector
                            def _(v, e=e, sk=sk, blkid=blkid, s=s, tt=tt, half=half,
                                  pid=pid):
                                p.wt(v, ymm_now[0], yst_ev[2 * tt + half])
                                if layer == 0:
                                    sc = mcols[:, (blkid * 4 + tt) * 8 + e:
                                               (blkid * 4 + tt) * 8 + e + 1]
                                else:
                                    sc = sel[:, 4 * (sk // 2) + 2 + (sk % 2):
                                             4 * (sk // 2) + 3 + (sk % 2)]
                                ych_ev[pid] = p.ev(v.tensor_scalar(
                                    out=yst[:, tt * D + half * 512: tt * D + (half + 1) * 512],
                                    in0=py_[pid][:, :], scalar1=sc, scalar2=None,
                                    op0=ALU.mult), pre + "yv", 1)

                    yv_done = p.now(pre + "yv")

                    @blk.gpsimd
                    def _(g, blkid=blkid, first=(acc_ev.get(blkid) is None)):
                        p.wt(g, yv_done)
                        p.wt(g, acc_ev.get(blkid))
                        if first:
                            p.wt(g, p.now(pre + "bc"))
                        for tt in range(4):
                            ev = p.ev(g.dma_start(
                                out=outs_t[blkid * 4 + tt],
                                in_=yst[:, TS(tt, D)], accum_op=ALU.add),
                                pre + f"ac{blkid}", 16)
                        for q in range(8):
                            yst_ev[q] = ev
                        acc_ev[blkid] = ev

        with nc.Block(pre + "F") as blk:
            p.fence(blk)


def ph_ffn0(p: P):
    _ph_ffn(p, 0)


def ph_ffn1(p: P):
    _ph_ffn(p, 1)


def ph_attn1(p: P):
    nc = p.nc
    xn = p.dram("xn1")
    x1_t = p.dram("x1").rearrange("(n p) d -> n p d", p=128)
    xm_t = p.dram("xmid1").rearrange("(n p) d -> n p d", p=128)
    th1 = p.dram("th1")
    th1_t = th1.rearrange("(n p) d -> n p d", p=128)
    ph1_t = p.dram("ph1").rearrange("(n p) d -> n p d", p=128)
    g1_t = p.dram("g1").rearrange("(n p) d -> n p d", p=128)
    with ExitStack() as stk:
        p.stack = stk
        wth = p.sb("a1wth", [128, 8 * LAT], BF16)
        wph = p.sb("a1wph", [128, 8 * LAT], BF16)
        wg = p.sb("a1wg", [128, 8 * D], BF16)
        xnf = p.sb("a1xnf", [128, 2 * 8 * 128], BF16)
        thb = p.sb("a1thb", [128, 2 * LAT], BF16)
        phb = p.sb("a1phb", [128, 2 * LAT], BF16)
        gb = p.sb("a1gb", [128, 2 * D], BF16)
        phl = p.sb("a1phl", [128, 2 * LAT], BF16)
        gl = p.sb("a1gl", [128, 2 * D], BF16)
        msb = p.sb("a1msb", [128, 2 * D], BF16)
        thf = p.sb("a1thf", [128, 2 * LAT], BF16)
        x1b = p.sb("a1x1", [128, 2 * D], F32)
        xmb = p.sb("a1xm", [128, 2 * D], F32)
        stkP = ExitStack()
        pth = stkP.enter_context(nc.psum_tensor("a1pth", [128, 512], F32))
        pph = stkP.enter_context(nc.psum_tensor("a1pph", [128, 512], F32))
        pg = stkP.enter_context(nc.psum_tensor("a1pg", [128, D], F32))

        with nc.Block("a1P") as blk:
            @blk.sync
            def _(sy):
                wthb, wphb, wgb = p.dram("wthb"), p.dram("wphb"), p.dram("wgb")
                for c in range(8):
                    p.ev(sy.dma_start(out=wth[:, TS(c, LAT)],
                                      in_=wthb[0:LAT, TS(c, 128)], transpose=True),
                         "a1wl", 16)
                    p.ev(sy.dma_start(out=wph[:, TS(c, LAT)],
                                      in_=wphb[0:LAT, TS(c, 128)], transpose=True),
                         "a1wl", 16)
                    for hf in range(2):
                        p.ev(sy.dma_start(out=wg[:, c * D + hf * 512: c * D + (hf + 1) * 512],
                                          in_=wgb[hf * 512:(hf + 1) * 512, TS(c, 128)],
                                          transpose=True), "a1wl", 16)
            wl_ev = p.now("a1wl")
            xnf_ev = [None, None]
            xl_evs = [None, None]
            st_ev = [None, None]
            pev = [None]
            for ti in range(NT):
                bb = ti % 2

                @blk.sync
                def _(sy, ti=ti, bb=bb):
                    p.wt(sy, xnf_ev[bb])
                    for c in range(8):
                        ev = p.ev(sy.dma_start(
                            out=xnf[:, bb * 8 * 128 + c * 128: bb * 8 * 128 + (c + 1) * 128],
                            in_=xn[ti * 128:(ti + 1) * 128, TS(c, 128)], transpose=True),
                            f"a1xl{bb}", 16)
                    xl_evs[bb] = ev

                @blk.tensor
                def _(t, ti=ti, bb=bb):
                    p.wt(t, wl_ev, xl_evs[bb], pev[0])
                    for c in range(8):
                        t.matmul(pth[:, 0:LAT],
                                 xnf[:, bb * 8 * 128 + c * 128: bb * 8 * 128 + (c + 1) * 128],
                                 wth[:, TS(c, LAT)], start=(c == 0), stop=(c == 7))
                    for c in range(8):
                        t.matmul(pph[:, 0:LAT],
                                 xnf[:, bb * 8 * 128 + c * 128: bb * 8 * 128 + (c + 1) * 128],
                                 wph[:, TS(c, LAT)], start=(c == 0), stop=(c == 7))
                    for hf in range(2):
                        for c in range(8):
                            inst = t.matmul(
                                pg[:, TS(hf, 512)],
                                xnf[:, bb * 8 * 128 + c * 128: bb * 8 * 128 + (c + 1) * 128],
                                wg[:, c * D + hf * 512: c * D + (hf + 1) * 512],
                                start=(c == 0), stop=(c == 7))
                    xnf_ev[bb] = p.ev(inst, "a1mm", 1)

                @blk.scalar
                def _(a, ti=ti, bb=bb):
                    p.wt(a, p.now("a1mm"), st_ev[bb])
                    p.ev(a.activation(out=thb[:, TS(bb, LAT)], in_=pth[:, 0:LAT],
                                      func=AF.Copy), "a1e", 1)
                    p.ev(a.activation(out=phb[:, TS(bb, LAT)], in_=pph[:, 0:LAT],
                                      func=AF.Copy), "a1e", 1)
                    pev[0] = p.ev(a.activation(out=gb[:, TS(bb, D)], in_=pg[:, :],
                                               func=AF.Copy), "a1e", 1)
                    p.wt(a, pev[0])
                    p.ev(a.dma_start(out=th1_t[ti], in_=thb[:, TS(bb, LAT)]),
                         f"a1st{bb}", 16)
                    p.ev(a.dma_start(out=ph1_t[ti], in_=phb[:, TS(bb, LAT)]),
                         f"a1st{bb}", 16)
                    st_ev[bb] = p.ev(a.dma_start(out=g1_t[ti], in_=gb[:, TS(bb, D)]),
                                     f"a1st{bb}", 16)

        stkP.close()
        stkM = ExitStack()
        pM = [stkM.enter_context(nc.psum_tensor(f"a1pM{j}", [128, 512], F32))
              for j in range(4)]
        pout = stkM.enter_context(nc.psum_tensor("a1po", [128, D], F32))

        mev_ev = [None]
        for s in range(NSEQ):
            with nc.Block(f"a1M{s}") as blk:
                pl_ev = [p.now("a1Mm"), p.now("a1Mm")]
                ml_evs = [None, None]
                for i in range(TPS):
                    bb = i % 2
                    ti = s * TPS + i

                    @blk.sync
                    def _(sy, ti=ti, bb=bb, i=i):
                        if i == 0:
                            p.wt(sy, p.now("a1st0"), p.now("a1st1"))
                        p.wt(sy, pl_ev[bb])
                        p.ev(sy.dma_start(out=phl[:, TS(bb, LAT)], in_=ph1_t[ti]),
                             f"a1pl{bb}", 16)
                        ml_evs[bb] = p.ev(sy.dma_start(out=gl[:, TS(bb, D)], in_=g1_t[ti]),
                                          f"a1pl{bb}", 16)

                    @blk.tensor
                    def _(t, i=i, bb=bb, s=s):
                        p.wt(t, ml_evs[bb])
                        if i == 0 and s == 1:
                            p.wt(t, mev_ev[0])
                        for lc in range(2):
                            for oh in range(2):
                                inst = t.matmul(
                                    pM[2 * lc + oh][:, :],
                                    phl[:, bb * LAT + lc * 128: bb * LAT + (lc + 1) * 128],
                                    gl[:, bb * D + oh * 512: bb * D + (oh + 1) * 512],
                                    start=(i == 0), stop=(i == TPS - 1))
                        pl_ev[bb] = p.ev(inst, "a1Mm", 1)

                @blk.vector
                def _(v, s=s):
                    p.wt(v, p.now("a1Mm"))
                    if s == 1:
                        p.wt(v, p.now("a1om"))
                    for lc in range(2):
                        for oh in range(2):
                            e = p.ev(v.tensor_scalar(
                                out=msb[:, lc * D + oh * 512: lc * D + (oh + 1) * 512],
                                in0=pM[2 * lc + oh][:, :], scalar1=float(L1SCALE),
                                scalar2=None, op0=ALU.mult), "a1Mv", 1)
                    mev_ev[0] = e

            with nc.Block(f"a1O{s}") as blk:
                tf_ev = [p.now("a1om"), p.now("a1om")]
                x1_ev = [p.now("a1ov"), p.now("a1ov")]
                xs_ev = [p.now("a1os0"), p.now("a1os1")]
                tl_evs = [None, None]
                xl2_evs = [None, None]
                po_ev = p.now("a1ov")
                for i in range(TPS):
                    bb = i % 2
                    ti = s * TPS + i

                    @blk.sync
                    def _(sy, ti=ti, bb=bb, i=i):
                        if i == 0:
                            p.wt(sy, p.now("a1st0"), p.now("a1st1"))
                        p.wt(sy, tf_ev[bb])
                        for lc in range(2):
                            ev = p.ev(sy.dma_start(
                                out=thf[:, bb * LAT + lc * 128: bb * LAT + (lc + 1) * 128],
                                in_=th1[ti * 128:(ti + 1) * 128, TS(lc, 128)],
                                transpose=True), f"a1tl{bb}", 16)
                        tl_evs[bb] = ev

                    @blk.scalar
                    def _(a, ti=ti, bb=bb):
                        p.wt(a, x1_ev[bb])
                        xl2_evs[bb] = p.ev(a.dma_start(out=x1b[:, TS(bb, D)], in_=x1_t[ti]),
                                           f"a1xr{bb}", 16)

                    @blk.tensor
                    def _(t, i=i, bb=bb, s=s):
                        p.wt(t, tl_evs[bb], po_ev)
                        if i == 0:
                            p.wt(t, p.now("a1Mv"))
                        for half in range(2):
                            for lc in range(2):
                                inst = t.matmul(
                                    pout[:, TS(half, 512)],
                                    thf[:, bb * LAT + lc * 128: bb * LAT + (lc + 1) * 128],
                                    msb[:, lc * D + half * 512: lc * D + (half + 1) * 512],
                                    start=(lc == 0), stop=(lc == 1))
                        tf_ev[bb] = p.ev(inst, "a1om", 1)

                    @blk.vector
                    def _(v, i=i, bb=bb):
                        p.wt(v, p.now("a1om"), xl2_evs[bb], xs_ev[bb])
                        e = p.ev(v.tensor_tensor(out=xmb[:, TS(bb, D)], in0=pout[:, :],
                                                 in1=x1b[:, TS(bb, D)], op=ALU.add),
                                 "a1ov", 1)
                        x1_ev[bb] = e
                    po_ev = p.now("a1ov")

                    @blk.scalar
                    def _(a, ti=ti, bb=bb):
                        p.wt(a, p.now("a1ov"))
                        xs_ev[bb] = p.ev(a.dma_start(out=xm_t[ti], in_=xmb[:, TS(bb, D)]),
                                         f"a1os{bb}", 16)

        stkM.close()
        with nc.Block("a1F") as blk:
            p.fence(blk)


def _phase_builders():
    return {
        "cast0": ph_cast0,
        "ln0": lambda p: ph_ln(p, "xin", "xn0", pooled=False, lo_name="xnlo0"),
        "ln1": lambda p: ph_ln(p, "x1", "xn1", pooled=True),
        "attn0": ph_attn0,
        "ffn0": ph_ffn0,
        "attn1": ph_attn1,
        "ffn1": ph_ffn1,
    }


def build_launch(phases, before, needed_later):
    nc = bass.Bass("TRN2", target_bir_lowering=False)
    p = P(nc, phases, before, needed_later)
    builders = _phase_builders()
    for ph in phases:
        builders[ph](p)
    return nc, p


LAUNCHES = [["cast0", "ln0", "attn0", "ffn0", "ln1", "attn1", "ffn1"]]


def kernel(**inputs):
    x = np.asarray(inputs["L0_attn_x"])
    per_core = {"xin": [np.ascontiguousarray(x[NSEQ * c:NSEQ * (c + 1)].reshape(TC, D))
                        for c in range(NCORES)]}
    produced = set()
    out = None
    for li, phases in enumerate(LAUNCHES):
        later_reads = set()
        for phs in LAUNCHES[li + 1:]:
            for ph in phs:
                later_reads |= set(PHASE_IO[ph][0])
        writes, reads = set(), set()
        for ph in phases:
            reads |= set(PHASE_IO[ph][0])
            writes |= set(PHASE_IO[ph][1])
        needed_later = (writes & later_reads) | ({"out"} & writes)
        nc, p = build_launch(phases, produced, needed_later)
        in_maps = []
        for c in range(NCORES):
            m = {}
            for name in reads:
                if name == "xin" or name in SCRATCH:
                    if name in per_core:
                        m[name] = per_core[name][c]
                else:
                    m[name] = np.ascontiguousarray(np.asarray(inputs[name]))
            in_maps.append(m)
        res = run_bass_kernel_spmd(nc, in_maps, core_ids=list(range(NCORES)))
        for name in writes:
            if name == "out":
                out = [r["out"] for r in res.results]
            elif name in needed_later:
                per_core[name] = [r[name] for r in res.results]
        produced |= writes
    full = np.concatenate([o.reshape(NSEQ, SL, D) for o in out], axis=0)
    return np.ascontiguousarray(full.astype(np.float32))


# version 06
# speedup vs baseline: 2.1606x; 1.6311x over previous
import numpy as np
import ml_dtypes
from contextlib import ExitStack
import concourse.bass as bass
import concourse.mybir as mybir
from concourse.bass_utils import run_bass_kernel_spmd
from concourse.bass_core_rust import IndirectOffsetOnAxis

F32 = mybir.dt.float32
BF16 = mybir.dt.bfloat16
U32 = mybir.dt.uint32
AF = mybir.ActivationFunctionType
ALU = mybir.AluOpType
AX = mybir.AxisListType
TS = bass.ts

B, SL, D = 16, 4096, 1024
NCORES = 8
NSEQ = B // NCORES
HEADS, DH, HID = 8, 64, 512
NE, FFD = 8, 4096
LAT = 256
EPS = 1e-5
SCALE = DH ** -0.5
L1SCALE = 1.0 / float(np.sqrt(D))
BLK = 512
NBLK0 = 23


def _derive():
    global TC, NT, TPS, NBLK, NBLKC, SCRATCH, INPUT_SHAPES, OUT_SHAPE
    TC = NSEQ * SL
    NT = TC // 128
    TPS = SL // 128
    NBLK = SL // BLK
    NBLKC = TC // BLK
    SCRATCH = {
        "xn0":   ([TC, D], BF16),
        "xnlo0": ([TC, D], BF16),
        "wrlob": ([128, D], BF16),
        "xmid0": ([TC, D], F32),
        "x1":    ([TC, D], F32),
        "xn1":   ([TC, D], BF16),
        "xmid1": ([TC, D], F32),
        "q0":    ([TC, HID], BF16),
        "v0":    ([TC, HID], BF16),
        "k0":    ([NSEQ, HID, SL], F32),
        "ek0":   ([NSEQ, HID, SL], BF16),
        "w1b":   ([NE * FFD, D], BF16),
        "w2b":   ([NE * D, FFD], BF16),
        "wthb":  ([LAT, D], BF16),
        "wphb":  ([LAT, D], BF16),
        "wgb":   ([D, D], BF16),
        "wrb":   ([128, D], BF16),
        "pool1": ([NSEQ, D], F32),
        "zsc":   ([128, 4], F32),
        "ctxs":  ([2, 64, 4, DH], BF16),
        "th1":   ([TC, LAT], BF16),
        "ph1":   ([TC, LAT], BF16),
        "g1":    ([TC, D], BF16),
        "mrows": ([NSEQ * 2, NE], F32),
        "b2w":   ([NSEQ * 2, D], BF16),
        "w1bT":  ([NE * D, FFD], BF16),
        "w2bT":  ([NE * FFD, D], BF16),
        "xnperm": ([NBLK0 * BLK, D], BF16),
        "yperm": ([NBLK0 * BLK, D], F32),
        "b1s":   ([NBLK0, FFD], F32),
        "b2s":   ([NBLK0, D], F32),
        "sbnc":  ([16, 512], F32),
    }
    INPUT_SHAPES = {
        "xin": ([TC, D], F32),
        "L0_attn_Wq": ([D, HID], F32), "L0_attn_Wk": ([D, HID], F32),
        "L0_attn_Wv": ([D, HID], F32), "L0_attn_Wo": ([HID, D], F32),
        "L0_attn_bo": ([D], F32),
        "L0_ffn_Wr": ([NE, D], F32), "L0_ffn_br": ([NE], F32),
        "L0_ffn_W1": ([NE, FFD, D], F32), "L0_ffn_b1": ([NE, FFD], F32),
        "L0_ffn_W2": ([NE, D, FFD], F32), "L0_ffn_b2": ([NE, D], F32),
        "L1_attn_W_theta": ([LAT, D], F32), "L1_attn_W_phi": ([LAT, D], F32),
        "L1_attn_W_g": ([D, D], F32),
        "L1_ffn_Wr": ([D, NE], F32),
        "L1_ffn_W1": ([NE, D, FFD], F32), "L1_ffn_b1": ([NE, FFD], F32),
        "L1_ffn_W2": ([NE, FFD, D], F32), "L1_ffn_b2": ([NE, D], F32),
    }
    OUT_SHAPE = ([TC, D], F32)


def set_dims(nseq=2, sl=4096):
    global NSEQ, SL
    NSEQ, SL = nseq, sl
    _derive()


_derive()

PHASE_IO = {
    "cast0": (["L0_ffn_W1", "L0_ffn_W2", "L1_attn_W_theta", "L1_attn_W_phi",
               "L1_attn_W_g", "L0_ffn_Wr"],
              ["w1b", "w2b", "w1bT", "w2bT", "wthb", "wphb", "wgb", "wrb",
               "wrlob"]),
    "ln0":   (["xin"], ["xn0", "xnlo0"]),
    "attn0": (["xn0", "xin", "L0_attn_Wq", "L0_attn_Wk", "L0_attn_Wv",
               "L0_attn_Wo", "L0_attn_bo"],
              ["xmid0", "q0", "v0", "k0", "ek0"]),
    "ffn0":  (["xn0", "xnlo0", "xmid0", "w1bT", "w2bT", "wrb", "wrlob",
               "L0_ffn_br", "L0_ffn_b1", "L0_ffn_b2"], ["x1"]),
    "ln1":   (["x1"], ["xn1", "pool1"]),
    "attn1": (["xn1", "x1", "wthb", "wphb", "wgb"],
              ["th1", "ph1", "g1", "xmid1"]),
    "ffn1":  (["xn1", "xmid1", "pool1", "L1_ffn_Wr", "L1_ffn_W1", "L1_ffn_b1",
               "L1_ffn_W2", "L1_ffn_b2"], ["out"]),
}


class P:

    def __init__(self, nc, launch_phases, before, needed_later):
        self.nc = nc
        self.cnt = {}
        self._sems = {}
        self._slots = {}
        self._es = ExitStack()
        self.stack = None
        self.dram_cache = {}
        self.produced_here = set()
        for ph in launch_phases:
            self.produced_here |= set(PHASE_IO[ph][1])
        self.before = before
        self.needed_later = needed_later
        self.dma_ch = set()

    def dram(self, name):
        if name in self.dram_cache:
            return self.dram_cache[name]
        if name == "out":
            shape, dt = OUT_SHAPE
            kind = "ExternalOutput"
        elif name in INPUT_SHAPES:
            shape, dt = INPUT_SHAPES[name]
            kind = "ExternalInput"
        else:
            shape, dt = SCRATCH[name]
            if name in self.produced_here:
                kind = "ExternalOutput" if name in self.needed_later else "Internal"
            elif name in self.before:
                kind = "ExternalInput"
            else:
                kind = "Internal"
        ap = self.nc.dram_tensor(name, shape, dt, kind=kind).ap()
        self.dram_cache[name] = ap
        return ap

    SEMPOOL = 90

    def _slot(self, name):
        if name not in self._slots:
            self._slots[name] = len(self._slots) % self.SEMPOOL
        return self._slots[name]

    def sem(self, slot):
        if slot not in self._sems:
            self._sems[slot] = self._es.enter_context(self.nc.semaphore(f"s{slot}"))
            self.cnt[slot] = 0
        return self._sems[slot]

    def ev(self, inst, ch, n=1):
        slot = self._slot(ch)
        inst.then_inc(self.sem(slot), n)
        self.cnt[slot] += n
        if n >= 16:
            self.dma_ch.add(ch)
        return (slot, self.cnt[slot])

    def wt(self, eng, *evts):
        for e in evts:
            if e is None:
                continue
            if isinstance(e, list):
                self.wt(eng, *e)
                continue
            slot, v = e
            if v > 0:
                eng.wait_ge(self.sem(slot), v)

    def now(self, ch):
        slot = self._slot(ch)
        return (slot, self.cnt.get(slot, 0))

    def fence(self, blk):
        chans = sorted(self.dma_ch)

        @blk.sync
        def _(s):
            for ch in chans:
                self.wt(s, self.now(ch))

    def sb(self, name, shape, dtype):
        return self.stack.enter_context(self.nc.sbuf_tensor(name, shape, dtype))

    def ps(self, name, shape, dtype=F32):
        return self.stack.enter_context(self.nc.psum_tensor(name, shape, dtype))


def ph_cast0(p: P):
    nc = p.nc
    jobs = [("L0_ffn_W1", "w1b", NE * FFD, D), ("L0_ffn_W2", "w2b", NE * D, FFD),
            ("L1_attn_W_theta", "wthb", LAT, D), ("L1_attn_W_phi", "wphb", LAT, D),
            ("L1_attn_W_g", "wgb", D, D)]
    with ExitStack() as stk:
        p.stack = stk
        buf = p.sb("c0buf", [128, 2 * FFD], BF16)
        wr = p.sb("c0wr", [128, D], BF16)
        wrf = p.sb("c0wrf", [128, D], F32)
        wrh32 = p.sb("c0wrh", [128, D], F32)
        wrlo_sb = p.sb("c0wrl", [128, D], BF16)
        with nc.Block("cast0") as blk:
            seq = []
            for src, dst, rows, w in jobs:
                sap = p.dram(src)
                sflat = sap.rearrange(" ".join(f"a{i}" for i in range(sap.ndim)) +
                                      " -> (" + " ".join(f"a{i}" for i in range(sap.ndim))
                                      + ")").rearrange("(n p w) -> n p w", p=128, w=w)
                dflat = p.dram(dst).rearrange("a b -> (a b)").rearrange(
                    "(n p w) -> n p w", p=128, w=w)
                for i in range(sflat.shape[0]):
                    seq.append((sflat[i], dflat[i], w))
            st_ev = [None, None]
            ld_ev = [None, None]
            for i, (s_ap, d_ap, w) in enumerate(seq):
                b = i % 2

                @blk.gpsimd
                def _(g, b=b, s_ap=s_ap, w=w):
                    p.wt(g, st_ev[b])
                    ld_ev[b] = p.ev(g.dma_start(out=buf[:, TS(b, FFD)][:, 0:w],
                                                in_=s_ap), f"c0ld{b}", 16)

                @blk.scalar
                def _(s, b=b, d_ap=d_ap, w=w):
                    p.wt(s, ld_ev[b])
                    st_ev[b] = p.ev(s.dma_start(out=d_ap, in_=buf[:, TS(b, FFD)][:, 0:w]),
                                    f"c0st{b}", 16)

            @blk.gpsimd
            def _(g):
                mev = p.ev(g.memset(wrf[:, :], 0.0), "c0v", 1)
                p.wt(g, mev)
                ld_ev[0] = p.ev(g.dma_start(out=wrf[0:8, :], in_=p.dram("L0_ffn_Wr")),
                                "c0ldw", 16)

            @blk.scalar
            def _(s):
                p.wt(s, ld_ev[0])
                e1 = p.ev(s.activation(out=wr[:, :], in_=wrf[:, :], func=AF.Copy),
                          "c0we", 1)
                p.wt(s, e1)
                p.ev(s.activation(out=wrh32[:, :], in_=wr[:, :], func=AF.Copy),
                     "c0we", 1)

            @blk.vector
            def _(v):
                p.wt(v, p.now("c0we"))
                p.ev(v.tensor_tensor(out=wrlo_sb[:, :], in0=wrf[:, :], in1=wrh32[:, :],
                                     op=ALU.subtract), "c0wv", 1)

            @blk.scalar
            def _(s):
                p.wt(s, p.now("c0wv"))
                p.ev(s.dma_start(out=p.dram("wrb"), in_=wr[:, :]), "c0stw", 16)
                p.ev(s.dma_start(out=p.dram("wrlob"), in_=wrlo_sb[:, :]), "c0stw", 16)
            p.fence(blk)

        wt1 = p.sb("cTw", [128, 2 * 8 * FFD], BF16)
        with nc.Block("cast0T") as blk:
            w1bd, w2bd = p.dram("w1b"), p.dram("w2b")
            w1Td, w2Td = p.dram("w1bT"), p.dram("w2bT")
            tl_ev = [None, None]
            ts_ev = [None, None]
            for gi in range(2 * NE):
                e, which = gi // 2, gi % 2
                pp = gi % 2
                @blk.sync
                def _(sy, e=e, which=which, pp=pp):
                    p.wt(sy, ts_ev[pp])
                    base = pp * 8 * FFD
                    if which == 0:
                        for dc in range(8):
                            for fg in range(8):
                                ev = p.ev(sy.dma_start(
                                    out=wt1[:, base + dc * FFD + fg * 512:
                                            base + dc * FFD + (fg + 1) * 512],
                                    in_=w1bd[e * FFD + fg * 512:e * FFD + (fg + 1) * 512,
                                             TS(dc, 128)], transpose=True), f"cTl{pp}", 16)
                    else:
                        for fc in range(32):
                            for dh in range(2):
                                ev = p.ev(sy.dma_start(
                                    out=wt1[:, base + fc * 1024 + dh * 512:
                                            base + fc * 1024 + (dh + 1) * 512],
                                    in_=w2bd[e * D + dh * 512:e * D + (dh + 1) * 512,
                                             TS(fc, 128)], transpose=True), f"cTl{pp}", 16)
                    tl_ev[pp] = ev

                @blk.scalar
                def _(s, e=e, which=which, pp=pp):
                    p.wt(s, tl_ev[pp])
                    base = pp * 8 * FFD
                    if which == 0:
                        for dc in range(8):
                            ev = p.ev(s.dma_start(
                                out=w1Td[e * D + dc * 128:e * D + (dc + 1) * 128, :],
                                in_=wt1[:, base + dc * FFD: base + (dc + 1) * FFD]),
                                f"cTs{pp}", 16)
                    else:
                        for fc in range(32):
                            ev = p.ev(s.dma_start(
                                out=w2Td[e * FFD + fc * 128:e * FFD + (fc + 1) * 128, :],
                                in_=wt1[:, base + fc * D: base + (fc + 1) * D]),
                                f"cTs{pp}", 16)
                    ts_ev[pp] = ev
            p.fence(blk)


def ph_ln(p: P, x_name, xn_name, pooled=False, lo_name=None):
    nc = p.nc
    x_t = p.dram(x_name).rearrange("(n p) d -> n p d", p=128)
    xn_t = p.dram(xn_name).rearrange("(n p) d -> n p d", p=128)
    if lo_name is not None:
        xnlo_t = p.dram(lo_name).rearrange("(n p) d -> n p d", p=128)
    pre = "L" + xn_name
    LD, LV, LA, ST, PEc = pre + "ld", pre + "v", pre + "a", pre + "st", pre + "pe"
    LAc, LV2, ST2 = pre + "ac", pre + "v2", pre + "s2"
    with ExitStack() as stk:
        p.stack = stk
        xb = p.sb(pre + "x", [128, 2 * D], F32)
        ob = p.sb(pre + "o", [128, 2 * D], BF16)
        sq = p.sb(pre + "sq", [128, D], F32)
        st_ = p.sb(pre + "sv", [128, 8], F32)
        cst = p.sb(pre + "cst", [128, 2], F32)
        if lo_name is not None:
            ob32 = p.sb(pre + "o32", [128, 2 * D], F32)
            obf = p.sb(pre + "obf", [128, 2 * D], F32)
            lob = p.sb(pre + "lob", [128, 2 * D], BF16)
        if pooled:
            onec = p.sb(pre + "one", [128, 1], BF16)
            poolp = [p.ps(pre + f"pp{h}", [1, 512]) for h in range(2 * NSEQ)]
            poolr = p.sb(pre + "pr", [1, NSEQ * D], F32)
        cons_ev = [None, None]
        stor_ev = [None, None]
        pe_ev = [None, None]
        ld_evs = [None, None]
        ocp_ev = [None, None]
        losub_ev = [None, None]
        lost_ev = [None, None]
        with nc.Block(pre) as blk:
            @blk.gpsimd
            def _(g):
                p.ev(g.memset(cst[:, 0:1], float(EPS)), LA, 1)
                p.ev(g.memset(cst[:, 1:2], 1.0 / D), LA, 1)
                if pooled:
                    p.ev(g.memset(onec[:, :], 1.0), LV, 1)
            for i in range(NT):
                b = i % 2
                sl = slice(b * D, (b + 1) * D)

                @blk.sync
                def _(s, i=i, b=b, sl=sl):
                    p.wt(s, cons_ev[b], pe_ev[b])
                    ld_evs[b] = p.ev(s.dma_start(out=xb[:, sl], in_=x_t[i]), LD + str(b), 16)

                @blk.vector
                def _(v, i=i, b=b, sl=sl):
                    p.wt(v, ld_evs[b])
                    e1 = p.ev(v.reduce_sum(st_[:, b:b + 1], xb[:, sl], axis=AX.X), LV, 1)
                    p.wt(v, e1)
                    p.ev(v.tensor_scalar(out=st_[:, 2 + b:3 + b], in0=st_[:, b:b + 1],
                                         scalar1=-1.0 / D, scalar2=None, op0=ALU.mult),
                         LV, 1)

                @blk.scalar
                def _(a, i=i, b=b, sl=sl):
                    p.wt(a, p.now(LV), p.now(LA))
                    e1 = p.ev(a.activation(out=sq[:, :], in_=xb[:, sl], func=AF.Square,
                                           bias=st_[:, 2 + b:3 + b], accum_out=st_[:, 4:5]),
                              LA, 1)
                    p.wt(a, e1)
                    p.ev(a.activation(out=st_[:, 5:6], in_=st_[:, 4:5], func=AF.Sqrt,
                                      scale=cst[:, 1:2], bias=cst[:, 0:1]), LA, 1)

                if lo_name is None:
                    @blk.vector
                    def _(v, i=i, b=b, sl=sl):
                        p.wt(v, p.now(LA), p.now(LV))
                        e1 = p.ev(v.reciprocal(st_[:, 6:7], st_[:, 5:6]), LV, 1)
                        p.wt(v, e1, stor_ev[b])
                        cons_ev[b] = p.ev(v.tensor_scalar(out=ob[:, sl], in0=xb[:, sl],
                                                          scalar1=st_[:, 2 + b:3 + b],
                                                          scalar2=st_[:, 6:7],
                                                          op0=ALU.add, op1=ALU.mult), LV, 1)

                    @blk.scalar
                    def _(s, i=i, b=b, sl=sl):
                        p.wt(s, p.now(LV))
                        stor_ev[b] = p.ev(s.dma_start(out=xn_t[i], in_=ob[:, sl]),
                                          ST + str(b), 16)
                else:
                    @blk.vector
                    def _(v, i=i, b=b, sl=sl):
                        p.wt(v, p.now(LA), p.now(LV))
                        e1 = p.ev(v.reciprocal(st_[:, 6:7], st_[:, 5:6]), LV, 1)
                        p.wt(v, e1, ocp_ev[b], losub_ev[b])
                        cons_ev[b] = p.ev(v.tensor_scalar(out=ob32[:, sl], in0=xb[:, sl],
                                                          scalar1=st_[:, 2 + b:3 + b],
                                                          scalar2=st_[:, 6:7],
                                                          op0=ALU.add, op1=ALU.mult), LV, 1)

                    @blk.scalar
                    def _(s, i=i, b=b, sl=sl):
                        p.wt(s, cons_ev[b], stor_ev[b])
                        e1 = p.ev(s.activation(out=ob[:, sl], in_=ob32[:, sl],
                                               func=AF.Copy), LAc, 1)
                        p.wt(s, e1)
                        ocp_ev[b] = p.ev(s.activation(out=obf[:, sl], in_=ob[:, sl],
                                                      func=AF.Copy), LAc, 1)
                        stor_ev[b] = p.ev(s.dma_start(out=xn_t[i], in_=ob[:, sl]),
                                          ST + str(b), 16)

                    @blk.vector
                    def _(v, i=i, b=b, sl=sl):
                        p.wt(v, ocp_ev[b], lost_ev[b])
                        losub_ev[b] = p.ev(v.tensor_tensor(out=lob[:, sl],
                                                           in0=ob32[:, sl],
                                                           in1=obf[:, sl],
                                                           op=ALU.subtract), LV2, 1)

                    @blk.scalar
                    def _(s, i=i, b=b, sl=sl):
                        p.wt(s, losub_ev[b])
                        lost_ev[b] = p.ev(s.dma_start(out=xnlo_t[i], in_=lob[:, sl]),
                                          ST2 + str(b), 16)

                if pooled:
                    @blk.tensor
                    def _(t, i=i, b=b, sl=sl):
                        p.wt(t, p.now(LV))
                        sidx = i // TPS
                        first = (i % TPS) == 0
                        last = (i % TPS) == TPS - 1
                        t.matmul(poolp[2 * sidx][:, :], onec[:, :],
                                 ob[:, sl][:, 0:512], start=first, stop=last)
                        i2 = t.matmul(poolp[2 * sidx + 1][:, :], onec[:, :],
                                      ob[:, sl][:, 512:1024], start=first, stop=last)
                        pe_ev[b] = p.ev(i2, PEc, 1)

            if pooled:
                @blk.vector
                def _(v):
                    p.wt(v, p.now(PEc), p.now(LV))
                    for s in range(NSEQ):
                        for h in range(2):
                            p.ev(v.tensor_scalar(
                                out=poolr[:, s * D + h * 512: s * D + (h + 1) * 512],
                                in0=poolp[2 * s + h][:, :], scalar1=1.0 / SL,
                                scalar2=None, op0=ALU.mult), LV, 1)

                @blk.scalar
                def _(s):
                    p.wt(s, p.now(LV))
                    p.ev(s.dma_start(out=p.dram("pool1").rearrange("s d -> (s d)")[None, :],
                                     in_=poolr[:, :]), ST, 16)
            p.fence(blk)


def ph_attn0(p: P):
    nc = p.nc
    xn = p.dram("xn0")
    xin_t = p.dram("xin").rearrange("(n p) d -> n p d", p=128)
    xmid_t = p.dram("xmid0").rearrange("(n p) d -> n p d", p=128)
    q0 = p.dram("q0")
    q0_t = q0.rearrange("(n p) d -> n p d", p=128)
    v0_t = p.dram("v0").rearrange("(n p) d -> n p d", p=128)
    k0 = p.dram("k0")
    ek0 = p.dram("ek0")
    with ExitStack() as stk:
        p.stack = stk
        wq = p.sb("a0wq", [128, 8 * HID], BF16)
        wk = p.sb("a0wk", [128, 8 * HID], BF16)
        wv = p.sb("a0wv", [128, 8 * HID], BF16)
        wo = p.sb("a0wo", [128, 4 * D], BF16)
        bo = p.sb("a0bo", [1, D], BF16)
        ones1 = p.sb("a0one", [1, 128], BF16)
        xnf = p.sb("a0xnf", [128, 2 * 8 * BLK], BF16)
        kev = p.sb("a0kev", [128, 4 * BLK], F32)
        qf = p.sb("a0qf", [128, HID], F32)
        qe = p.sb("a0qe", [128, HID], F32)
        qs = p.sb("a0qs", [128, 2 * HID], BF16)
        vb = p.sb("a0vb", [128, 2 * HID], BF16)
        mq = p.sb("a0mq", [128, 3 * HEADS], F32)
        kst = p.sb("a0kst", [128, 32], F32)
        tm1 = p.sb("a0tm1", [128, 4], F32)
        ekb = p.sb("a0ekb", [128, 2 * BLK], BF16)
        zp = p.sb("a0zp", [128, 2], F32)
        ektm = p.sb("a0ektm", [128, 2 * HID], BF16)
        vtm = p.sb("a0vtm", [128, 2 * HID], BF16)
        ctxn = p.sb("a0ctxn", [64, HID], BF16)
        ctx2 = p.sb("a0ctx2", [128, 256], BF16)
        zlo = p.sb("a0zlo", [64, 8], F32)
        qfm = p.sb("a0qfm", [128, 2 * HID], BF16)
        af = p.sb("a0af", [128, HID], BF16)
        xmb = p.sb("a0xmb", [128, 2 * D], F32)
        xr = p.sb("a0xr", [128, 2 * D], F32)
        stkA = ExitStack()
        pk = [stkA.enter_context(nc.psum_tensor(f"a0pk{j}", [128, BLK], F32))
              for j in range(4)]
        pq = stkA.enter_context(nc.psum_tensor("a0pq", [128, BLK], F32))
        pv = stkA.enter_context(nc.psum_tensor("a0pv", [128, BLK], F32))

        with nc.Block("a0A") as blk:
            @blk.gpsimd
            def _(g):
                for name, dst, wdt, nch in [("L0_attn_Wq", wq, HID, 8),
                                            ("L0_attn_Wk", wk, HID, 8),
                                            ("L0_attn_Wv", wv, HID, 8)]:
                    ap = p.dram(name).rearrange("(c p) h -> c p h", p=128)
                    for c in range(nch):
                        p.ev(g.dma_start(out=dst[:, TS(c, wdt)], in_=ap[c]), "a0wl", 16)
                apo = p.dram("L0_attn_Wo").rearrange("(c p) d -> c p d", p=128)
                for c in range(4):
                    p.ev(g.dma_start(out=wo[:, TS(c, D)], in_=apo[c]), "a0wl", 16)
                p.ev(g.dma_start(out=bo[:, :], in_=p.dram("L0_attn_bo")[None, :]),
                     "a0wl", 16)
                p.ev(g.memset(ones1[:, :], 1.0), "a0ini", 1)
                for sq in range(NSEQ):
                    p.ev(g.memset(kst[:, 16 * sq:16 * sq + 4], -30000.0), "a0ini", 1)
                    p.ev(g.memset(kst[:, 16 * sq + 8:16 * sq + 12], 0.0), "a0ini", 1)
            wl_ev = [p.now("a0wl"), p.now("a0ini")]

            xnf_ev = [None, None]
            xl_ev = [None, None]
            kev_ev = [[], [], [], []]
            kps_ev = [None] * 4
            qs_ev = [None, None]
            vb_ev = [None, None]
            pq_ev = None
            pv_ev = None
            for s in range(NSEQ):
                for j in range(NBLK):
                    bb = (s * NBLK + j) % 2
                    t0 = s * SL + j * BLK

                    @blk.sync
                    def _(sy, bb=bb, t0=t0):
                        p.wt(sy, xnf_ev[bb])
                        for c in range(8):
                            ev = p.ev(sy.dma_start(
                                out=xnf[:, bb * 8 * BLK + c * BLK: bb * 8 * BLK + (c + 1) * BLK],
                                in_=xn[t0:t0 + BLK, TS(c, 128)], transpose=True),
                                f"a0xl{bb}", 16)
                        xl_ev[bb] = ev

                    @blk.tensor
                    def _(t, bb=bb):
                        p.wt(t, wl_ev, xl_ev[bb], *[e for e in kps_ev if e])
                        for kc in range(4):
                            for c in range(8):
                                inst = t.matmul(
                                    pk[kc][:, :],
                                    wk[:, c * HID + kc * 128: c * HID + (kc + 1) * 128],
                                    xnf[:, bb * 8 * BLK + c * BLK: bb * 8 * BLK + (c + 1) * BLK],
                                    start=(c == 0), stop=(c == 7))
                        p.ev(inst, "a0kmm", 1)

                    @blk.scalar
                    def _(a, s=s, j=j):
                        p.wt(a, p.now("a0kmm"))
                        for kc in range(4):
                            p.wt(a, kev_ev[kc])
                            e = p.ev(a.activation(out=kev[:, TS(kc, BLK)], in_=pk[kc][:, :],
                                                  func=AF.Copy), "a0ke", 1)
                            kps_ev[kc] = e
                            p.wt(a, e)
                            sev = p.ev(a.dma_start(
                                out=k0[s, kc * 128:(kc + 1) * 128, j * BLK:(j + 1) * BLK],
                                in_=kev[:, TS(kc, BLK)]), f"a0ks{kc}", 16)
                            kev_ev[kc] = [sev]

                    @blk.vector
                    def _(v, s=s):
                        p.wt(v, p.now("a0ke"), p.now("a0kv"))
                        for kc in range(4):
                            e = p.ev(v.reduce_max(tm1[:, kc:kc + 1], kev[:, TS(kc, BLK)],
                                                  axis=AX.X), "a0kv", 1)
                            kev_ev[kc].append(e)
                        p.wt(v, p.now("a0kv"))
                        p.ev(v.tensor_tensor(out=kst[:, 16 * s:16 * s + 4],
                                             in0=kst[:, 16 * s:16 * s + 4],
                                             in1=tm1[:, 0:4], op=ALU.max), "a0kv", 1)

                    for tt in range(4):
                        ti = (t0 + tt * 128) // 128
                        qb = (s * NBLK * 4 + j * 4 + tt) % 2

                        @blk.tensor
                        def _(t, bb=bb, tt=tt):
                            p.wt(t, pq_ev)
                            for c in range(8):
                                inst = t.matmul(
                                    pq[:, :],
                                    xnf[:, bb * 8 * BLK + c * BLK + tt * 128:
                                        bb * 8 * BLK + c * BLK + (tt + 1) * 128],
                                    wq[:, TS(c, HID)], start=(c == 0), stop=(c == 7))
                            p.ev(inst, "a0qmm", 1)

                        @blk.scalar
                        def _(a):
                            p.wt(a, p.now("a0qmm"), p.now("a0qv"))
                            p.ev(a.activation(out=qf[:, :], in_=pq[:, :][:, 0:HID],
                                              func=AF.Copy), "a0qe", 1)
                        pq_ev = p.now("a0qe")

                        @blk.vector
                        def _(v):
                            p.wt(v, p.now("a0qe"))
                            q3 = qf[:, :].rearrange("p (h d) -> p h d", d=DH)
                            e1 = p.ev(v.reduce_max(mq[:, 0:HEADS], q3, axis=AX.X), "a0qv", 1)
                            p.wt(v, e1)
                            mqb = mq[:, 0:HEADS].unsqueeze(2).broadcast_to([128, HEADS, DH])
                            p.ev(v.tensor_tensor(
                                out=qe[:, :].rearrange("p (h d) -> p h d", d=DH),
                                in0=q3, in1=mqb, op=ALU.subtract), "a0qv", 1)

                        @blk.scalar
                        def _(a):
                            p.wt(a, p.now("a0qv"))
                            p.ev(a.activation(out=qe[:, :], in_=qe[:, :], func=AF.Exp),
                                 "a0qe", 1)

                        @blk.vector
                        def _(v, qb=qb):
                            p.wt(v, p.now("a0qe"))
                            q3 = qe[:, :].rearrange("p (h d) -> p h d", d=DH)
                            e1 = p.ev(v.reduce_sum(mq[:, HEADS:2 * HEADS], q3, axis=AX.X),
                                      "a0qv", 1)
                            p.wt(v, e1)
                            e2 = p.ev(v.reciprocal(mq[:, 2 * HEADS:3 * HEADS],
                                                   mq[:, HEADS:2 * HEADS]), "a0qv", 1)
                            p.wt(v, e2, qs_ev[qb])
                            rqb = mq[:, 2 * HEADS:3 * HEADS].unsqueeze(2).broadcast_to(
                                [128, HEADS, DH])
                            p.ev(v.tensor_tensor(
                                out=qs[:, TS(qb, HID)].rearrange("p (h d) -> p h d", d=DH),
                                in0=q3, in1=rqb, op=ALU.mult), "a0qv", 1)

                        @blk.scalar
                        def _(a, ti=ti, qb=qb):
                            p.wt(a, p.now("a0qv"))
                            qs_ev[qb] = p.ev(a.dma_start(out=q0_t[ti],
                                                         in_=qs[:, TS(qb, HID)]),
                                             f"a0qss{qb}", 16)

                        @blk.tensor
                        def _(t, bb=bb, tt=tt):
                            p.wt(t, pv_ev)
                            for c in range(8):
                                inst = t.matmul(
                                    pv[:, :],
                                    xnf[:, bb * 8 * BLK + c * BLK + tt * 128:
                                        bb * 8 * BLK + c * BLK + (tt + 1) * 128],
                                    wv[:, TS(c, HID)], start=(c == 0), stop=(c == 7))
                            ev = p.ev(inst, "a0vmm", 1)
                            if tt == 3:
                                xnf_ev[bb] = ev

                        @blk.scalar
                        def _(a, ti=ti, qb=qb):
                            p.wt(a, p.now("a0vmm"), vb_ev[qb])
                            e = p.ev(a.activation(out=vb[:, TS(qb, HID)],
                                                  in_=pv[:, :][:, 0:HID], func=AF.Copy),
                                     "a0ve", 1)
                            p.wt(a, e)
                            vb_ev[qb] = p.ev(a.dma_start(out=v0_t[ti],
                                                         in_=vb[:, TS(qb, HID)]),
                                             f"a0vss{qb}", 16)
                        pv_ev = p.now("a0ve")

        stkA.close()
        stkB = ExitStack()
        pctx = stkB.enter_context(nc.psum_tensor("a0pctx", [64, HID], F32))
        patt = stkB.enter_context(nc.psum_tensor("a0patt", [128, BLK], F32))
        py = stkB.enter_context(nc.psum_tensor("a0py", [128, D], F32))

        ctx_ev = [None]
        for s in range(NSEQ):
            with nc.Block(f"a0B{s}") as blk:
                @blk.vector
                def _(v, s=s):
                    p.ev(v.tensor_scalar(out=kst[:, 16 * s + 4:16 * s + 8],
                                         in0=kst[:, 16 * s:16 * s + 4], scalar1=-1.0,
                                         scalar2=None, op0=ALU.mult), "b0v", 1)

                kl_ev = [None, None]
                exp_ev = [None, None]
                eks_ev = [p.now("b0es0"), p.now("b0es1")]
                zp_ev = [None, None]
                for kc in range(4):
                    for j in range(NBLK):
                        bb = (kc * NBLK + j) % 2

                        @blk.sync
                        def _(sy, s=s, kc=kc, j=j, bb=bb):
                            if s == 0 and kc == 0 and j == 0:
                                for kk in range(4):
                                    p.wt(sy, p.now(f"a0ks{kk}"))
                                p.wt(sy, p.now("a0vss0"), p.now("a0vss1"))
                            p.wt(sy, exp_ev[bb])
                            kl_ev[bb] = p.ev(sy.dma_start(
                                out=kev[:, TS(bb, BLK)],
                                in_=k0[s, kc * 128:(kc + 1) * 128, j * BLK:(j + 1) * BLK]),
                                f"b0kl{bb}", 16)

                        @blk.scalar
                        def _(a, s=s, kc=kc, j=j, bb=bb):
                            p.wt(a, kl_ev[bb], zp_ev[bb], eks_ev[bb], p.now("b0v"))
                            e = p.ev(a.activation(out=ekb[:, TS(bb, BLK)],
                                                  in_=kev[:, TS(bb, BLK)], func=AF.Exp,
                                                  bias=kst[:, 16 * s + 4 + kc:
                                                           16 * s + 5 + kc],
                                                  accum_out=zp[:, bb:bb + 1]), "b0e", 1)
                            exp_ev[bb] = e
                            p.wt(a, e)
                            eks_ev[bb] = p.ev(a.dma_start(
                                out=ek0[s, kc * 128:(kc + 1) * 128, j * BLK:(j + 1) * BLK],
                                in_=ekb[:, TS(bb, BLK)]), f"b0es{bb}", 16)

                        @blk.vector
                        def _(v, s=s, kc=kc, bb=bb):
                            p.wt(v, p.now("b0e"), p.now("b0v"))
                            zp_ev[bb] = p.ev(v.tensor_tensor(
                                out=kst[:, 16 * s + 8 + kc:16 * s + 9 + kc],
                                in0=kst[:, 16 * s + 8 + kc:16 * s + 9 + kc],
                                in1=zp[:, bb:bb + 1], op=ALU.add), "b0v", 1)

                @blk.vector
                def _(v, s=s):
                    p.wt(v, p.now("b0v"))
                    p.ev(v.reciprocal(kst[:, 16 * s + 12:16 * s + 16],
                                      kst[:, 16 * s + 8:16 * s + 12]), "b0v", 1)

                @blk.scalar
                def _(a, s=s):
                    p.wt(a, p.now("b0v"))
                    if s == 1:
                        p.wt(a, p.now("b0zl0"), p.now("b0cv"))
                    e = p.ev(a.dma_start(out=p.dram("zsc"),
                                         in_=kst[:, 16 * s + 12:16 * s + 16]),
                             f"b0zs{s}", 16)
                    p.wt(a, e)
                    p.ev(a.dma_start(
                        out=zlo[:, :].rearrange("d (hp kc) -> d hp kc", hp=2),
                        in_=p.dram("zsc").rearrange("(hp d) kc -> d hp kc", hp=2)),
                        f"b0zl{s}", 16)

                et_ev = [p.now("b0mm"), p.now("b0mm")]
                vt_ev = [p.now("b0mm"), p.now("b0mm")]
                el_evs = [None, None]
                vl_evs = [None, None]
                for i in range(TPS):
                    bb = i % 2

                    @blk.sync
                    def _(sy, s=s, i=i, bb=bb):
                        if i == 0:
                            p.wt(sy, p.now("b0es0"), p.now("b0es1"))
                        p.wt(sy, et_ev[bb])
                        el_evs[bb] = p.ev(sy.dma_start(
                            out=ektm[:, TS(bb, HID)],
                            in_=ek0[s, 0:HID, i * 128:(i + 1) * 128],
                            transpose=True), f"b0el{bb}", 16)

                    @blk.scalar
                    def _(a, s=s, i=i, bb=bb):
                        p.wt(a, vt_ev[bb])
                        vl_evs[bb] = p.ev(a.dma_start(out=vtm[:, TS(bb, HID)],
                                                      in_=v0_t[(s * SL) // 128 + i]),
                                          f"b0vl{bb}", 16)

                    @blk.tensor
                    def _(t, s=s, i=i, bb=bb):
                        p.wt(t, el_evs[bb], vl_evs[bb])
                        if i == 0:
                            p.wt(t, ctx_ev[0])
                        for h in range(HEADS):
                            inst = t.matmul(
                                pctx[:, TS(h, DH)],
                                ektm[:, bb * HID + h * DH: bb * HID + (h + 1) * DH],
                                vtm[:, bb * HID + h * DH: bb * HID + (h + 1) * DH],
                                start=(i == 0 and h == 0),
                                stop=(i == TPS - 1 and h == HEADS - 1))
                        e = p.ev(inst, "b0mm", 1)
                        et_ev[bb] = e
                        vt_ev[bb] = e

                @blk.vector
                def _(v, s=s):
                    p.wt(v, p.now("b0mm"), p.now("b0v"), p.now(f"b0zl{s}"))
                    for h in range(HEADS):
                        zc = zlo[:, (h % 2) * 4 + h // 2: (h % 2) * 4 + h // 2 + 1]
                        e = p.ev(v.tensor_scalar(out=ctxn[:, TS(h, DH)],
                                                 in0=pctx[:, TS(h, DH)],
                                                 scalar1=zc,
                                                 scalar2=float(SCALE), op0=ALU.mult,
                                                 op1=ALU.mult), "b0cv", 1)
                    ctx_ev[0] = e

                @blk.scalar
                def _(a, s=s):
                    p.wt(a, p.now("b0cv"))
                    if s == 1:
                        p.wt(a, p.now("c0am"))
                    e = p.ev(a.dma_start(
                        out=p.dram("ctxs").rearrange("hp d hc e -> d hc hp e"),
                        in_=ctxn[:, :].rearrange("p (hc hp e) -> p hc hp e",
                                                 hp=2, e=DH)),
                        f"b0cs{s}", 16)
                    p.wt(a, e)
                    p.ev(a.dma_start(
                        out=ctx2[:, :].rearrange("p (hc e) -> p hc e", e=DH),
                        in_=p.dram("ctxs").rearrange("hp d hc e -> (hp d) hc e")),
                        f"b0c2{s}", 16)

            with nc.Block(f"a0C{s}") as blk:
                qf_ev = [p.now("c0am"), p.now("c0am")]
                xr_ev = [p.now("c0v"), p.now("c0v")]
                af_hold = [p.now("c0ym")]
                xm_ev = [p.now("c0xs0"), p.now("c0xs1")]
                ql_evs = [None, None]
                xl_evs = [None, None]
                patt_ev = p.now("c0ae")
                py_ev = p.now("c0v")
                for i in range(TPS):
                    bb = i % 2
                    ti = (s * SL) // 128 + i

                    @blk.sync
                    def _(sy, s=s, i=i, bb=bb, ti=ti):
                        if i == 0 and s == 0:
                            p.wt(sy, p.now("a0qss0"), p.now("a0qss1"))
                        p.wt(sy, qf_ev[bb])
                        for c in range(4):
                            e = p.ev(sy.dma_start(
                                out=qfm[:, bb * HID + c * 128: bb * HID + (c + 1) * 128],
                                in_=q0[ti * 128:(ti + 1) * 128, TS(c, 128)],
                                transpose=True), f"c0ql{bb}", 16)
                        ql_evs[bb] = e

                    @blk.scalar
                    def _(a, i=i, bb=bb, ti=ti):
                        p.wt(a, xr_ev[bb])
                        xl_evs[bb] = p.ev(a.dma_start(out=xr[:, TS(bb, D)], in_=xin_t[ti]),
                                          f"c0xl{bb}", 16)

                    @blk.tensor
                    def _(t, i=i, bb=bb, s=s):
                        p.wt(t, ql_evs[bb], patt_ev)
                        if i == 0:
                            p.wt(t, p.now(f"b0c2{s}"))
                        for c in range(4):
                            for hh in range(2):
                                pb = hh * 64
                                inst = t.matmul(
                                    patt[pb:pb + 64, TS(c, 128)],
                                    ctx2[pb:pb + 64, TS(c, DH)],
                                    qfm[:, bb * HID + c * 128:
                                        bb * HID + (c + 1) * 128][pb:pb + 64, :],
                                    start=(c == 0), stop=(c == 3),
                                    skip_group_check=True)
                        qf_ev[bb] = p.ev(inst, "c0am", 1)

                    @blk.scalar
                    def _(a, i=i, bb=bb):
                        p.wt(a, p.now("c0am"), af_hold[0])
                        p.ev(a.activation(out=af[:, :], in_=patt[:, :],
                                          func=AF.Copy), "c0ae", 1)
                    patt_ev = p.now("c0ae")

                    @blk.tensor
                    def _(t, i=i, bb=bb):
                        p.wt(t, p.now("c0ae"), py_ev)
                        for half in range(2):
                            for c in range(4):
                                t.matmul(py[:, TS(half, 512)],
                                         af[:, TS(c, 128)],
                                         wo[:, c * D + half * 512: c * D + (half + 1) * 512],
                                         start=(c == 0), stop=False)
                            inst = t.matmul(py[:, TS(half, 512)], ones1[:, :],
                                            bo[:, TS(half, 512)], start=False, stop=True)
                        af_hold[0] = p.ev(inst, "c0ym", 1)

                    @blk.vector
                    def _(v, i=i, bb=bb):
                        p.wt(v, p.now("c0ym"), xl_evs[bb], xm_ev[bb])
                        e = p.ev(v.tensor_tensor(out=xmb[:, TS(bb, D)], in0=py[:, :],
                                                 in1=xr[:, TS(bb, D)], op=ALU.add), "c0v", 1)
                        xr_ev[bb] = e
                    py_ev = p.now("c0v")

                    @blk.scalar
                    def _(a, i=i, bb=bb, ti=ti):
                        p.wt(a, p.now("c0v"))
                        xm_ev[bb] = p.ev(a.dma_start(out=xmid_t[ti], in_=xmb[:, TS(bb, D)]),
                                         f"c0xs{bb}", 16)

        stkB.close()
        with nc.Block("a0F") as blk:
            p.fence(blk)


def _ph_ffn(p: P, layer):
    nc = p.nc
    if layer == 0:
        xn, base, outs, pre = p.dram("xn0"), p.dram("xmid0"), p.dram("x1"), "f0"
    else:
        xn, base, outs, pre = p.dram("xn1"), p.dram("xmid1"), p.dram("out"), "f1"
    outs_t = outs.rearrange("(n p) d -> n p d", p=128)
    with ExitStack() as stk:
        p.stack = stk
        w1t = p.sb(pre + "w1", [128, 8 * FFD], BF16)
        w2t = p.sb(pre + "w2", [128, 8 * FFD], BF16)
        xnf = p.sb(pre + "xnf", [128, 8 * BLK], BF16)
        hb = p.sb(pre + "hb", [128, 32 * BLK], BF16)
        yst = p.sb(pre + "yst", [128, 4 * D], F32)
        ones1 = p.sb(pre + "o1", [1, 128], BF16)
        if layer == 0:
            ones5 = p.sb(pre + "o5", [1, BLK], BF16)
            b1e = p.sb(pre + "b1", [1, FFD], BF16)
            b2e = p.sb(pre + "b2", [1, D], BF16)
        ph_ = [p.ps(pre + "ph0", [128, BLK]), p.ps(pre + "ph1", [128, BLK])]
        py_ = [p.ps(pre + "py0", [128, BLK]), p.ps(pre + "py1", [128, BLK])]

        with nc.Block(pre + "R") as blk:
            @blk.gpsimd
            def _(g):
                p.ev(g.memset(ones1[:, :], 1.0), pre + "g", 1)
                if layer == 0:
                    p.ev(g.memset(ones5[:, :], 1.0), pre + "g", 1)
                else:
                    p.ev(g.dma_start(out=outs, in_=base), pre + "bc", 16)

            if layer == 0:
                rxn = p.sb(pre + "rxn", [128, 8 * 128], BF16)
                rxnlo = p.sb(pre + "rxl", [128, 8 * 128], BF16)
                wrt = p.sb(pre + "wrt", [128, 8 * 128], BF16)
                wrtlo = p.sb(pre + "wrtl", [128, 8 * 128], BF16)
                brr = p.sb(pre + "brr", [1, 8], BF16)
                brf = p.sb(pre + "brf", [1, 8], F32)
                brh = p.sb(pre + "brh", [1, 8], F32)
                brlo = p.sb(pre + "brlo", [1, 8], BF16)
                lgt = p.sb(pre + "lgt", [128, 8], F32)
                lmx = p.sb(pre + "lmx", [128, 1], F32)
                mcols = p.sb(pre + "mc", [128, NT * 8], F32)
                plg = p.ps(pre + "plg", [128, 512])
                wrbd = p.dram("wrb")
                wrlod = p.dram("wrlob")
                xnlo = p.dram("xnlo0")

                @blk.sync
                def _(sy):
                    for c in range(8):
                        p.ev(sy.dma_start(out=wrt[:, TS(c, 128)],
                                          in_=wrbd[:, TS(c, 128)], transpose=True),
                             pre + "wl", 16)
                        p.ev(sy.dma_start(out=wrtlo[:, TS(c, 128)],
                                          in_=wrlod[:, TS(c, 128)], transpose=True),
                             pre + "wl", 16)

                @blk.gpsimd
                def _(g):
                    p.ev(g.dma_start(out=brr[:, :], in_=p.dram("L0_ffn_br")[None, :]),
                         pre + "wlb", 16)
                    p.ev(g.dma_start(out=brf[:, :], in_=p.dram("L0_ffn_br")[None, :]),
                         pre + "wlb", 16)

                @blk.scalar
                def _(a):
                    p.wt(a, p.now(pre + "wlb"))
                    p.ev(a.activation(out=brh[:, :], in_=brr[:, :], func=AF.Copy),
                         pre + "ba", 1)

                @blk.vector
                def _(v):
                    p.wt(v, p.now(pre + "ba"))
                    p.ev(v.tensor_tensor(out=brlo[:, :], in0=brf[:, :], in1=brh[:, :],
                                         op=ALU.subtract), pre + "bv", 1)
                rt_wl = [p.now(pre + "wl"), p.now(pre + "wlb"), p.now(pre + "bv")]

                xe_ev = [None, None]
                rxl_evs = [None, None]
                lg_ev = None
                for ti in range(NT):
                    bb = ti % 2

                    @blk.sync
                    def _(sy, ti=ti):
                        p.wt(sy, xe_ev[0])
                        for c in range(8):
                            p.ev(sy.dma_start(
                                out=rxn[:, TS(c, 128)],
                                in_=xn[ti * 128:(ti + 1) * 128, TS(c, 128)],
                                transpose=True), pre + "rxl", 16)
                            e = p.ev(sy.dma_start(
                                out=rxnlo[:, TS(c, 128)],
                                in_=xnlo[ti * 128:(ti + 1) * 128, TS(c, 128)],
                                transpose=True), pre + "rxl", 16)
                        rxl_evs[0] = e

                    @blk.tensor
                    def _(t, ti=ti):
                        p.wt(t, rt_wl, rxl_evs[0], lg_ev)
                        for c in range(8):
                            t.matmul(plg[:, 0:8],
                                     rxn[:, TS(c, 128)],
                                     wrt[:, c * 128:c * 128 + 8],
                                     start=(c == 0), stop=False)
                        for c in range(8):
                            t.matmul(plg[:, 0:8],
                                     rxnlo[:, TS(c, 128)],
                                     wrt[:, c * 128:c * 128 + 8],
                                     start=False, stop=False)
                        for c in range(8):
                            t.matmul(plg[:, 0:8],
                                     rxn[:, TS(c, 128)],
                                     wrtlo[:, c * 128:c * 128 + 8],
                                     start=False, stop=False)
                        t.matmul(plg[:, 0:8], ones1[:, :], brr[:, :],
                                 start=False, stop=False)
                        inst = t.matmul(plg[:, 0:8], ones1[:, :], brlo[:, :],
                                        start=False, stop=True)
                        xe_ev[0] = p.ev(inst, pre + "rmm", 1)

                    @blk.scalar
                    def _(a, ti=ti):
                        p.wt(a, p.now(pre + "rmm"), p.now(pre + "rv"))
                        p.ev(a.activation(out=lgt[:, :], in_=plg[:, 0:8], func=AF.Copy),
                             pre + "re", 1)
                    lg_ev = p.now(pre + "re")

                    @blk.vector
                    def _(v, ti=ti):
                        p.wt(v, p.now(pre + "re"))
                        e = p.ev(v.reduce_max(lmx[:, :], lgt[:, :], axis=AX.X), pre + "rv", 1)
                        p.wt(v, e)
                        p.ev(v.tensor_scalar(out=mcols[:, ti * 8:(ti + 1) * 8],
                                             in0=lgt[:, :], scalar1=lmx[:, 0:1],
                                             scalar2=None, op0=ALU.is_ge), pre + "rv", 1)

                Lt = yst[:, 0:128]
                rowA = yst[0:1, 128:640]
                cnt64 = yst[0:64, 640:648]
                pref64 = yst[0:64, 648:656]
                slotsel = yst[:, 656:720]
                kcol = yst[:, 720:721]
                onec = yst[:, 721:722]
                onesf1 = yst[0:1, 722:850]
                trow = yst[0:1, 850:858]
                tmp8 = yst[0:1, 858:866]
                prowF = yst[0:1, 866:874]
                bbrow = yst[0:1, 874:882]
                brow = yst[0:1, 882:890]
                pcol8 = yst[0:8, 890:891]
                bcol8 = yst[0:8, 891:892]
                biota = yst[0:1, 892:892 + NBLK0]
                bacc = yst[0:1, 915:915 + NBLK0]
                eball = yst[:, 938:938 + NBLK0]
                eb1k = yst[:, 961:961 + NBLK0]
                eb4k = yst[:, 984:984 + NBLK0]
                eb32 = yst[:, 1007:1007 + NBLK0]
                eb8 = yst[:, 1030:1030 + NBLK0]
                iotp = yst[:, 1053:1054]
                idxt0 = yst[:, 1054:1055]
                prowU = p.sb(pre + "pU", [1, 8], U32)
                slotu = p.sb(pre + "slU", [128, NT], U32)
                idxw1b = p.sb(pre + "ix1", [128, NBLK0], U32)
                idxw2b = p.sb(pre + "ix2", [128, NBLK0], U32)
                idxb1 = p.sb(pre + "ib1", [128, NBLK0], U32)
                idxb2 = p.sb(pre + "ib2", [128, NBLK0], U32)
                sbnc = p.dram("sbnc")
                stkR = ExitStack()
                prnk = stkR.enter_context(nc.psum_tensor(pre + "prk", [128, 512], F32))
                pmsc = stkR.enter_context(nc.psum_tensor(pre + "pms", [128, 512], F32))

                @blk.gpsimd
                def _(g):
                    p.ev(g.iota(Lt, pattern=[[1, 128]], base=0,
                                channel_multiplier=0,
                                allow_small_or_imprecise_dtypes=True), pre + "dg", 1)
                    p.ev(g.iota(kcol, pattern=[[1, 1]], base=1,
                                channel_multiplier=1,
                                allow_small_or_imprecise_dtypes=True), pre + "dg", 1)
                    p.ev(g.iota(biota, pattern=[[1, NBLK0]], base=0,
                                channel_multiplier=0,
                                allow_small_or_imprecise_dtypes=True), pre + "dg", 1)
                    p.ev(g.iota(iotp, pattern=[[1, 1]], base=0,
                                channel_multiplier=1,
                                allow_small_or_imprecise_dtypes=True), pre + "dg", 1)
                    p.ev(g.memset(onec, 1.0), pre + "dg", 1)
                    p.ev(g.memset(onesf1, 1.0), pre + "dg", 1)
                    p.ev(g.memset(bacc, -1.0), pre + "dg", 1)

                @blk.vector
                def _(v):
                    p.wt(v, p.now(pre + "dg"))
                    p.ev(v.tensor_scalar(out=Lt, in0=Lt, scalar1=kcol,
                                         scalar2=None, op0=ALU.is_ge), pre + "dv", 1)

                @blk.tensor
                def _(t):
                    p.wt(t, p.now(pre + "rv"), p.now(pre + "dg"))
                    p.ev(t.matmul(pmsc[0:1, 0:512], onec[:, :], mcols[:, :],
                                  start=True, stop=True), pre + "dm", 1)

                @blk.scalar
                def _(a):
                    p.wt(a, p.now(pre + "dm"))
                    e = p.ev(a.activation(out=rowA[:, :], in_=pmsc[0:1, 0:512],
                                          func=AF.Copy), pre + "de", 1)
                    p.wt(a, e)
                    e = p.ev(a.dma_start(out=sbnc[0][None, :], in_=rowA[:, :]),
                             pre + "db", 16)
                    p.wt(a, e)
                    p.ev(a.dma_start(out=cnt64[:, :],
                                     in_=sbnc[0].rearrange("(t e) -> t e", e=8)),
                         pre + "db", 16)

                @blk.tensor
                def _(t):
                    p.wt(t, p.now(pre + "db"))
                    p.ev(t.matmul(pmsc[0:1, 0:8], onec[0:64, 0:1], cnt64[:, :],
                                  start=True, stop=True), pre + "dm", 1)

                @blk.scalar
                def _(a):
                    p.wt(a, p.now(pre + "dm"))
                    p.ev(a.activation(out=trow[:, :], in_=pmsc[0:1, 0:8],
                                      func=AF.Copy), pre + "de", 1)

                @blk.vector
                def _(v):
                    p.wt(v, p.now(pre + "de"))
                    e = p.ev(v.tensor_scalar(out=tmp8[:, :], in0=trow[:, :],
                                             scalar1=511.0, scalar2=None, op0=ALU.add),
                             pre + "dv", 1)
                    p.wt(v, e)
                    e = p.ev(v.tensor_scalar(out=tmp8[:, :], in0=tmp8[:, :],
                                             scalar1=1.0 / 512.0, scalar2=-0.499,
                                             op0=ALU.mult, op1=ALU.add), pre + "dv", 1)
                    p.wt(v, e)
                    e = p.ev(v.tensor_copy(out=prowU[:, :], in_=tmp8[:, :]), pre + "dv", 1)
                    p.wt(v, e)
                    e = p.ev(v.tensor_copy(out=prowF[:, :], in_=prowU[:, :]), pre + "dv", 1)

                @blk.scalar
                def _(a):
                    p.wt(a, p.now(pre + "dv"))
                    e = p.ev(a.dma_start(out=sbnc[2][None, 0:8], in_=prowF[:, :]),
                             pre + "db", 16)
                    p.wt(a, e)
                    p.ev(a.dma_start(out=pcol8[:, :], in_=sbnc[2][0:8].unsqueeze(1)),
                         pre + "db", 16)

                @blk.tensor
                def _(t):
                    p.wt(t, p.now(pre + "db"), p.now(pre + "dv"))
                    p.ev(t.matmul(pmsc[0:8, 0:1], Lt[0:8, 0:8], pcol8[:, :],
                                  start=True, stop=True), pre + "dm", 1)

                @blk.scalar
                def _(a):
                    p.wt(a, p.now(pre + "dm"))
                    e = p.ev(a.activation(out=bcol8[:, :], in_=pmsc[0:8, 0:1],
                                          func=AF.Copy), pre + "de", 1)
                    p.wt(a, e)
                    e = p.ev(a.dma_start(out=sbnc[3][0:8].unsqueeze(1), in_=bcol8[:, :]),
                             pre + "db", 16)
                    p.wt(a, e)
                    p.ev(a.dma_start(out=bbrow[:, :], in_=sbnc[3][None, 0:8]),
                         pre + "db", 16)

                @blk.vector
                def _(v):
                    p.wt(v, p.now(pre + "db"))
                    p.ev(v.tensor_scalar(out=brow[:, :], in0=bbrow[:, :], scalar1=512.0,
                                         scalar2=None, op0=ALU.mult), pre + "dv", 1)

                @blk.tensor
                def _(t):
                    p.wt(t, p.now(pre + "dv"))
                    t.matmul(pmsc[0:64, 0:8], Lt[0:64, 0:64], cnt64[:, :],
                             start=True, stop=False)
                    p.ev(t.matmul(pmsc[0:64, 0:8], onesf1[0:1, 0:64], brow[:, :],
                                  start=False, stop=True), pre + "dm", 1)

                @blk.scalar
                def _(a):
                    p.wt(a, p.now(pre + "dm"))
                    e = p.ev(a.activation(out=pref64[:, :], in_=pmsc[0:64, 0:8],
                                          func=AF.Copy), pre + "de", 1)
                    p.wt(a, e)
                    e = p.ev(a.dma_start(out=sbnc[1].rearrange("(t e) -> t e", e=8),
                                         in_=pref64[:, :]), pre + "db", 16)
                    p.wt(a, e)
                    p.ev(a.dma_start(out=rowA[:, :], in_=sbnc[1][None, :]), pre + "db", 16)

                @blk.tensor
                def _(t):
                    p.wt(t, p.now(pre + "db"))
                    t.matmul(prnk[:, :], Lt[:, :], mcols[:, :], start=True, stop=False)
                    p.ev(t.matmul(prnk[:, :], onesf1[:, :], rowA[:, :],
                                  start=False, stop=True), pre + "dm", 1)

                @blk.vector
                def _(v):
                    p.wt(v, p.now(pre + "dm"))
                    e = p.ev(v.tensor_tensor(out=mcols[:, :], in0=mcols[:, :],
                                             in1=prnk[:, :], op=ALU.mult), pre + "dv", 1)
                    p.wt(v, e)
                    e = p.ev(v.reduce_sum(slotsel[:, :],
                                          mcols[:, :].rearrange("p (t e) -> p t e", e=8),
                                          axis=AX.X), pre + "dv", 1)
                    p.wt(v, e)
                    p.ev(v.tensor_copy(out=slotu[:, :], in_=slotsel[:, :]), pre + "dv", 1)

                @blk.vector
                def _(v):
                    p.wt(v, p.now(pre + "db"), p.now(pre + "dg"))
                    for e_ in range(NE):
                        ev = p.ev(v.scalar_tensor_tensor(
                            out=bacc[:, :], in0=biota[:, :],
                            scalar=bbrow[:, e_:e_ + 1], in1=bacc[:, :],
                            op0=ALU.is_ge, op1=ALU.add), pre + "dv", 1)
                        p.wt(v, ev)

                @blk.tensor
                def _(t):
                    p.wt(t, p.now(pre + "dv"))
                    p.ev(t.matmul(pmsc[:, 0:NBLK0], onesf1[:, :], bacc[:, :],
                                  start=True, stop=True), pre + "dm", 1)

                @blk.scalar
                def _(a):
                    p.wt(a, p.now(pre + "dm"))
                    p.ev(a.activation(out=eball[:, :], in_=pmsc[:, 0:NBLK0],
                                      func=AF.Copy), pre + "de", 1)

                @blk.vector
                def _(v):
                    p.wt(v, p.now(pre + "de"))
                    for dst, mul in [(eb1k, 1024.0), (eb4k, 4096.0),
                                     (eb32, 32.0), (eb8, 8.0)]:
                        ev = p.ev(v.tensor_scalar(out=dst, in0=eball,
                                                  scalar1=mul, scalar2=None,
                                                  op0=ALU.mult), pre + "dv", 1)
                    p.wt(v, ev)
                    for b in range(NBLK0):
                        for dst, ebt, clamp in [
                                (idxw1b, eb1k, None), (idxw2b, eb4k, None),
                                (idxb1, eb32, 255.0), (idxb2, eb8, 63.0)]:
                            if clamp is None:
                                ev = p.ev(v.tensor_scalar(
                                    out=idxt0, in0=iotp, scalar1=ebt[:, b:b + 1],
                                    scalar2=None, op0=ALU.add), pre + "dv", 1)
                            else:
                                ev = p.ev(v.tensor_scalar(
                                    out=idxt0, in0=iotp, scalar1=ebt[:, b:b + 1],
                                    scalar2=clamp, op0=ALU.add, op1=ALU.min),
                                    pre + "dv", 1)
                            p.wt(v, ev)
                            ev = p.ev(v.tensor_copy(out=dst[:, b:b + 1], in_=idxt0),
                                      pre + "dv", 1)
                            p.wt(v, ev)

            else:
                wr1 = p.sb(pre + "wr1", [128, 8 * 8], F32)
                pcol = p.sb(pre + "pc", [128, 8], F32)
                lgt = p.sb(pre + "lg", [1, 32], F32)
                onesf = p.sb(pre + "of", [1, 128], F32)
                siota = p.sb(pre + "si", [1, 8], F32)
                seltmp = p.sb(pre + "stm", [1, 8], F32)
                selr = p.sb(pre + "slr", [1, NSEQ * 4], F32)
                sel = p.sb(pre + "sel", [128, 8], F32)
                maskT = p.sb(pre + "mkT", [8, NSEQ * 2], BF16)
                maskTf = p.sb(pre + "mkF", [8, NSEQ * 2], F32)
                mones = p.sb(pre + "mo", [8, NSEQ * 2 * 512], BF16)
                b1all = p.sb(pre + "b1a", [8, FFD], BF16)
                b2all = p.sb(pre + "b2a", [8, D], BF16)
                b2row = p.sb(pre + "b2r", [1, 2 * D], BF16)
                b2st = p.sb(pre + "b2t", [1, 512], BF16)
                iotW1 = p.sb(pre + "io1", [128, 16], F32)
                iotW2 = p.sb(pre + "io2", [128, 32], F32)
                ebase = p.sb(pre + "eb", [128, 2], F32)
                idxt = p.sb(pre + "ixt", [128, 32], F32)
                idxw1a = p.sb(pre + "ix1", [128, NSEQ * 2 * 16], U32)
                idxw2a = p.sb(pre + "ix2", [128, NSEQ * 2 * 32], U32)
                plg = p.ps(pre + "plg", [128, 512])
                pb2 = p.ps(pre + "pb2", [1, 512])
                pool1 = p.dram("pool1")
                mrows = p.dram("mrows")

                @blk.gpsimd
                def _(g):
                    p.ev(g.memset(onesf[:, :], 1.0), pre + "g", 1)
                    p.ev(g.memset(mones[:, :], 1.0), pre + "g", 1)
                    p.ev(g.iota(siota[:, :], pattern=[[1, 8]], base=0,
                                channel_multiplier=0,
                                allow_small_or_imprecise_dtypes=True), pre + "g", 1)
                    p.ev(g.iota(iotW1[:, :].rearrange("p (a b) -> p a b", b=2),
                                pattern=[[256, 8], [1, 2]], base=0,
                                channel_multiplier=2,
                                allow_small_or_imprecise_dtypes=True), pre + "g", 1)
                    p.ev(g.iota(iotW2[:, :], pattern=[[128, 32]], base=0,
                                channel_multiplier=1,
                                allow_small_or_imprecise_dtypes=True), pre + "g", 1)
                    ap = p.dram("L1_ffn_Wr").rearrange("(c p) e -> c p e", p=128)
                    for c in range(8):
                        p.ev(g.dma_start(out=wr1[:, TS(c, 8)], in_=ap[c]), pre + "wl", 16)
                    p.ev(g.dma_start(out=b1all[:, :], in_=p.dram("L1_ffn_b1")),
                         pre + "wl", 16)
                    p.ev(g.dma_start(out=b2all[:, :], in_=p.dram("L1_ffn_b2")),
                         pre + "wl", 16)

                for s in range(NSEQ):
                    @blk.gpsimd
                    def _(g, s=s):
                        p.wt(g, p.now(pre + "rmm"))
                        p.ev(g.dma_start(out=pcol[:, :],
                                         in_=pool1[s].rearrange("(c p) -> p c", p=128),
                                         allow_slow_non_contiguous=True), pre + "pl", 16)

                    @blk.tensor
                    def _(t, s=s):
                        p.wt(t, p.now(pre + "wl"), p.now(pre + "pl"), p.now(pre + "re"))
                        for c in range(8):
                            inst = t.matmul(plg[0:1, 0:8], pcol[:, c:c + 1],
                                            wr1[:, TS(c, 8)], start=(c == 0), stop=(c == 7))
                        p.ev(inst, pre + "rmm", 1)

                    @blk.scalar
                    def _(a, s=s):
                        p.wt(a, p.now(pre + "rmm"), p.now(pre + "rv"))
                        p.ev(a.activation(out=lgt[:, 0:8], in_=plg[0:1, 0:8], func=AF.Copy),
                             pre + "re", 1)

                    @blk.vector
                    def _(v, s=s):
                        p.wt(v, p.now(pre + "re"), p.now(pre + "ms"))
                        e = p.ev(v.reduce_max(lgt[:, 8:9], lgt[:, 0:8], axis=AX.X),
                                 pre + "rv", 1)
                        p.wt(v, e)
                        e = p.ev(v.tensor_scalar(out=lgt[:, 16:24], in0=lgt[:, 0:8],
                                                 scalar1=lgt[:, 8:9], scalar2=None,
                                                 op0=ALU.is_ge), pre + "rv", 1)
                        p.wt(v, e)
                        e = p.ev(v.scalar_tensor_tensor(out=lgt[:, 24:32], in0=lgt[:, 16:24],
                                                        scalar=-1e30, in1=lgt[:, 0:8],
                                                        op0=ALU.mult, op1=ALU.add),
                                 pre + "rv", 1)
                        p.wt(v, e)
                        e = p.ev(v.reduce_max(lgt[:, 9:10], lgt[:, 24:32], axis=AX.X),
                                 pre + "rv", 1)
                        p.wt(v, e)
                        e = p.ev(v.tensor_scalar(out=lgt[:, 24:32], in0=lgt[:, 24:32],
                                                 scalar1=lgt[:, 9:10], scalar2=None,
                                                 op0=ALU.is_ge), pre + "rv", 1)
                        p.wt(v, e)
                        e = p.ev(v.tensor_tensor(out=lgt[:, 10:11], in0=lgt[:, 9:10],
                                                 in1=lgt[:, 8:9], op=ALU.subtract),
                                 pre + "rv", 1)

                    @blk.scalar
                    def _(a, s=s):
                        p.wt(a, p.now(pre + "rv"))
                        p.ev(a.activation(out=lgt[:, 11:12], in_=lgt[:, 10:11], func=AF.Exp),
                             pre + "re", 1)

                    @blk.vector
                    def _(v, s=s):
                        p.wt(v, p.now(pre + "re"), p.now(pre + "g"))
                        e = p.ev(v.tensor_scalar(out=lgt[:, 12:13], in0=lgt[:, 11:12],
                                                 scalar1=1.0, scalar2=None, op0=ALU.add),
                                 pre + "rv", 1)
                        p.wt(v, e)
                        e = p.ev(v.reciprocal(lgt[:, 13:14], lgt[:, 12:13]), pre + "rv", 1)
                        p.wt(v, e)
                        e = p.ev(v.tensor_tensor(out=lgt[:, 14:15], in0=lgt[:, 11:12],
                                                 in1=lgt[:, 13:14], op=ALU.mult), pre + "rv", 1)
                        p.wt(v, e)
                        e = p.ev(v.tensor_tensor(out=seltmp[:, :], in0=lgt[:, 16:24],
                                                 in1=siota[:, :], op=ALU.mult), pre + "rv", 1)
                        p.wt(v, e)
                        e = p.ev(v.reduce_sum(selr[:, 4 * s:4 * s + 1], seltmp[:, :],
                                              axis=AX.X), pre + "rv", 1)
                        p.wt(v, e)
                        e = p.ev(v.tensor_tensor(out=seltmp[:, :], in0=lgt[:, 24:32],
                                                 in1=siota[:, :], op=ALU.mult), pre + "rv", 1)
                        p.wt(v, e)
                        e = p.ev(v.reduce_sum(selr[:, 4 * s + 1:4 * s + 2], seltmp[:, :],
                                              axis=AX.X), pre + "rv", 1)
                        p.wt(v, e)
                        e = p.ev(v.tensor_copy(out=selr[:, 4 * s + 2:4 * s + 3],
                                               in_=lgt[:, 13:14]), pre + "rv", 1)
                        p.wt(v, e)
                        e = p.ev(v.tensor_copy(out=selr[:, 4 * s + 3:4 * s + 4],
                                               in_=lgt[:, 14:15]), pre + "rv", 1)

                    @blk.scalar
                    def _(a, s=s):
                        p.wt(a, p.now(pre + "rv"))
                        p.ev(a.dma_start(out=mrows[2 * s][None, :], in_=lgt[:, 16:24]),
                             pre + "ms", 16)
                        p.ev(a.dma_start(out=mrows[2 * s + 1][None, :], in_=lgt[:, 24:32]),
                             pre + "ms", 16)

                @blk.tensor
                def _(t):
                    p.wt(t, p.now(pre + "rv"), p.now(pre + "g"))
                    inst = t.matmul(plg[:, 16:24], onesf[:, :], selr[:, :],
                                    start=True, stop=True)
                    p.ev(inst, pre + "bm", 1)

                @blk.scalar
                def _(a):
                    p.wt(a, p.now(pre + "bm"))
                    p.ev(a.activation(out=sel[:, :], in_=plg[:, 16:24], func=AF.Copy),
                         pre + "sb", 1)

                @blk.gpsimd
                def _(g):
                    p.wt(g, p.now(pre + "ms"))
                    p.ev(g.dma_start(out=maskT[:, :], in_=mrows.rearrange("k e -> e k"),
                                     allow_slow_non_contiguous=True), pre + "mt", 16)
                    p.ev(g.dma_start(out=maskTf[:, :], in_=mrows.rearrange("k e -> e k"),
                                     allow_slow_non_contiguous=True), pre + "mt", 16)

                @blk.vector
                def _(v):
                    p.wt(v, p.now(pre + "mt"), p.now(pre + "g"))
                    for sk in range(NSEQ * 2):
                        p.ev(v.tensor_scalar(out=mones[:, TS(sk, 512)],
                                             in0=mones[:, TS(sk, 512)],
                                             scalar1=maskTf[:, sk:sk + 1], scalar2=None,
                                             op0=ALU.mult), pre + "mo", 1)

                for sk in range(NSEQ * 2):
                    for half in range(2):
                        @blk.tensor
                        def _(t, sk=sk, half=half):
                            p.wt(t, p.now(pre + "mt"), p.now(pre + "wl"),
                                 p.now(pre + "be"))
                            p.ev(t.matmul(pb2[:, :], maskT[:, sk:sk + 1],
                                          b2all[0:8, half * 512:(half + 1) * 512],
                                          start=True, stop=True), pre + "bm", 1)

                        @blk.scalar
                        def _(a, sk=sk, half=half):
                            p.wt(a, p.now(pre + "bm"))
                            ebe = p.ev(a.activation(out=b2st[:, :], in_=pb2[0:1, :],
                                                    func=AF.Copy), pre + "be", 1)
                            p.wt(a, ebe)
                            p.ev(a.dma_start(
                                out=p.dram("b2w")[sk][None, half * 512:(half + 1) * 512],
                                in_=b2st[:, :]), pre + "be", 16)

                @blk.vector
                def _(v):
                    p.wt(v, p.now(pre + "sb"), p.now(pre + "g"))
                    for sk in range(NSEQ * 2):
                        s_, k_ = sk // 2, sk % 2
                        ecol = sel[:, 4 * s_ + k_:4 * s_ + k_ + 1]
                        e = p.ev(v.tensor_scalar(out=ebase[:, 0:1], in0=ecol,
                                                 scalar1=2048.0, scalar2=None,
                                                 op0=ALU.mult), pre + "ix", 1)
                        p.wt(v, e)
                        e = p.ev(v.tensor_scalar(out=idxt[:, 0:16], in0=iotW1[:, :],
                                                 scalar1=ebase[:, 0:1], scalar2=None,
                                                 op0=ALU.add), pre + "ix", 1)
                        p.wt(v, e)
                        e = p.ev(v.tensor_copy(out=idxw1a[:, TS(sk, 16)],
                                               in_=idxt[:, 0:16]), pre + "ix", 1)
                        p.wt(v, e)
                        e = p.ev(v.tensor_scalar(out=ebase[:, 1:2], in0=ecol,
                                                 scalar1=4096.0, scalar2=None,
                                                 op0=ALU.mult), pre + "ix", 1)
                        p.wt(v, e)
                        e = p.ev(v.tensor_scalar(out=idxt[:, :], in0=iotW2[:, :],
                                                 scalar1=ebase[:, 1:2], scalar2=None,
                                                 op0=ALU.add), pre + "ix", 1)
                        p.wt(v, e)
                        e = p.ev(v.tensor_copy(out=idxw2a[:, TS(sk, 32)],
                                               in_=idxt[:, :]), pre + "ix", 1)
                        p.wt(v, e)

        if layer == 0:
            stkR.close()
            xnstg = xnf[:, 0:2 * D]
            xnpd = p.dram("xnperm")
            ypd = p.dram("yperm")
            yperm_t = ypd.rearrange("(n p) d -> n p d", p=128)
            b1fl = p.dram("L0_ffn_b1").rearrange("e (j x) -> (e j) x", x=128)
            b2fl = p.dram("L0_ffn_b2").rearrange("e (j x) -> (e j) x", x=128)
            bg = p.sb(pre + "bg", [128, 128], F32)
            with nc.Block(pre + "D") as blk:
                scl_ev = [None, None]
                ssc_ev = [None, None]
                for ti in range(NT):
                    pp = ti % 2

                    @blk.scalar
                    def _(a, ti=ti, pp=pp):
                        p.wt(a, ssc_ev[pp])
                        scl_ev[pp] = p.ev(a.dma_start(
                            out=xnstg[:, TS(pp, D)],
                            in_=xn[ti * 128:(ti + 1) * 128, :]), pre + f"sl{pp}", 16)

                    @blk.gpsimd
                    def _(g, ti=ti, pp=pp):
                        p.wt(g, scl_ev[pp])
                        ssc_ev[pp] = p.ev(g.indirect_dma_start(
                            out=xnpd, out_offset=IndirectOffsetOnAxis(
                                ap=slotu[:, ti:ti + 1], axis=0),
                            in_=xnstg[:, TS(pp, D)], in_offset=None), pre + "sc", 16)

        with nc.Block(pre + "E") as blk:
            if layer == 0:
                w1Td = p.dram("w1bT")
                w2Td = p.dram("w2bT")
                passes = [(None, b, [b]) for b in range(NBLK0)]
            else:
                w1f = p.dram("L1_ffn_W1").rearrange("e d (x h) -> (e d x) h", h=2048)
                w2f = p.dram("L1_ffn_W2").rearrange("e f d -> (e f) d")
                passes = [(None, 2 * s + k, list(range(s * NBLK, (s + 1) * NBLK)))
                          for s in range(NSEQ) for k in range(2)]
            acc_ev = {}
            xl_now = [None]
            hch_ev = [None, None]
            ych_ev = [None, None]
            hbl_ev = [None] * 32
            yst_ev = [None] * 8
            hmm_now = [None]
            ymm_now = [None]
            for pi, (e, sk, blkids) in enumerate(passes):
                if layer == 0:
                    b = sk

                    @blk.gpsimd
                    def _(g, b=b, pi=pi):
                        if pi >= 1:
                            p.wt(g, p.now(pre + "hmm"))
                        for dc in range(8):
                            p.ev(g.indirect_dma_start(
                                out=w1t[:, TS(dc, FFD)], out_offset=None, in_=w1Td,
                                in_offset=IndirectOffsetOnAxis(
                                    ap=idxw1b[:, b:b + 1], axis=0),
                                element_offset=dc * 128 * FFD), pre + "w1l", 16)

                    @blk.gpsimd
                    def _(g, b=b, pi=pi):
                        if pi >= 1:
                            p.wt(g, p.now(pre + "ymm"))
                        for fc in range(32):
                            p.ev(g.indirect_dma_start(
                                out=w2t[:, TS(fc, D)], out_offset=None, in_=w2Td,
                                in_offset=IndirectOffsetOnAxis(
                                    ap=idxw2b[:, b:b + 1], axis=0),
                                element_offset=fc * 128 * D), pre + "w2l", 16)
                        p.wt(g, p.now(pre + "bb"))
                        p.ev(g.indirect_dma_start(
                            out=bg[:, :], out_offset=None, in_=b1fl,
                            in_offset=IndirectOffsetOnAxis(
                                ap=idxb1[:, b:b + 1], axis=0)), pre + "bg", 16)

                    @blk.scalar
                    def _(a, b=b, pi=pi):
                        p.wt(a, p.now(pre + "bg"))
                        p.ev(a.dma_start(
                            out=p.dram("b1s")[b].rearrange("(j x) -> j x", x=128),
                            in_=bg[0:32, :]), pre + "bb", 16)

                    @blk.gpsimd
                    def _(g, b=b, pi=pi):
                        p.wt(g, p.now(pre + "bb"))
                        p.ev(g.indirect_dma_start(
                            out=bg[:, :], out_offset=None, in_=b2fl,
                            in_offset=IndirectOffsetOnAxis(
                                ap=idxb2[:, b:b + 1], axis=0)), pre + "bg", 16)

                    @blk.scalar
                    def _(a, b=b, pi=pi):
                        p.wt(a, p.now(pre + "bg"))
                        p.ev(a.dma_start(
                            out=p.dram("b2s")[b].rearrange("(j x) -> j x", x=128),
                            in_=bg[0:8, :]), pre + "bb", 16)

                    @blk.gpsimd
                    def _(g, b=b, pi=pi):
                        if pi >= 1:
                            p.wt(g, p.now(pre + "hmm"), p.now(pre + "ymm"))
                        p.wt(g, p.now(pre + "bb"))
                        p.ev(g.dma_start(out=b1e[:, :],
                                         in_=p.dram("b1s")[b][None, :]), pre + "bl", 16)
                        p.ev(g.dma_start(out=b2e[:, :],
                                         in_=p.dram("b2s")[b][None, :]), pre + "bl", 16)
                    wle = [p.now(pre + "w1l"), p.now(pre + "w2l"), p.now(pre + "bl"),
                           p.now(pre + "g")]
                else:
                    @blk.gpsimd
                    def _(g, sk=sk, pi=pi):
                        if pi >= 1:
                            p.wt(g, p.now(pre + "hmm"))
                        for dc in range(8):
                            for half in range(2):
                                c = sk * 16 + dc * 2 + half
                                p.ev(g.indirect_dma_start(
                                    out=w1t[:, dc * FFD + half * 2048:
                                            dc * FFD + half * 2048 + 2048],
                                    out_offset=None, in_=w1f,
                                    in_offset=IndirectOffsetOnAxis(
                                        ap=idxw1a[:, c:c + 1], axis=0)),
                                    pre + "w1l", 16)

                    @blk.gpsimd
                    def _(g, sk=sk, pi=pi):
                        if pi >= 1:
                            p.wt(g, p.now(pre + "ymm"))
                        p.wt(g, p.now(pre + "be"))
                        for fc in range(32):
                            c = sk * 32 + fc
                            p.ev(g.indirect_dma_start(
                                out=w2t[:, TS(fc, D)], out_offset=None, in_=w2f,
                                in_offset=IndirectOffsetOnAxis(
                                    ap=idxw2a[:, c:c + 1], axis=0)),
                                pre + "w2l", 16)
                        p.ev(g.dma_start(out=b2row[:, (pi % 2) * D:(pi % 2 + 1) * D],
                                         in_=p.dram("b2w")[sk][None, :]),
                             pre + "w2l", 16)
                    wle = [p.now(pre + "w1l"), p.now(pre + "w2l"), p.now(pre + "g"),
                           p.now(pre + "wl")]

                for blkid in blkids:
                    s = blkid // NBLK
                    t0 = blkid * BLK
                    xnE = xnpd if layer == 0 else xn

                    @blk.sync
                    def _(sy, blkid=blkid, t0=t0, pi=pi, xnE=xnE):
                        p.wt(sy, p.now(pre + "hmm"))
                        if layer == 0 and pi == 0:
                            p.wt(sy, p.now(pre + "sc"))
                        for c in range(8):
                            ev = p.ev(sy.dma_start(
                                out=xnf[:, c * BLK:(c + 1) * BLK],
                                in_=xnE[t0:t0 + BLK, TS(c, 128)], transpose=True),
                                pre + "xl", 16)
                        xl_now[0] = ev

                    for fc in range(32):
                        @blk.tensor
                        def _(t, sk=sk, fc=fc):
                            if fc == 0:
                                p.wt(t, *wle)
                                p.wt(t, xl_now[0])
                            p.wt(t, hch_ev[fc % 2])
                            for dc in range(8):
                                t.matmul(ph_[fc % 2][:, :],
                                         w1t[:, dc * FFD + fc * 128: dc * FFD + (fc + 1) * 128],
                                         xnf[:, TS(dc, BLK)], start=(dc == 0), stop=False)
                            if layer == 0:
                                inst = t.matmul(ph_[fc % 2][:, :], b1e[:, TS(fc, 128)],
                                                ones5[:, :], start=False, stop=True)
                            else:
                                inst = t.matmul(ph_[fc % 2][:, :], b1all[0:8, TS(fc, 128)],
                                                mones[0:8, TS(sk, 512)],
                                                start=False, stop=True)
                            hmm_now[0] = p.ev(inst, pre + "hmm", 1)

                        @blk.scalar
                        def _(a, fc=fc):
                            p.wt(a, hmm_now[0], hbl_ev[fc])
                            hch_ev[fc % 2] = p.ev(a.activation(
                                out=hb[:, TS(fc, BLK)], in_=ph_[fc % 2][:, :],
                                func=(AF.Relu if layer == 0 else AF.Gelu)), pre + "he", 1)

                    for tt in range(4):
                        for half in range(2):
                            pid = (2 * tt + half) % 2

                            @blk.tensor
                            def _(t, pi=pi, tt=tt, half=half, pid=pid):
                                if tt == 0 and half == 0:
                                    p.wt(t, p.now(pre + "he"))
                                p.wt(t, ych_ev[pid])
                                for fc in range(32):
                                    t.matmul(
                                        py_[pid][:, :],
                                        hb[:, fc * BLK + tt * 128: fc * BLK + (tt + 1) * 128],
                                        w2t[:, fc * 1024 + half * 512:
                                            fc * 1024 + (half + 1) * 512] if layer == 0 else
                                        w2t[:, fc * D + half * 512: fc * D + (half + 1) * 512],
                                        start=(fc == 0), stop=False)
                                if layer == 0:
                                    inst = t.matmul(py_[pid][:, :], ones1[:, :],
                                                    b2e[:, TS(half, 512)],
                                                    start=False, stop=True)
                                else:
                                    inst = t.matmul(py_[pid][:, :], ones1[:, :],
                                                    b2row[:, (pi % 2) * D + half * 512:
                                                          (pi % 2) * D + (half + 1) * 512],
                                                    start=False, stop=True)
                                ymm_now[0] = p.ev(inst, pre + "ymm", 1)
                                if tt == 3 and half == 1:
                                    for fc in range(32):
                                        hbl_ev[fc] = ymm_now[0]

                            @blk.vector
                            def _(v, sk=sk, tt=tt, half=half, pid=pid):
                                p.wt(v, ymm_now[0], yst_ev[2 * tt + half])
                                if layer == 0:
                                    sc = 1.0
                                else:
                                    sc = sel[:, 4 * (sk // 2) + 2 + (sk % 2):
                                             4 * (sk // 2) + 3 + (sk % 2)]
                                ych_ev[pid] = p.ev(v.tensor_scalar(
                                    out=yst[:, tt * D + half * 512: tt * D + (half + 1) * 512],
                                    in0=py_[pid][:, :], scalar1=sc, scalar2=None,
                                    op0=ALU.mult), pre + "yv", 1)

                    yv_done = p.now(pre + "yv")

                    if layer == 0:
                        @blk.gpsimd
                        def _(g, blkid=blkid):
                            p.wt(g, yv_done)
                            for tt in range(4):
                                ev = p.ev(g.dma_start(
                                    out=yperm_t[blkid * 4 + tt],
                                    in_=yst[:, TS(tt, D)]), pre + "ys", 16)
                            for q in range(8):
                                yst_ev[q] = ev
                    else:
                        @blk.gpsimd
                        def _(g, blkid=blkid, first=(acc_ev.get(blkid) is None)):
                            p.wt(g, yv_done)
                            p.wt(g, acc_ev.get(blkid))
                            if first:
                                p.wt(g, p.now(pre + "bc"))
                            for tt in range(4):
                                ev = p.ev(g.dma_start(
                                    out=outs_t[blkid * 4 + tt],
                                    in_=yst[:, TS(tt, D)], accum_op=ALU.add),
                                    pre + f"ac{blkid}", 16)
                            for q in range(8):
                                yst_ev[q] = ev
                            acc_ev[blkid] = ev

        if layer == 0:
            yg = yst[:, 2 * D:4 * D]
            base_t = base.rearrange("(n p) d -> n p d", p=128)
            with nc.Block(pre + "U") as blk:
                ug_ev = [None, None]
                ux_ev = [None, None]
                uv_ev = [None, None]
                us_ev = [None, None]
                for ti in range(NT):
                    pp = ti % 2

                    @blk.gpsimd
                    def _(g, ti=ti, pp=pp):
                        if ti == 0:
                            p.wt(g, p.now(pre + "ys"))
                        p.wt(g, uv_ev[pp])
                        ug_ev[pp] = p.ev(g.indirect_dma_start(
                            out=yg[:, TS(pp, D)], out_offset=None, in_=ypd,
                            in_offset=IndirectOffsetOnAxis(
                                ap=slotu[:, ti:ti + 1], axis=0)), pre + f"ug{pp}", 16)

                    @blk.scalar
                    def _(a, ti=ti, pp=pp):
                        if ti == 0:
                            p.wt(a, p.now(pre + "ys"))
                        p.wt(a, us_ev[pp])
                        ux_ev[pp] = p.ev(a.dma_start(out=yst[:, TS(pp, D)],
                                                     in_=base_t[ti]), pre + f"ux{pp}", 16)

                    @blk.vector
                    def _(v, ti=ti, pp=pp):
                        p.wt(v, ug_ev[pp], ux_ev[pp])
                        uv_ev[pp] = p.ev(v.tensor_tensor(
                            out=yst[:, TS(pp, D)], in0=yg[:, TS(pp, D)],
                            in1=yst[:, TS(pp, D)], op=ALU.add), pre + "uv", 1)

                    @blk.scalar
                    def _(a, ti=ti, pp=pp):
                        p.wt(a, uv_ev[pp])
                        us_ev[pp] = p.ev(a.dma_start(out=outs_t[ti],
                                                     in_=yst[:, TS(pp, D)]),
                                         pre + f"us{pp}", 16)

        with nc.Block(pre + "F") as blk:
            p.fence(blk)


def ph_ffn0(p: P):
    _ph_ffn(p, 0)


def ph_ffn1(p: P):
    _ph_ffn(p, 1)


def ph_attn1(p: P):
    nc = p.nc
    xn = p.dram("xn1")
    x1_t = p.dram("x1").rearrange("(n p) d -> n p d", p=128)
    xm_t = p.dram("xmid1").rearrange("(n p) d -> n p d", p=128)
    th1 = p.dram("th1")
    th1_t = th1.rearrange("(n p) d -> n p d", p=128)
    ph1_t = p.dram("ph1").rearrange("(n p) d -> n p d", p=128)
    g1_t = p.dram("g1").rearrange("(n p) d -> n p d", p=128)
    with ExitStack() as stk:
        p.stack = stk
        wth = p.sb("a1wth", [128, 8 * LAT], BF16)
        wph = p.sb("a1wph", [128, 8 * LAT], BF16)
        wg = p.sb("a1wg", [128, 8 * D], BF16)
        xnf = p.sb("a1xnf", [128, 2 * 8 * 128], BF16)
        thb = p.sb("a1thb", [128, 2 * LAT], BF16)
        phb = p.sb("a1phb", [128, 2 * LAT], BF16)
        gb = p.sb("a1gb", [128, 2 * D], BF16)
        phl = p.sb("a1phl", [128, 2 * LAT], BF16)
        gl = p.sb("a1gl", [128, 2 * D], BF16)
        msb = p.sb("a1msb", [128, 2 * D], BF16)
        thf = p.sb("a1thf", [128, 2 * LAT], BF16)
        x1b = p.sb("a1x1", [128, 2 * D], F32)
        xmb = p.sb("a1xm", [128, 2 * D], F32)
        stkP = ExitStack()
        pth = stkP.enter_context(nc.psum_tensor("a1pth", [128, 512], F32))
        pph = stkP.enter_context(nc.psum_tensor("a1pph", [128, 512], F32))
        pg = stkP.enter_context(nc.psum_tensor("a1pg", [128, D], F32))

        with nc.Block("a1P") as blk:
            @blk.sync
            def _(sy):
                wthb, wphb, wgb = p.dram("wthb"), p.dram("wphb"), p.dram("wgb")
                for c in range(8):
                    p.ev(sy.dma_start(out=wth[:, TS(c, LAT)],
                                      in_=wthb[0:LAT, TS(c, 128)], transpose=True),
                         "a1wl", 16)
                    p.ev(sy.dma_start(out=wph[:, TS(c, LAT)],
                                      in_=wphb[0:LAT, TS(c, 128)], transpose=True),
                         "a1wl", 16)
                    for hf in range(2):
                        p.ev(sy.dma_start(out=wg[:, c * D + hf * 512: c * D + (hf + 1) * 512],
                                          in_=wgb[hf * 512:(hf + 1) * 512, TS(c, 128)],
                                          transpose=True), "a1wl", 16)
            wl_ev = p.now("a1wl")
            xnf_ev = [None, None]
            xl_evs = [None, None]
            st_ev = [None, None]
            pev = [None]
            for ti in range(NT):
                bb = ti % 2

                @blk.sync
                def _(sy, ti=ti, bb=bb):
                    p.wt(sy, xnf_ev[bb])
                    for c in range(8):
                        ev = p.ev(sy.dma_start(
                            out=xnf[:, bb * 8 * 128 + c * 128: bb * 8 * 128 + (c + 1) * 128],
                            in_=xn[ti * 128:(ti + 1) * 128, TS(c, 128)], transpose=True),
                            f"a1xl{bb}", 16)
                    xl_evs[bb] = ev

                @blk.tensor
                def _(t, ti=ti, bb=bb):
                    p.wt(t, wl_ev, xl_evs[bb], pev[0])
                    for c in range(8):
                        t.matmul(pth[:, 0:LAT],
                                 xnf[:, bb * 8 * 128 + c * 128: bb * 8 * 128 + (c + 1) * 128],
                                 wth[:, TS(c, LAT)], start=(c == 0), stop=(c == 7))
                    for c in range(8):
                        t.matmul(pph[:, 0:LAT],
                                 xnf[:, bb * 8 * 128 + c * 128: bb * 8 * 128 + (c + 1) * 128],
                                 wph[:, TS(c, LAT)], start=(c == 0), stop=(c == 7))
                    for hf in range(2):
                        for c in range(8):
                            inst = t.matmul(
                                pg[:, TS(hf, 512)],
                                xnf[:, bb * 8 * 128 + c * 128: bb * 8 * 128 + (c + 1) * 128],
                                wg[:, c * D + hf * 512: c * D + (hf + 1) * 512],
                                start=(c == 0), stop=(c == 7))
                    xnf_ev[bb] = p.ev(inst, "a1mm", 1)

                @blk.scalar
                def _(a, ti=ti, bb=bb):
                    p.wt(a, p.now("a1mm"), st_ev[bb])
                    p.ev(a.activation(out=thb[:, TS(bb, LAT)], in_=pth[:, 0:LAT],
                                      func=AF.Copy), "a1e", 1)
                    p.ev(a.activation(out=phb[:, TS(bb, LAT)], in_=pph[:, 0:LAT],
                                      func=AF.Copy), "a1e", 1)
                    pev[0] = p.ev(a.activation(out=gb[:, TS(bb, D)], in_=pg[:, :],
                                               func=AF.Copy), "a1e", 1)
                    p.wt(a, pev[0])
                    p.ev(a.dma_start(out=th1_t[ti], in_=thb[:, TS(bb, LAT)]),
                         f"a1st{bb}", 16)
                    p.ev(a.dma_start(out=ph1_t[ti], in_=phb[:, TS(bb, LAT)]),
                         f"a1st{bb}", 16)
                    st_ev[bb] = p.ev(a.dma_start(out=g1_t[ti], in_=gb[:, TS(bb, D)]),
                                     f"a1st{bb}", 16)

        stkP.close()
        stkM = ExitStack()
        pM = [stkM.enter_context(nc.psum_tensor(f"a1pM{j}", [128, 512], F32))
              for j in range(4)]
        pout = stkM.enter_context(nc.psum_tensor("a1po", [128, D], F32))

        mev_ev = [None]
        for s in range(NSEQ):
            with nc.Block(f"a1M{s}") as blk:
                pl_ev = [p.now("a1Mm"), p.now("a1Mm")]
                ml_evs = [None, None]
                for i in range(TPS):
                    bb = i % 2
                    ti = s * TPS + i

                    @blk.sync
                    def _(sy, ti=ti, bb=bb, i=i):
                        if i == 0:
                            p.wt(sy, p.now("a1st0"), p.now("a1st1"))
                        p.wt(sy, pl_ev[bb])
                        p.ev(sy.dma_start(out=phl[:, TS(bb, LAT)], in_=ph1_t[ti]),
                             f"a1pl{bb}", 16)
                        ml_evs[bb] = p.ev(sy.dma_start(out=gl[:, TS(bb, D)], in_=g1_t[ti]),
                                          f"a1pl{bb}", 16)

                    @blk.tensor
                    def _(t, i=i, bb=bb, s=s):
                        p.wt(t, ml_evs[bb])
                        if i == 0 and s == 1:
                            p.wt(t, mev_ev[0])
                        for lc in range(2):
                            for oh in range(2):
                                inst = t.matmul(
                                    pM[2 * lc + oh][:, :],
                                    phl[:, bb * LAT + lc * 128: bb * LAT + (lc + 1) * 128],
                                    gl[:, bb * D + oh * 512: bb * D + (oh + 1) * 512],
                                    start=(i == 0), stop=(i == TPS - 1))
                        pl_ev[bb] = p.ev(inst, "a1Mm", 1)

                @blk.vector
                def _(v, s=s):
                    p.wt(v, p.now("a1Mm"))
                    if s == 1:
                        p.wt(v, p.now("a1om"))
                    for lc in range(2):
                        for oh in range(2):
                            e = p.ev(v.tensor_scalar(
                                out=msb[:, lc * D + oh * 512: lc * D + (oh + 1) * 512],
                                in0=pM[2 * lc + oh][:, :], scalar1=float(L1SCALE),
                                scalar2=None, op0=ALU.mult), "a1Mv", 1)
                    mev_ev[0] = e

            with nc.Block(f"a1O{s}") as blk:
                tf_ev = [p.now("a1om"), p.now("a1om")]
                x1_ev = [p.now("a1ov"), p.now("a1ov")]
                xs_ev = [p.now("a1os0"), p.now("a1os1")]
                tl_evs = [None, None]
                xl2_evs = [None, None]
                po_ev = p.now("a1ov")
                for i in range(TPS):
                    bb = i % 2
                    ti = s * TPS + i

                    @blk.sync
                    def _(sy, ti=ti, bb=bb, i=i):
                        if i == 0:
                            p.wt(sy, p.now("a1st0"), p.now("a1st1"))
                        p.wt(sy, tf_ev[bb])
                        for lc in range(2):
                            ev = p.ev(sy.dma_start(
                                out=thf[:, bb * LAT + lc * 128: bb * LAT + (lc + 1) * 128],
                                in_=th1[ti * 128:(ti + 1) * 128, TS(lc, 128)],
                                transpose=True), f"a1tl{bb}", 16)
                        tl_evs[bb] = ev

                    @blk.scalar
                    def _(a, ti=ti, bb=bb):
                        p.wt(a, x1_ev[bb])
                        xl2_evs[bb] = p.ev(a.dma_start(out=x1b[:, TS(bb, D)], in_=x1_t[ti]),
                                           f"a1xr{bb}", 16)

                    @blk.tensor
                    def _(t, i=i, bb=bb, s=s):
                        p.wt(t, tl_evs[bb], po_ev)
                        if i == 0:
                            p.wt(t, p.now("a1Mv"))
                        for half in range(2):
                            for lc in range(2):
                                inst = t.matmul(
                                    pout[:, TS(half, 512)],
                                    thf[:, bb * LAT + lc * 128: bb * LAT + (lc + 1) * 128],
                                    msb[:, lc * D + half * 512: lc * D + (half + 1) * 512],
                                    start=(lc == 0), stop=(lc == 1))
                        tf_ev[bb] = p.ev(inst, "a1om", 1)

                    @blk.vector
                    def _(v, i=i, bb=bb):
                        p.wt(v, p.now("a1om"), xl2_evs[bb], xs_ev[bb])
                        e = p.ev(v.tensor_tensor(out=xmb[:, TS(bb, D)], in0=pout[:, :],
                                                 in1=x1b[:, TS(bb, D)], op=ALU.add),
                                 "a1ov", 1)
                        x1_ev[bb] = e
                    po_ev = p.now("a1ov")

                    @blk.scalar
                    def _(a, ti=ti, bb=bb):
                        p.wt(a, p.now("a1ov"))
                        xs_ev[bb] = p.ev(a.dma_start(out=xm_t[ti], in_=xmb[:, TS(bb, D)]),
                                         f"a1os{bb}", 16)

        stkM.close()
        with nc.Block("a1F") as blk:
            p.fence(blk)


def _phase_builders():
    return {
        "cast0": ph_cast0,
        "ln0": lambda p: ph_ln(p, "xin", "xn0", pooled=False, lo_name="xnlo0"),
        "ln1": lambda p: ph_ln(p, "x1", "xn1", pooled=True),
        "attn0": ph_attn0,
        "ffn0": ph_ffn0,
        "attn1": ph_attn1,
        "ffn1": ph_ffn1,
    }


def build_launch(phases, before, needed_later):
    nc = bass.Bass("TRN2", target_bir_lowering=False)
    p = P(nc, phases, before, needed_later)
    builders = _phase_builders()
    for ph in phases:
        builders[ph](p)
    return nc, p


LAUNCHES = [["cast0", "ln0", "attn0", "ffn0", "ln1", "attn1", "ffn1"]]


def kernel(**inputs):
    x = np.asarray(inputs["L0_attn_x"])
    per_core = {"xin": [np.ascontiguousarray(x[NSEQ * c:NSEQ * (c + 1)].reshape(TC, D))
                        for c in range(NCORES)]}
    produced = set()
    out = None
    for li, phases in enumerate(LAUNCHES):
        later_reads = set()
        for phs in LAUNCHES[li + 1:]:
            for ph in phs:
                later_reads |= set(PHASE_IO[ph][0])
        writes, reads = set(), set()
        for ph in phases:
            reads |= set(PHASE_IO[ph][0])
            writes |= set(PHASE_IO[ph][1])
        needed_later = (writes & later_reads) | ({"out"} & writes)
        nc, p = build_launch(phases, produced, needed_later)
        in_maps = []
        for c in range(NCORES):
            m = {}
            for name in reads:
                if name == "xin" or name in SCRATCH:
                    if name in per_core:
                        m[name] = per_core[name][c]
                else:
                    m[name] = np.ascontiguousarray(np.asarray(inputs[name]))
            in_maps.append(m)
        res = run_bass_kernel_spmd(nc, in_maps, core_ids=list(range(NCORES)))
        for name in writes:
            if name == "out":
                out = [r["out"] for r in res.results]
            elif name in needed_later:
                per_core[name] = [r[name] for r in res.results]
        produced |= writes
    full = np.concatenate([o.reshape(NSEQ, SL, D) for o in out], axis=0)
    return np.ascontiguousarray(full.astype(np.float32))
